# version 15
# baseline (speedup 1.0000x reference)
import numpy as np
from contextlib import ExitStack

import concourse.bass as bass
import concourse.mybir as mybir
from concourse.bass_utils import run_bass_kernel_spmd

F32 = mybir.dt.float32
BF16 = mybir.dt.bfloat16
AF = mybir.ActivationFunctionType
ALU = mybir.AluOpType
AX = mybir.AxisListType

NCORE = 8
OWN = 2048
NT = OWN // 512
NCH = OWN // 128

BS = [4, 8, 16, 32]
NN = [8, 4, 2, 1]
RL = [4, 8, 8, 8]

ENGS = ("TE", "VE", "SE", "GE", "SY")


DPOOL = {"SY": 48, "GE": 24, "SE": 8, "CC": 4}
PAIR_RG = [[0, 1], [2, 3], [4, 5], [6, 7]]


class Em:
    def __init__(self, nc, stack):
        self.nc = nc
        self.specs = {e: [] for e in ENGS}
        self.pcnt = {e: 0 for e in ENGS}
        self.psem = {e: stack.enter_context(nc.semaphore(f"p{e}")) for e in ENGS}
        self.dpool = {e: [stack.enter_context(nc.semaphore(f"D{e}{i}")) for i in range(n)]
                      for e, n in DPOOL.items()}
        self.duses = {e: [0] * n for e, n in DPOOL.items()}
        self.drot = {e: 0 for e in DPOOL}
        self.ddirty = {e: set() for e in DPOOL}
        self.hw = {e: {} for e in ENGS}
        self.xsems = {}
        self._stack = stack

    def xsem(self, name):
        h = self._stack.enter_context(self.nc.semaphore(name))
        self.xsems[name] = h
        return h

    def emit(self, e, fn, waits=(), inc=True):
        wl = []
        for w in waits:
            if w is None:
                continue
            if w[0] == "s":
                key = ("s", w[1])
                n = w[2]
            elif w[0] == "p":
                _, src, n = w
                assert n <= self.pcnt[src], f"future token {w} vs {self.pcnt[src]}"
                key = ("p", src, n) if src == "GE" else ("p", src)
            else:
                _, src, i, n = w
                assert n <= self.duses[src][i], f"future dma token {w}"
                key = ("D", src, i)
                n = 16 * n
            if self.hw[e].get(key, 0) >= n:
                continue
            self.hw[e][key] = n
            wl.append((key, n))
        tok = None
        if inc:
            self.pcnt[e] += 1
            tok = ("p", e, self.pcnt[e])
        self.specs[e].append((fn, wl, tok))
        return tok

    def dma(self, e, out, in_, waits=(), **kw):
        i = self.drot[e] % len(self.dpool[e])
        self.drot[e] += 1
        prev = self.duses[e][i]
        w = list(waits)
        if prev:
            w.append(("D", e, i, prev))
        self.duses[e][i] = prev + 1
        self.ddirty[e].add(i)
        tok = ("D", e, i, prev + 1)
        wl = []
        for ww in w:
            if ww is None:
                continue
            if ww[0] == "s":
                key = ("s", ww[1])
                n = ww[2]
            elif ww[0] == "p":
                _, src, n = ww
                assert n <= self.pcnt[src]
                key = ("p", src, n) if src == "GE" else ("p", src)
            else:
                _, src, j, n = ww
                assert n <= self.duses[src][j]
                key = ("D", src, j)
                n = 16 * n
            if self.hw[e].get(key, 0) >= n:
                continue
            self.hw[e][key] = n
            wl.append((key, n))
        sem = self.dpool[e][i]
        self.specs[e].append((lambda eng, out=out, in_=in_, kw=kw, sem=sem:
                              eng.dma_start(out=out, in_=in_, **kw).then_inc(sem, 16), wl, None))
        return tok

    def cc(self, kind, ins, outs, rg, waits=(), op=None):
        i = self.drot["CC"] % len(self.dpool["CC"])
        self.drot["CC"] += 1
        prev = self.duses["CC"][i]
        w = list(waits)
        if prev:
            w.append(("D", "CC", i, prev))
        self.duses["CC"][i] = prev + 1
        self.ddirty["CC"].add(i)
        tok = ("D", "CC", i, prev + 1)
        wl = []
        for ww in w:
            if ww is None:
                continue
            if ww[0] == "p":
                _, s2, n = ww
                key = ("p", s2, n) if s2 == "GE" else ("p", s2)
            else:
                _, s2, j, n = ww
                key = ("D", s2, j)
                n = 16 * n
            if self.hw["GE"].get(key, 0) >= n:
                continue
            self.hw["GE"][key] = n
            wl.append((key, n))
        sem = self.dpool["CC"][i]
        aop = op if op is not None else mybir.AluOpType.bypass
        self.specs["GE"].append((lambda eng, ins=ins, outs=outs, rg=rg, aop=aop, sem=sem:
                                 eng.collective_compute(kind, aop, replica_groups=rg,
                                                        ins=ins, outs=outs).then_inc(sem, 16),
                                 wl, None))
        return tok

    def rdma(self, out_ap, in_ap, pattern, rname, lname):
        rs = self.xsems[rname]
        ls = self.xsems[lname]
        self.specs["GE"].append((lambda eng, out_ap=out_ap, in_ap=in_ap, rs=rs, ls=ls,
                                 pattern=pattern: eng.remote_dma_broadcast_einops(
                                     out_ap, in_ap, rs, local_sem=ls, repeat=pattern),
                                 [], None))
        return ("s", rname, 16)

    def barrier(self, extra=()):
        snap = {e: self.pcnt[e] for e in ENGS}
        dwaits = []
        for q in DPOOL:
            dwaits += [("D", q, i, self.duses[q][i]) for i in self.ddirty[q]]
            self.ddirty[q] = set()
        for e in ENGS:
            waits = [("p", s, snap[s]) for s in ENGS if s != e and snap[s] > 0]
            self.emit(e, lambda eng: eng.nop(), waits=waits + dwaits + list(extra), inc=True)
        return None

    def replay(self):
        nc = self.nc

        def mk(e):
            def body(engobj):
                for fn, wl, tok in self.specs[e]:
                    for key, n in wl:
                        if key[0] == "p":
                            engobj.wait_ge(self.psem[key[1]], n)
                        elif key[0] == "s":
                            engobj.wait_ge(self.xsems[key[1]], n)
                        else:
                            engobj.wait_ge(self.dpool[key[1]][key[2]], n)
                    ins = fn(engobj)
                    if tok is not None:
                        ins.then_inc(self.psem[e], 1)
            return body

        with nc.Block() as block:
            block.tensor(mk("TE"))
            block.vector(mk("VE"))
            block.scalar(mk("SE"))
            block.gpsimd(mk("GE"))
            block.sync(mk("SY"))


class Ring:

    def __init__(self, nc, stack, n, shape, dt, name="r"):
        self.t = [sbuf(nc, stack, f"{name}{i}", shape, dt) for i in range(n)]
        self.use = [[] for _ in range(n)]
        self.i = 0

    def next(self):
        i = self.i % len(self.t)
        self.i += 1
        w = self.use[i]
        self.use[i] = []
        return self.t[i].ap(), w, i

    def used(self, slot, tok):
        if tok is not None:
            self.use[slot].append(tok)


class PT:

    def __init__(self, ap):
        self._ap = ap
        self.readers = []

    def ap(self):
        return self._ap

    def wr(self):
        w = self.readers
        self.readers = []
        return w

    def rd(self, tok):
        if tok is not None:
            self.readers.append(tok)


def make_psum(nc):
    banks = [PT(nc.alloc_psum_tensor(f"pb{i}", [128, 512], F32).ap()) for i in range(7)]
    ptb = PT(nc.alloc_psum_tensor("ptb", [128, 1024], BF16).ap())
    return banks, ptb



def host_constants():
    c = {}
    c["c_ident"] = np.eye(128, dtype=np.float32)
    c["c_tril"] = np.tril(np.ones((128, 128), dtype=np.float32))
    masks = np.zeros((9, 64, 64), dtype=np.float32)
    for n in range(16):
        masks[0, 4 * n:4 * n + 4, 4 * n:4 * n + 4] = 1
    for l in range(4):
        for n in range(NN[l]):
            t0 = n * 2 * BS[l]
            b0 = t0 + BS[l]
            masks[1 + l, b0:b0 + BS[l], t0:t0 + BS[l]] = 1
            masks[5 + l, t0:t0 + BS[l], b0:b0 + BS[l]] = 1
    c["c_l0masks"] = np.ascontiguousarray(np.concatenate([masks[k] for k in range(9)], axis=1))
    c["c_triu"] = np.triu(np.ones((128, 128), dtype=np.float32))
    c["c_ident2"] = np.vstack([np.eye(64, dtype=np.float32)] * 2)
    c["c_ones"] = np.ones((128, 1), dtype=np.float32)
    c["c_onesrow"] = np.ones((1, 128), dtype=np.float32)
    e2 = np.zeros((128, 2), dtype=np.float32)
    e2[:64, 0] = 1.0
    e2[64:, 1] = 1.0
    c["c_e2"] = e2
    e2t = np.zeros((2, 128), dtype=np.float32)
    e2t[0, :64] = 1.0
    e2t[1, 64:] = 1.0
    c["c_e2t"] = e2t
    return c



def decl_in(nc, name, shape, dt=F32):
    return nc.dram_tensor(name, list(shape), dt, kind="ExternalInput").ap()


def decl_out(nc, name, shape, dt=F32):
    return nc.dram_tensor(name, list(shape), dt, kind="ExternalOutput").ap()


_NAMEC = [0]


def sbuf(nc, stack, name, shape, dt):
    _NAMEC[0] += 1
    return stack.enter_context(nc.sbuf_tensor(f"{name}_{_NAMEC[0]}", list(shape), dt))


def dram_scr(nc, name, shape, dt):
    pool = getattr(nc, "_scrpool", None)
    if pool is not None and name in pool:
        return pool[name]
    _NAMEC[0] += 1
    return nc.dram_tensor(f"{name}_{_NAMEC[0]}", list(shape), dt, kind="Internal").ap()


def _ctx(ctx):
    if ctx is not None:
        return ctx + (False,)
    nc = bass.Bass("TRN2", target_bir_lowering=False, debug=False, detect_race_conditions=False)
    stack = ExitStack()
    em = Em(nc, stack)
    banks, ptb = make_psum(nc)
    return nc, stack, em, banks, ptb, True


def _io_in(io, nc, name, shape, dt=F32):
    if io and name in io:
        return io[name]
    return decl_in(nc, name, shape, dt)


def _io_out(io, nc, name, shape, dt=F32):
    if io and name in io:
        return io[name]
    return decl_out(nc, name, shape, dt)


def _lc(io, em, nc, stack, name, shape, want_bf=False):
    if io and name in io.get("c", {}):
        return io["c"][name]
    return load_const(em, nc, stack, name, shape, want_bf=want_bf)


def load_const(em, nc, stack, name, shape, want_bf=False, q="SY"):
    d = decl_in(nc, name, shape)
    h = sbuf(nc, stack, "sb_" + name, shape, F32)
    tok = em.dma(q, h.ap(), d)
    res = {"f32": h.ap(), "ftok": tok, "btok": tok}
    if want_bf:
        hb = sbuf(nc, stack, "sbb_" + name, shape, BF16)
        res["btok"] = em.emit("VE", lambda v: v.tensor_copy(hb.ap(), h.ap()), waits=[tok])
        res["bf"] = hb.ap()
    return res


def memset(em, e, ap, val, waits=()):
    return em.emit(e, lambda eng: eng.memset(ap, val), waits=waits)


def bcast_dma(em, dst_ap, dram_row, P=128, n=None, waits=(), q="GE"):
    n = n if n is not None else dram_row.shape[-1]
    return em.dma(q, dst_ap, dram_row.broadcast_to([P, n]), waits=waits)


def bcast_row(em, bankPT, row_ap, onesrow_f32, n=512, off=0, waits=()):
    w = bankPT.wr() + list(waits)
    hi, hw_, hi_i = em.bc_hi.next()
    lo, lw_, lo_i = em.bc_lo.next()
    lf, ff_, lf_i = em.bc_f.next()
    t1 = em.emit("VE", lambda v: v.tensor_copy(hi[0:1, 0:n], row_ap), waits=list(waits) + hw_)
    t2 = em.emit("VE", lambda v: v.tensor_sub(lf[0:1, 0:n], row_ap, hi[0:1, 0:n]), waits=[t1] + ff_)
    t3 = em.emit("VE", lambda v: v.tensor_copy(lo[0:1, 0:n], lf[0:1, 0:n]), waits=[t2] + lw_)
    em.bc_f.used(lf_i, t3)
    ob = em.onesrow_bf
    m1 = em.emit("TE", lambda t, ob=ob: t.matmul(bankPT.ap()[0:128, off:off + n], ob,
                                                 hi[0:1, 0:n], start=True, stop=False), waits=[t1] + w)
    m2 = em.emit("TE", lambda t, ob=ob: t.matmul(bankPT.ap()[0:128, off:off + n], ob,
                                                 lo[0:1, 0:n], start=False, stop=True), waits=[t3])
    em.bc_hi.used(hi_i, m1)
    em.bc_lo.used(lo_i, m2)
    return m2


def stats_rows(em, nc, st, sl, pb, eps_ap, want_extra, toks, bw, sel_bf, sel_tok, W):
    cb, cw, ci = st["cbr"].next()
    c1 = em.emit("SE", lambda s, cb=cb: s.activation(out=cb[0:33, 0:W], in_=pb.ap()[0:33, 0:W],
                                                     func=AF.Copy), waits=toks + cw)
    lo, lw_, li_ = st["cbl"].next()
    c2 = em.emit("VE", lambda v, cb=cb, lo=lo: v.tensor_sub(lo[0:33, 0:W], pb.ap()[0:33, 0:W],
                                                            cb[0:33, 0:W]), waits=[c1] + lw_)
    pb.rd(c1)
    pb.rd(c2)
    war = bw.wr()
    mm1 = em.emit("TE", lambda t, cb=cb: t.matmul(bw.ap()[0:128, 0:W], sel_bf, cb[0:33, 0:W],
                                                  start=True, stop=False),
                  waits=[c1, sel_tok] + war)
    mmW = em.emit("TE", lambda t, lo=lo: t.matmul(bw.ap()[0:128, 0:W], sel_bf, lo[0:33, 0:W],
                                                  start=False, stop=True), waits=[c2])
    st["cbr"].used(ci, mm1)
    st["cbl"].used(li_, mmW)
    g = st["wide"]
    mT, vT, sT, xT = g["m"], g["v"], g["s"], g["x"]
    pw = st.get("prev_readers", [])
    tm = em.emit("VE", lambda v: v.tensor_scalar_mul(mT[0:64, 0:W], bw.ap()[0:64, 0:W],
                                                     1.0 / 1024.0), waits=[mmW] + pw)
    s1 = em.emit("VE", lambda v: v.tensor_mul(xT[0:64, 0:W], mT[0:64, 0:W], mT[0:64, 0:W]),
                 waits=[tm])
    s2 = em.emit("VE", lambda v: v.scalar_tensor_tensor(vT[0:64, 0:W], bw.ap()[64:128, 0:W],
                                                        1.0 / 1024.0, xT[0:64, 0:W],
                                                        ALU.mult, ALU.subtract), waits=[s1] + pw)
    bw.rd(tm)
    bw.rd(s2)
    t5 = em.emit("SE", lambda s: s.activation(out=sT[0:64, 0:W], in_=vT[0:64, 0:W], func=AF.Sqrt,
                                              bias=eps_ap[0:64, 0:1]), waits=[s2])
    t6 = em.emit("VE", lambda v: v.reciprocal(xT[0:64, 0:W], sT[0:64, 0:W]), waits=[t5, s1])
    rd = [em.dma("SY", st["scr_dram"][0:1, sl], mT[0:1, 0:W], waits=[tm]),
          em.dma("SY", st["scr_dram"][1:2, sl], xT[0:1, 0:W], waits=[t6])]
    last = t6
    if want_extra:
        rT, aT, bT = g["r"], g["a"], g["b"]
        u1 = em.emit("SE", lambda s: s.activation(out=rT[0:64, 0:W], in_=vT[0:64, 0:W],
                                                  func=AF.Sqrt), waits=[s2] + pw)
        u2 = em.emit("VE", lambda v: v.tensor_mul(aT[0:64, 0:W], rT[0:64, 0:W], xT[0:64, 0:W]),
                     waits=[u1, t6])
        u3 = em.emit("VE", lambda v: v.tensor_scalar_add(aT[0:64, 0:W], aT[0:64, 0:W], 1e-8),
                     waits=[u2])
        u4 = em.emit("VE", lambda v: v.reciprocal(aT[0:64, 0:W], aT[0:64, 0:W]), waits=[u3])
        u5 = em.emit("VE", lambda v: v.tensor_mul(bT[0:64, 0:W], aT[0:64, 0:W], xT[0:64, 0:W]),
                     waits=[u4])
        rd.append(em.dma("SY", st["scr_dram"][2:3, sl], bT[0:1, 0:W], waits=[u5]))
        w1 = em.emit("VE", lambda v: v.tensor_mul(rT[0:64, 0:W], xT[0:64, 0:W], xT[0:64, 0:W]),
                     waits=[u2])
        w2 = em.emit("VE", lambda v: v.tensor_mul(rT[0:64, 0:W], rT[0:64, 0:W], vT[0:64, 0:W]),
                     waits=[w1])
        w3 = em.emit("SE", lambda s: s.activation(out=rT[0:64, 0:W], in_=rT[0:64, 0:W],
                                                  func=AF.Sqrt, bias=eps_ap[0:64, 1:2]),
                     waits=[w2])
        w4 = em.emit("VE", lambda v: v.reciprocal(rT[0:64, 0:W], rT[0:64, 0:W]), waits=[w3])
        rd.append(em.dma("SY", st["scr_dram"][3:4, sl], rT[0:1, 0:W], waits=[w4]))
        last = w4
    st["prev_readers"] = rd + [last]
    return last


def stats_phase(em, nc, stack, banks, xap, ntok, st, onesb, eps_ap, want_extra=True,
                scr_dram=None, chunk_off=None):
    co = chunk_off if chunk_off is not None else [0] * 8
    assert scr_dram is not None
    with ExitStack() as ph:
        sel = sbuf(nc, ph, "sel33", [33, 128], BF16)
        z1 = memset(em, "VE", sel.ap(), 0.0)
        z2 = memset(em, "VE", sel.ap()[0:1, 0:64], 1.0, waits=[z1])
        z3 = memset(em, "VE", sel.ap()[32:33, 64:128], 1.0, waits=[z2])
        st = dict(st)
        st["scr_dram"] = scr_dram
        st["cbr"] = Ring(nc, ph, 3, (33, 512), BF16, "stcb")
        st["cbl"] = Ring(nc, ph, 3, (33, 512), BF16, "stcl")
        st["wide"] = {k: sbuf(nc, ph, "stw" + k, [64, 512], F32).ap()
                      for k in ("m", "v", "s", "x", "r", "a", "b")}
        xr = Ring(nc, ph, 3, (128, 512), F32, "stx")
        cb = Ring(nc, ph, 3, (128, 512), BF16, "stc")
        sq = Ring(nc, ph, 3, (128, 512), BF16, "stq")
        pb = banks[6]
        bw = banks[5]
        z0 = memset(em, "VE", pb.ap()[0:33, 0:512], 0.0, waits=pb.wr())
        pb.rd(z0)
        for it in range(ntok // 512):
            sl = slice(512 * it, 512 * (it + 1))
            war = pb.wr()
            mml = []
            for cch in range(8):
                xa, xw, xi = xr.next()
                lt = em.dma("SY", xa, xap[128 * cch:128 * (cch + 1),
                                          co[cch] + 512 * it:co[cch] + 512 * (it + 1)], waits=xw)
                ca, cw, ci = cb.next()
                t1 = em.emit("VE", lambda v, xa=xa, ca=ca: v.tensor_copy(ca, xa), waits=[lt] + cw)
                sa, sw, si = sq.next()
                t2 = em.emit("SE", lambda s, xa=xa, sa=sa: s.activation(out=sa, in_=xa, func=AF.Square),
                             waits=[lt] + sw)
                xr.used(xi, t1)
                xr.used(xi, t2)
                m1 = em.emit("TE", lambda t, ca=ca, cch=cch: t.matmul(
                    pb.ap()[0:1, 0:512], onesb, ca, start=(cch == 0), stop=(cch == 7)),
                    waits=[t1] + (war if cch == 0 else []))
                m2 = em.emit("TE", lambda t, sa=sa, cch=cch: t.matmul(
                    pb.ap()[32:33, 0:512], onesb, sa, start=(cch == 0), stop=(cch == 7)),
                    waits=[t2])
                cb.used(ci, m1)
                sq.used(si, m2)
                mml = [m1, m2]
            stats_rows(em, nc, st, sl, pb, eps_ap, want_extra, mml, bw, sel.ap(), z3, 512)
        em.barrier()


def norm_mm_phase(em, nc, stack, banks, xap, st_row_m, st_row_s, out_cb, it_sl, rings, bt_tok):
    raise NotImplementedError



def build_L0A(io=None, ctx=None):
    nc, stack, em, banks, ptb, standalone = _ctx(ctx)

    xfm = _io_in(io, nc, "xfm", [1024, OWN])
    wq_d = _io_in(io, nc, "l0wq", [1024, 64])
    wk_d = _io_in(io, nc, "l0wk", [1024, 64])
    wv_d = _io_in(io, nc, "l0wv", [1024, 64])
    wup_d = _io_in(io, nc, "l0wup", [4096, 1024])
    wdn_d = _io_in(io, nc, "l0wdn", [1024, 4096])
    ulmats = {fam + str(l): _io_in(io, nc, f"l0{fam}{l}", [BS[l], RL[l]])
              for fam in ("UL", "VR", "UR", "VL") for l in range(4)}

    qf_o = _io_out(io, nc, "qf", [64, OWN], BF16)
    kf_o = _io_out(io, nc, "kf", [64, OWN], BF16)
    vf_o = _io_out(io, nc, "vf", [64, OWN], BF16)
    kt_o = _io_out(io, nc, "kt", [OWN, 64], BF16)
    vt_o = _io_out(io, nc, "vt", [OWN, 64], BF16)
    aux_o = _io_out(io, nc, "aux", [4, OWN, 4, 64], BF16)
    g0_o = _io_out(io, nc, "g0", [1024, OWN], BF16)
    deltas_o = _io_out(io, nc, "deltas", [NCH, 10, 64, 64], F32)

    ones = _lc(io, em, nc, stack, "c_ones", [128, 1], want_bf=True)
    onesrow = _lc(io, em, nc, stack, "c_onesrow", [1, 128])
    ident = _lc(io, em, nc, stack, "c_ident", [128, 128], want_bf=True)
    ident2 = _lc(io, em, nc, stack, "c_ident2", [128, 64], want_bf=True)
    e2t_c = _lc(io, em, nc, stack, "c_e2t", [2, 128], want_bf=True)
    onesrow_b = sbuf(nc, stack, "onesrow_bf", [1, 128], BF16)
    em.emit("VE", lambda v: v.tensor_copy(onesrow_b.ap(), onesrow["f32"]), waits=[onesrow["ftok"]])
    em.onesrow_bf = onesrow_b.ap()
    em.bc_hi = Ring(nc, stack, 4, (1, 512), BF16, "bch")
    em.bc_lo = Ring(nc, stack, 4, (1, 512), BF16, "bcl")
    em.bc_f = Ring(nc, stack, 4, (1, 512), F32, "bcf")
    eps = sbuf(nc, stack, "epsc", [128, 4], F32)
    memset(em, "VE", eps.ap()[:, 0:1], 1e-5)
    memset(em, "VE", eps.ap()[:, 1:2], 1e-6)
    stats = {k: sbuf(nc, stack, "st_" + k, [1, OWN], F32).ap()
             for k in ("m", "rstd", "s2", "rstd2", "var")}
    stats_scr = (io["stats_scr"] if io and "stats_scr" in io
                 else dram_scr(nc, "stats_scr", [8, OWN], F32))
    em.barrier()

    stats["e2t"] = e2t_c
    stats_phase(em, nc, stack, banks, xfm, OWN, stats, ones["bf"], eps.ap(), want_extra=True,
                scr_dram=stats_scr)

    xp_toks, xp = build_xp(em, nc, stack, banks, ulmats)
    em.barrier()

    qkv_phase(em, nc, stack, banks, ptb, xfm, stats_scr, wq_d, wk_d, wv_d,
              qf_o, kf_o, vf_o, kt_o, vt_o, ones, onesrow, ident, ident2)
    em.barrier()

    aux_delta_phase(em, nc, stack, banks, xp, kf_o, vf_o, kt_o, vt_o, aux_o, deltas_o, ones)
    em.barrier()

    ffn_ternary(em, nc, stack, banks, ptb, xfm, stats_scr, wup_d, wdn_d, g0_o, ones, onesrow, ident)
    em.barrier()
    if standalone:
        em.replay()
        stack.close()
        return nc
    return None


def build_xp(em, nc, stack, banks, ulmats):
    xp = sbuf(nc, stack, "xp", [64, 16 * 64], BF16)
    mset = memset(em, "GE", xp.ap(), 0.0)
    smalb = sbuf(nc, stack, "smalb", [32, 32 * 16], BF16)
    tr_toks = {}
    off = 0
    for fam in ("UL", "VR", "UR", "VL"):
        for l in range(4):
            d = ulmats[fam + str(l)]
            tok = em.dma("GE", smalb.ap()[0:RL[l], off:off + BS[l]],
                         d.rearrange("b r -> r b"), allow_slow_non_contiguous=True)
            tr_toks[(fam, l)] = (tok, off)
            off += 32
    prod = sbuf(nc, stack, "prod", [32, 4 * 64], F32)
    pb = banks[0]
    place_toks = []
    last_place = {k: [] for k in range(4)}
    for l in range(4):
        for kidx, (famA, famB) in enumerate((("UL", "UL"), ("VR", "VR"), ("UL", "VL"), ("VR", "UR"))):
            ta, offa = tr_toks[(famA, l)]
            tb, offb = tr_toks[(famB, l)]
            war = pb.wr()
            mm = em.emit("TE", lambda t, offa=offa, offb=offb, l=l, kidx=kidx: t.matmul(
                pb.ap()[0:BS[l], 64 * kidx:64 * kidx + BS[l]],
                smalb.ap()[0:RL[l], offa:offa + BS[l]],
                smalb.ap()[0:RL[l], offb:offb + BS[l]], start=True, stop=True),
                waits=[ta, tb] + war)
            ev = em.emit("VE", lambda v, l=l, kidx=kidx: v.tensor_copy(
                prod.ap()[0:BS[l], 64 * kidx:64 * kidx + BS[l]],
                pb.ap()[0:BS[l], 64 * kidx:64 * kidx + BS[l]]),
                waits=[mm] + last_place[kidx])
            pb.rd(ev)
            last_place[kidx] = []
            xpoff = 64 * (kidx * 4 + l)
            for n in range(NN[l]):
                t0 = n * 2 * BS[l]
                o0 = t0 if kidx in (0, 2) else t0 + BS[l]
                tokp = em.dma("GE", xp.ap()[o0:o0 + BS[l], xpoff + o0:xpoff + o0 + BS[l]],
                              prod.ap()[0:BS[l], 64 * kidx:64 * kidx + BS[l]],
                              waits=[ev, mset])
                place_toks.append(tokp)
                last_place[kidx].append(tokp)
    return place_toks, xp


def qkv_phase(em, nc, stack, banks, ptb, xfm, st, wq_d, wk_d, wv_d,
              qf_o, kf_o, vf_o, kt_o, vt_o, ones, onesrow, ident, ident2):
    with ExitStack() as ph:
        wqk = sbuf(nc, ph, "wqk", [128, 8 * 128], BF16)
        wvv = sbuf(nc, ph, "wvv", [128, 8 * 64], BF16)
        wtoks = []
        for cch in range(8):
            wtoks.append(em.dma("GE", wqk.ap()[:, 128 * cch:128 * cch + 64],
                                wq_d[128 * cch:128 * (cch + 1), :]))
            wtoks.append(em.dma("GE", wqk.ap()[:, 128 * cch + 64:128 * cch + 128],
                                wk_d[128 * cch:128 * (cch + 1), :]))
            wtoks.append(em.dma("GE", wvv.ap()[:, 64 * cch:64 * (cch + 1)],
                                wv_d[128 * cch:128 * (cch + 1), :]))
        xr = Ring(nc, ph, 3, (128, 512), F32, "qx")
        xnr = Ring(nc, ph, 3, (128, 512), BF16, "qxn")
        fr = Ring(nc, ph, 3, (128, 512), F32, "qft")
        er = Ring(nc, ph, 4, (128, 512), F32, "qel")
        ebr = Ring(nc, ph, 3, (128, 512), BF16, "qeb")
        tmr = Ring(nc, ph, 4, (128, 64), BF16, "qtm")
        bQK, bV = banks[2], banks[3]
        mB = sbuf(nc, ph, "qmB", [128, OWN], F32)
        rB = sbuf(nc, ph, "qrB", [128, OWN], F32)
        btm = bcast_dma(em, mB.ap(), st[0:1, :], n=OWN)
        btr = bcast_dma(em, rB.ap(), st[1:2, :], n=OWN)
        for it in range(NT):
            sl = slice(512 * it, 512 * (it + 1))
            warqk = bQK.wr()
            warv = bV.wr()
            mmq = mmv = None
            for cch in range(8):
                xa, xw, xi = xr.next()
                lt = em.dma("SY", xa, xfm[128 * cch:128 * (cch + 1), sl], waits=xw)
                fa, fw, fi = fr.next()
                t1 = em.emit("VE", lambda v, fa=fa, xa=xa, sl=sl: v.tensor_sub(fa, xa, mB.ap()[:, sl]),
                             waits=[lt, btm] + fw)
                na, nw, ni = xnr.next()
                t2 = em.emit("VE", lambda v, na=na, fa=fa, sl=sl: v.tensor_mul(na, fa, rB.ap()[:, sl]),
                             waits=[t1, btr] + nw)
                xr.used(xi, t1)
                fr.used(fi, t2)
                mmq = em.emit("TE", lambda t, cch=cch, na=na: t.matmul(
                    bQK.ap()[0:128, 0:512], wqk.ap()[:, 128 * cch:128 * (cch + 1)],
                    na, start=(cch == 0), stop=(cch == 7)),
                    waits=[t2] + wtoks + (warqk if cch == 0 else []))
                mmv = em.emit("TE", lambda t, cch=cch, na=na: t.matmul(
                    bV.ap()[0:64, 0:512], wvv.ap()[:, 64 * cch:64 * (cch + 1)],
                    na, start=(cch == 0), stop=(cch == 7)),
                    waits=[t2] + (warv if cch == 0 else []))
                xnr.used(ni, mmq)
                xnr.used(ni, mmv)
            ea, ew, ei = er.next()
            t1 = em.emit("VE", lambda v, ea=ea: v.tensor_scalar_min(ea, bQK.ap()[0:128, 0:512], 0.0),
                         waits=[mmq] + ew)
            e2a, ew2, ei2 = er.next()
            t1b = em.emit("SE", lambda s, ea=ea, e2a=e2a: s.activation(out=e2a, in_=ea, func=AF.Exp),
                          waits=[t1] + ew2)
            fa2, fw2, fi2 = er.next()
            t2 = em.emit("VE", lambda v, fa2=fa2: v.tensor_scalar_max(fa2, bQK.ap()[0:128, 0:512], 0.0),
                         waits=[mmq] + fw2)
            bQK.rd(t1)
            bQK.rd(t2)
            qa, qw, qi = ebr.next()
            t3 = em.emit("VE", lambda v, qa=qa, e2a=e2a, fa2=fa2: v.tensor_add(qa, e2a, fa2),
                         waits=[t1b, t2] + qw)
            er.used(ei, t1b)
            er.used(ei2, t3)
            er.used(fi2, t3)
            s1 = em.dma("SY", qf_o[:, sl], qa[0:64, :], waits=[t3])
            s2 = em.dma("SY", kf_o[:, sl], qa[64:128, :], waits=[t3])
            ebr.used(qi, s1)
            ebr.used(qi, s2)
            va, vw, vi = ebr.next()
            t4 = em.emit("VE", lambda v, va=va: v.tensor_copy(va[0:64, :], bV.ap()[0:64, 0:512]),
                         waits=[mmv] + vw)
            bV.rd(t4)
            s3 = em.dma("SY", vf_o[:, sl], va[0:64, :], waits=[t4])
            ebr.used(vi, s3)
            for ck in range(4):
                gt = 512 * it + 128 * ck
                wtr = ptb.wr()
                trk = em.emit("TE", lambda t, qa=qa, ck=ck: t.transpose(
                    ptb.ap()[0:128, 0:64], qa[64:128, 128 * ck:128 * (ck + 1)],
                    ident2["bf"][64:128, 0:64]), waits=[t3] + wtr)
                ka, kw, ki = tmr.next()
                ev1 = em.emit("VE", lambda v, ka=ka: v.tensor_copy(ka, ptb.ap()[0:128, 0:64]),
                              waits=[trk] + kw)
                ptb.rd(ev1)
                sk = em.dma("SY", kt_o[gt:gt + 128, :], ka, waits=[ev1])
                tmr.used(ki, sk)
                trv = em.emit("TE", lambda t, va=va, ck=ck: t.transpose(
                    ptb.ap()[0:128, 64:128], va[0:64, 128 * ck:128 * (ck + 1)],
                    ident["bf"][0:64, 0:64]), waits=[t4, ev1])
                va2, vw2, vi2 = tmr.next()
                ev2 = em.emit("VE", lambda v, va2=va2: v.tensor_copy(va2, ptb.ap()[0:128, 64:128]),
                              waits=[trv] + vw2)
                ptb.rd(ev2)
                sv = em.dma("SY", vt_o[gt:gt + 128, :], va2, waits=[ev2])
                tmr.used(vi2, sv)
                ebr.used(qi, trk)
                ebr.used(vi, trv)


def aux_delta_phase(em, nc, stack, banks, xp, kf_o, vf_o, kt_o, vt_o, aux_o, deltas_o, ones):
    with ExitStack() as ph:
        zpad = sbuf(nc, ph, "zpad", [63, 64], F32)
        zpad_tok = memset(em, "VE", zpad.ap(), 0.0)
        kfr = Ring(nc, ph, 3, (64, 128), BF16, "akf")
        vfr = Ring(nc, ph, 3, (64, 128), BF16, "avf")
        ktr = Ring(nc, ph, 3, (128, 64), BF16, "akt")
        vtr = Ring(nc, ph, 3, (128, 64), BF16, "avt")
        auxtr = Ring(nc, ph, 10, (128, 256), BF16, "aaux")
        d1r = Ring(nc, ph, 3, (64, 320), F32, "adl1")
        d2r = Ring(nc, ph, 3, (64, 256), F32, "adl2")
        zr = Ring(nc, ph, 3, (1, 64), F32, "adlz")
        bsets = [(banks[0], banks[1]), (banks[4], banks[5])]
        bD1, bD2 = banks[2], banks[3]
        state = {}

        def stage_p(ch):
            gt = 128 * ch
            kfa, kfw, kfi = kfr.next()
            lt1 = em.dma("SY", kfa, kf_o[:, gt:gt + 128], waits=kfw)
            vfa, vfw, vfi = vfr.next()
            lt2 = em.dma("SY", vfa, vf_o[:, gt:gt + 128], waits=vfw)
            kta, ktw, kti = ktr.next()
            lt3 = em.dma("SY", kta, kt_o[gt:gt + 128, :], waits=ktw)
            vta, vtw, vti = vtr.next()
            lt4 = em.dma("SY", vta, vt_o[gt:gt + 128, :], waits=vtw)
            bs = bsets[ch % 2]
            mms = []
            for kind in range(4):
                srct, srctok = (kfa, lt1) if kind in (0, 2) else (vfa, lt2)
                bank = bs[kind // 2]
                poff = 256 * (kind % 2)
                war = bank.wr() if kind < 2 else []
                mms.append(em.emit("TE", lambda t, srct=srct, kind=kind, poff=poff, bank=bank: t.matmul(
                    bank.ap()[0:128, poff:poff + 256], srct,
                    xp.ap()[0:64, 256 * kind:256 * (kind + 1)],
                    start=True, stop=True), waits=[srctok] + war))
            state[ch] = dict(mms=mms, kta=kta, kti=kti, vta=vta, vti=vti,
                             lt3=lt3, lt4=lt4, kfi=kfi, vfi=vfi, bs=bs)

        def stage_d(ch):
            st = state.pop(ch)
            gt = 128 * ch
            bs = st["bs"]
            tiles = {}
            for kind in range(4):
                bank = bs[kind // 2]
                poff = 256 * (kind % 2)
                aa, aw, ai = auxtr.next()
                ev = em.emit("VE", lambda v, aa=aa, poff=poff, bank=bank: v.tensor_copy(
                    aa, bank.ap()[0:128, poff:poff + 256]), waits=[st["mms"][-1]] + aw)
                bank.rd(ev)
                stA = em.dma("SY", aux_o[kind, gt:gt + 128].rearrange("p l c -> p (l c)"),
                             aa, waits=[ev])
                auxtr.used(ai, stA)
                tiles[kind] = (aa, ev, ai)
            war1 = bD1.wr()
            war2 = bD2.wr()
            mmz = em.emit("TE", lambda t, kta=st["kta"]: t.matmul(
                bD1.ap()[0:1, 320:384], ones["bf"], kta, start=True, stop=True),
                waits=[st["lt3"]] + war1)
            mm0 = em.emit("TE", lambda t, kta=st["kta"], vta=st["vta"]: t.matmul(
                bD1.ap()[0:64, 0:64], vta, kta, start=True, stop=True),
                waits=[st["lt3"], st["lt4"]])
            last1 = mm0
            for l in range(4):
                vva, vvev, vvi = tiles[1]
                kua, kuev, kui = tiles[0]
                last1 = em.emit("TE", lambda t, vva=vva, kua=kua, l=l: t.matmul(
                    bD1.ap()[0:64, 64 + 64 * l:128 + 64 * l],
                    vva[:, 64 * l:64 * (l + 1)], kua[:, 64 * l:64 * (l + 1)],
                    start=True, stop=True), waits=[vvev, kuev])
                auxtr.used(vvi, last1)
                auxtr.used(kui, last1)
            last2 = None
            for l in range(4):
                kva, kvev, kvi = tiles[2]
                vua, vuev, vui = tiles[3]
                last2 = em.emit("TE", lambda t, kva=kva, vua=vua, l=l: t.matmul(
                    bD2.ap()[0:64, 64 * l:64 * (l + 1)],
                    kva[:, 64 * l:64 * (l + 1)], vua[:, 64 * l:64 * (l + 1)],
                    start=True, stop=True), waits=[kvev, vuev] + (war2 if l == 0 else []))
                auxtr.used(kvi, last2)
                auxtr.used(vui, last2)
            d1a, d1w, d1i = d1r.next()
            ev1 = em.emit("VE", lambda v, d1a=d1a: v.tensor_copy(d1a, bD1.ap()[0:64, 0:320]),
                          waits=[last1, mmz] + d1w)
            bD1.rd(ev1)
            st1 = None
            for s_ in range(5):
                st1 = em.dma("SY", deltas_o[ch, s_], d1a[:, 64 * s_:64 * (s_ + 1)], waits=[ev1])
                d1r.used(d1i, st1)
            za, zw, zi = zr.next()
            evz = em.emit("VE", lambda v, za=za: v.tensor_copy(za, bD1.ap()[0:1, 320:384]),
                          waits=[mmz, last1] + zw)
            bD1.rd(evz)
            stz = em.dma("SY", deltas_o[ch, 9, 0:1, :], za, waits=[evz])
            zr.used(zi, stz)
            em.dma("SY", deltas_o[ch, 9, 1:64, :], zpad.ap(), waits=[zpad_tok])
            d2a, d2w, d2i = d2r.next()
            ev2 = em.emit("VE", lambda v, d2a=d2a: v.tensor_copy(d2a, bD2.ap()[0:64, 0:256]),
                          waits=[last2] + d2w)
            bD2.rd(ev2)
            for s_ in range(4):
                st2 = em.dma("SY", deltas_o[ch, 5 + s_], d2a[:, 64 * s_:64 * (s_ + 1)], waits=[ev2])
                d2r.used(d2i, st2)
            ktr.used(st["kti"], mm0)
            ktr.used(st["kti"], mmz)
            vtr.used(st["vti"], mm0)
            kfr.used(st["kfi"], st["mms"][-1])
            vfr.used(st["vfi"], st["mms"][-1])

        for ch in range(NCH):
            stage_p(ch)
            if ch >= 1:
                stage_d(ch - 1)
        stage_d(NCH - 1)


def ternary_prep(em, nc, wtT_dram, banks, ptb, wd, FDIM, DDIM, ones, onesrow, ident, name):
    nfch = FDIM // 128
    DW = min(DDIM, 1024)
    ncs = DDIM // DW
    ph = ExitStack()
    wr = Ring(nc, ph, 3, (128, DW), F32, name + "wf")
    mr = Ring(nc, ph, 2, (128, DW), F32, name + "msk")
    sgr = Ring(nc, ph, 2, (128, DW), F32, name + "sgn")
    wtb = Ring(nc, ph, 3, (128, DW), BF16, name + "wtb")
    tcol = Ring(nc, ph, 2, (128, 1), F32, name + "tc")
    acc = sbuf(nc, ph, name + "acc", [128, 1], F32)
    alpha = sbuf(nc, ph, name + "al", [128, 1], F32)
    z0 = memset(em, "VE", acc.ap(), 0.0)
    atoks = [z0]
    for fc in range(nfch):
        for cs in range(ncs):
            wa, ww, wis = wr.next()
            lt = em.dma("SY", wa, wd[128 * fc:128 * (fc + 1), DW * cs:DW * (cs + 1)], waits=ww)
            ta, tw, ti = tcol.next()
            rt = em.emit("VE", lambda v, wa=wa, ta=ta: v.tensor_reduce(
                ta, wa, AX.X, ALU.add, apply_absolute_value=True), waits=[lt] + tw)
            ad = em.emit("VE", lambda v, ta=ta: v.tensor_add(acc.ap(), acc.ap(), ta), waits=[rt] + atoks)
            tcol.used(ti, ad)
            wr.used(wis, rt)
            atoks = [ad]
    pb = banks[0]
    war = pb.wr()
    spl = sbuf(nc, ph, name + "spl", [128, 4], BF16)
    spf = sbuf(nc, ph, name + "spf", [128, 2], F32)
    s1 = em.emit("VE", lambda v: v.tensor_copy(spl.ap()[:, 0:1], acc.ap()), waits=atoks)
    s2 = em.emit("VE", lambda v: v.tensor_sub(spf.ap()[:, 0:1], acc.ap(), spl.ap()[:, 0:1]), waits=[s1])
    s3 = em.emit("VE", lambda v: v.tensor_copy(spl.ap()[:, 1:2], spf.ap()[:, 0:1]), waits=[s2])
    s4 = em.emit("VE", lambda v: v.tensor_sub(spf.ap()[:, 1:2], spf.ap()[:, 0:1], spl.ap()[:, 1:2]),
                 waits=[s3])
    s5 = em.emit("VE", lambda v: v.tensor_copy(spl.ap()[:, 2:3], spf.ap()[:, 1:2]), waits=[s4])
    mm = None
    for j, stok in ((0, s1), (1, s3), (2, s5)):
        mm = em.emit("TE", lambda t, j=j: t.matmul(pb.ap()[0:1, 0:1], ones["bf"], spl.ap()[:, j:j + 1],
                                                   start=(j == 0), stop=(j == 2)),
                     waits=[stok] + (war if j == 0 else []))
    th = em.emit("VE", lambda v: v.tensor_scalar_mul(alpha.ap()[0:1, :], pb.ap()[0:1, 0:1],
                                                     0.5 / (FDIM * DDIM)), waits=[mm])
    pb.rd(th)
    war = pb.wr()
    b1 = em.emit("VE", lambda v: v.tensor_copy(spl.ap()[0:1, 3:4], alpha.ap()[0:1, 0:1]), waits=[th])
    b2 = em.emit("VE", lambda v: v.tensor_sub(spf.ap()[0:1, 1:2], alpha.ap()[0:1, 0:1],
                                              spl.ap()[0:1, 3:4]), waits=[b1])
    b3 = em.emit("VE", lambda v: v.tensor_copy(spl.ap()[0:1, 2:3], spf.ap()[0:1, 1:2]), waits=[b2])
    ob = em.onesrow_bf
    mb1 = em.emit("TE", lambda t, ob=ob: t.matmul(pb.ap()[0:128, 4:5], ob, spl.ap()[0:1, 3:4],
                                           start=True, stop=False), waits=[b1] + war)
    mb2 = em.emit("TE", lambda t, ob=ob: t.matmul(pb.ap()[0:128, 4:5], ob, spl.ap()[0:1, 2:3],
                                           start=False, stop=True), waits=[b3])
    thr_ev = em.emit("VE", lambda v: v.tensor_copy(alpha.ap(), pb.ap()[0:128, 4:5]), waits=[mb2])
    pb.rd(thr_ev)
    for fc in range(nfch):
        for cs in range(ncs):
            wa, ww, wis = wr.next()
            lt = em.dma("SY", wa, wd[128 * fc:128 * (fc + 1), DW * cs:DW * (cs + 1)], waits=ww)
            ma, mw, mi = mr.next()
            t1 = em.emit("SE", lambda s, wa=wa, ma=ma: s.activation(out=ma, in_=wa, func=AF.Abs),
                         waits=[lt] + mw)
            sa, sw, si = sgr.next()
            t3 = em.emit("SE", lambda s, wa=wa, sa=sa: s.activation(out=sa, in_=wa, func=AF.Sign),
                         waits=[lt] + sw)
            wr.used(wis, t1)
            wr.used(wis, t3)
            ba, bw, bi = wtb.next()
            t4 = em.emit("VE", lambda v, ba=ba, ma=ma, sa=sa: v.scalar_tensor_tensor(
                ba, ma, alpha.ap()[:, 0:1], sa, ALU.is_gt, ALU.mult),
                waits=[t1, t3, thr_ev] + bw)
            mr.used(mi, t4)
            sgr.used(si, t4)
            wtr = ptb.wr()
            tts = []
            for j in range(DW // 128):
                tts.append(em.emit("TE", lambda t, ba=ba, j=j: t.transpose(
                    ptb.ap()[0:128, 128 * j:128 * (j + 1)], ba[:, 128 * j:128 * (j + 1)],
                    ident["bf"]), waits=[t4] + (wtr if j == 0 else [])))
            ta5, tw5, ti5 = wtb.next()
            ev = em.emit("VE", lambda v, ta5=ta5: v.tensor_copy(ta5, ptb.ap()[0:128, 0:DW]),
                         waits=tts + tw5)
            ptb.rd(ev)
            for j in range(DW // 128):
                dc = cs * (DW // 128) + j
                st5 = em.dma("SY", wtT_dram[:, FDIM * dc + 128 * fc:FDIM * dc + 128 * (fc + 1)],
                             ta5[:, 128 * j:128 * (j + 1)], waits=[ev])
                wtb.used(ti5, st5)
            wtb.used(bi, tts[-1])
    em.barrier()
    ph.close()


def ffn_ternary(em, nc, stack, banks, ptb, xfm, st, wup_d, wdn_d, g0_o, ones, onesrow, ident):
    hkind = "ExternalOutput" if getattr(ffn_ternary, "_dbghst", False) else "Internal"
    h_d = nc.dram_tensor("h_scr", [4096, OWN], BF16, kind=hkind).ap()
    if getattr(ffn_ternary, "_dbghst", False):
        ffn_ternary._xno = nc.dram_tensor("xn_dbg_o", [1024, 512], BF16, kind="ExternalOutput").ap()
        ffn_ternary._bco = nc.dram_tensor("bcb_dbg", [4, 512], F32, kind="ExternalOutput").ap()
    hst_a = sbuf(nc, stack, "hst_a", [1, OWN], F32)
    hst_b = sbuf(nc, stack, "hst_b", [1, OWN], F32)
    hst_scr = dram_scr(nc, "hst_scr", [1, OWN], F32)

    wtup_d = dram_scr(nc, "wtup_scr", [128, 8 * 4096], BF16)
    wtdn_d = dram_scr(nc, "wtdn_scr", [128, 32 * 1024], BF16)
    ternary_prep(em, nc, wtup_d, banks, ptb, wup_d, 4096, 1024, ones, onesrow, ident, "wtu")
    ternary_prep(em, nc, wtdn_d, banks, ptb, wdn_d, 1024, 4096, ones, onesrow, ident, "wtd")

    with ExitStack() as ph:
        wtup = sbuf(nc, ph, "wtu", [128, 8 * 4096], BF16)
        for dc in range(8):
            em.dma("SY", wtup.ap()[:, 4096 * dc:4096 * (dc + 1)], wtup_d[:, 4096 * dc:4096 * (dc + 1)])
        em.barrier()
        xr = Ring(nc, ph, 3, (128, 512), F32, "ux")
        fr = Ring(nc, ph, 3, (128, 512), F32, "uf")
        xnr = Ring(nc, ph, 10, (128, 512), BF16, "uxn")
        hr = Ring(nc, ph, 3, (128, 512), BF16, "uh")
        sqr = Ring(nc, ph, 3, (128, 512), BF16, "usq")
        bH0, bH1, bS = banks[2], banks[3], banks[4]
        mB = sbuf(nc, ph, "umB", [128, OWN], F32)
        sB = sbuf(nc, ph, "usB", [128, OWN], F32)
        btm = bcast_dma(em, mB.ap(), st[0:1, :], n=OWN)
        bts = bcast_dma(em, sB.ap(), st[2:3, :], n=OWN)
        for it in range(NT):
            sl = slice(512 * it, 512 * (it + 1))
            xn_list = []
            for cch in range(8):
                xa, xw, xi = xr.next()
                lt = em.dma("SY", xa, xfm[128 * cch:128 * (cch + 1), sl], waits=xw)
                fa, fw, fi = fr.next()
                t1 = em.emit("VE", lambda v, fa=fa, xa=xa, sl=sl: v.tensor_sub(fa, xa, mB.ap()[:, sl]),
                             waits=[lt, btm] + fw)
                na, nw, ni = xnr.next()
                t2 = em.emit("VE", lambda v, na=na, fa=fa, sl=sl: v.tensor_mul(na, fa, sB.ap()[:, sl]),
                             waits=[t1, bts] + nw)
                xr.used(xi, t1)
                fr.used(fi, t2)
                xn_list.append((na, t2, ni))
                if getattr(ffn_ternary, "_dbghst", False) and it == 0:
                    dtok = em.dma("SY", ffn_ternary._xno[128 * cch:128 * (cch + 1), :], na, waits=[t2])
                    xnr.used(ni, dtok)
            if getattr(ffn_ternary, "_dbghst", False) and it == 0:
                em.dma("SY", ffn_ternary._bco[0:1, :], mB.ap()[0:1, 0:512], waits=[btm])
                em.dma("SY", ffn_ternary._bco[1:2, :], sB.ap()[0:1, 0:512], waits=[bts])
                em.dma("SY", ffn_ternary._bco[2:3, :], sB.ap()[64:65, 0:512], waits=[bts])
            last_hm = None
            for fc in range(32):
                hb = bH0 if fc % 2 == 0 else bH1
                war = hb.wr()
                mml = None
                for cch in range(8):
                    na, t2, ni = xn_list[cch]
                    mml = em.emit("TE", lambda t, fc=fc, cch=cch, na=na, hb=hb: t.matmul(
                        hb.ap()[0:128, 0:512], wtup.ap()[:, 4096 * cch + 128 * fc:4096 * cch + 128 * (fc + 1)],
                        na, start=(cch == 0), stop=(cch == 7)),
                        waits=[t2] + (war if cch == 0 else []))
                ha, hw, hi = hr.next()
                ge = em.emit("SE", lambda s, ha=ha, hb=hb: s.activation(
                    out=ha, in_=hb.ap()[0:128, 0:512], func=AF.Gelu), waits=[mml] + hw)
                hb.rd(ge)
                sto = em.dma("SY", h_d[128 * fc:128 * (fc + 1), sl], ha, waits=[ge])
                qa, qw, qi = sqr.next()
                sq = em.emit("SE", lambda s, qa=qa, ha=ha: s.activation(out=qa, in_=ha, func=AF.Square),
                             waits=[ge] + qw)
                warS = bS.wr() if fc == 0 else []
                hm = em.emit("TE", lambda t, qa=qa, fc=fc: t.matmul(
                    bS.ap()[32 * (fc % 3):32 * (fc % 3) + 1, 0:512], ones["bf"], qa,
                    start=(fc < 3), stop=(fc >= 29)),
                    waits=[sq] + warS)
                sqr.used(qi, hm)
                hr.used(hi, sto)
                hr.used(hi, sq)
                last_hm = hm
            for na, t2, ni in xn_list:
                xnr.used(ni, last_hm)
            ta1 = em.emit("VE", lambda v, sl=sl: v.tensor_copy(hst_a.ap()[0:1, sl], bS.ap()[0:1, 0:512]),
                          waits=[last_hm])
            ta2 = em.emit("VE", lambda v, sl=sl: v.tensor_add(hst_a.ap()[0:1, sl], hst_a.ap()[0:1, sl],
                                                              bS.ap()[32:33, 0:512]), waits=[ta1])
            ta3 = em.emit("VE", lambda v, sl=sl: v.tensor_add(hst_a.ap()[0:1, sl], hst_a.ap()[0:1, sl],
                                                              bS.ap()[64:65, 0:512]), waits=[ta2])
            bS.rd(ta1)
            bS.rd(ta2)
            bS.rd(ta3)
            t5 = em.emit("SE", lambda s, sl=sl: s.activation(out=hst_a.ap()[0:1, sl],
                                                             in_=hst_a.ap()[0:1, sl], func=AF.Sqrt),
                         waits=[ta3])
            t6 = em.emit("VE", lambda v, sl=sl: v.tensor_scalar(
                hst_a.ap()[0:1, sl], hst_a.ap()[0:1, sl], 1.0 / 64.0, 1e-8, ALU.mult, ALU.add), waits=[t5])
            t7 = em.emit("VE", lambda v, sl=sl: v.reciprocal(hst_b.ap()[0:1, sl], hst_a.ap()[0:1, sl]),
                         waits=[t6])
            em.dma("SY", hst_scr[0:1, sl], hst_b.ap()[0:1, sl], waits=[t7])
        em.barrier()

    with ExitStack() as ph:
        wtdn = sbuf(nc, ph, "wtd", [128, 32 * 1024], BF16)
        for dc in range(32):
            em.dma("SY", wtdn.ap()[:, 1024 * dc:1024 * (dc + 1)], wtdn_d[:, 1024 * dc:1024 * (dc + 1)])
        em.barrier()
        hrd = Ring(nc, ph, 34, (128, 512), BF16, "dh")
        glr = Ring(nc, ph, 3, (128, 512), BF16, "dg")
        fgr = Ring(nc, ph, 3, (128, 512), F32, "dgf")
        rbB = sbuf(nc, ph, "drbB", [128, OWN], F32)
        bG0, bG1 = banks[2], banks[3]
        trb = bcast_dma(em, rbB.ap(), hst_scr[0:1, :], n=OWN)
        for it in range(NT):
            sl = slice(512 * it, 512 * (it + 1))
            hn_list = []
            for fc in range(32):
                ha, hw, hi = hrd.next()
                lt = em.dma("SY", ha, h_d[128 * fc:128 * (fc + 1), sl], waits=hw)
                hn_list.append((ha, lt, hi))
            last = None
            for dc in range(8):
                gb = bG0 if dc % 2 == 0 else bG1
                war = gb.wr()
                mml = None
                for fc in range(32):
                    ha, t1, hi = hn_list[fc]
                    mml = em.emit("TE", lambda t, dc=dc, fc=fc, ha=ha, gb=gb: t.matmul(
                        gb.ap()[0:128, 0:512], wtdn.ap()[:, 1024 * fc + 128 * dc:1024 * fc + 128 * (dc + 1)],
                        ha, start=(fc == 0), stop=(fc == 31)),
                        waits=[t1] + (war if fc == 0 else []))
                fa2, fw2, fi2 = fgr.next()
                tm2 = em.emit("VE", lambda v, fa2=fa2, gb=gb, sl=sl: v.tensor_mul(
                    fa2, gb.ap()[0:128, 0:512], rbB.ap()[:, sl]), waits=[mml, trb] + fw2)
                gb.rd(tm2)
                ga, gw, gi = glr.next()
                si = em.emit("SE", lambda s, ga=ga, fa2=fa2: s.activation(out=ga, in_=fa2, func=AF.Silu),
                             waits=[tm2] + gw)
                fgr.used(fi2, si)
                sto = em.dma("SY", g0_o[128 * dc:128 * (dc + 1), sl], ga, waits=[si])
                glr.used(gi, sto)
                last = sto
            for ha, t1, hi in hn_list:
                hrd.used(hi, last)




def build_L0B(nch=NCH, upto=99, io=None, ctx=None):
    nc, stack, em, banks, ptb, standalone = _ctx(ctx)

    xfm = _io_in(io, nc, "xfm", [1024, OWN])
    qf_d = _io_in(io, nc, "qf", [64, OWN], BF16)
    kf_d = _io_in(io, nc, "kf", [64, OWN], BF16)
    vf_d = _io_in(io, nc, "vf", [64, OWN], BF16)
    kt_d = _io_in(io, nc, "kt", [OWN, 64], BF16)
    vt_d = _io_in(io, nc, "vt", [OWN, 64], BF16)
    aux_d = _io_in(io, nc, "aux", [4, OWN, 4, 64], BF16)
    g0_d = _io_in(io, nc, "g0", [1024, OWN], BF16)
    dprev_d = _io_in(io, nc, "dprev", [NCH, 10, 64, 64], F32)
    wo_d = _io_in(io, nc, "l0wo", [64, 1024])
    ulmats = {fam + str(l): _io_in(io, nc, f"l0{fam}{l}", [BS[l], RL[l]])
              for fam in ("UL", "VR", "UR", "VL") for l in range(4)}
    x1_o = _io_out(io, nc, "x1", [1024, OWN], F32)

    ones = _lc(io, em, nc, stack, "c_ones", [128, 1], want_bf=True)
    onesrow = _lc(io, em, nc, stack, "c_onesrow", [1, 128])
    ident = _lc(io, em, nc, stack, "c_ident", [128, 128], want_bf=True)
    triu = _lc(io, em, nc, stack, "c_triu", [128, 128], want_bf=True)
    masks = _lc(io, em, nc, stack, "c_l0masks", [64, 9 * 64])
    onesrow_b = sbuf(nc, stack, "onesrow_bf", [1, 128], BF16)
    em.emit("VE", lambda v: v.tensor_copy(onesrow_b.ap(), onesrow["f32"]), waits=[onesrow["ftok"]])
    em.onesrow_bf = onesrow_b.ap()
    em.bc_hi = Ring(nc, stack, 4, (1, 512), BF16, "bch")
    em.bc_lo = Ring(nc, stack, 4, (1, 512), BF16, "bcl")
    em.bc_f = Ring(nc, stack, 4, (1, 512), F32, "bcf")
    em.barrier()

    NT_L = [ (NN[l] + 2) // 3 for l in range(4) ]
    LOFF = [0]
    for l in range(4):
        LOFF.append(LOFF[-1] + NT_L[l])
    FAMT = LOFF[-1]
    xpp = sbuf(nc, stack, "xpp", [64, 3 * FAMT * 128], BF16)
    mset = memset(em, "GE", xpp.ap(), 0.0)
    smal = sbuf(nc, stack, "smalB", [32, 16 * 32], BF16)
    ltoks = {}
    off = 0
    for fam in ("VR", "UL", "VL"):
        for l in range(4):
            d = ulmats[fam + str(l)]
            tok = em.dma("GE", smal.ap()[0:BS[l], off:off + RL[l]], d)
            ltoks[(fam, l)] = (tok, off)
            off += 32
    ptoks = []
    for fi, fam in enumerate(("VR", "UL", "VL")):
        for l in range(4):
            tok, soff = ltoks[(fam, l)]
            for n in range(NN[l]):
                tile = n // 3
                slot = 32 * (n % 3)
                base = (fi * FAMT + LOFF[l] + tile) * 128
                t0 = n * 2 * BS[l]
                o0 = t0 + BS[l] if fam == "VR" else t0
                ptoks.append(em.dma("GE", xpp.ap()[o0:o0 + BS[l], base + slot:base + slot + RL[l]],
                                    smal.ap()[0:BS[l], soff:soff + RL[l]], waits=[tok, mset]))
    leafp = sbuf(nc, stack, "leafp", [64, 6 * 128], BF16)
    mset2 = memset(em, "GE", leafp.ap(), 0.0)
    for leaf in range(16):
        tile = leaf // 3
        slot = 32 * (leaf % 3)
        ptoks.append(em.dma("GE", leafp.ap()[4 * leaf:4 * leaf + 4, 128 * tile + slot:128 * tile + slot + 4],
                            ident["f32"][0:4, 0:4], waits=[ident["ftok"], mset2]))
    em.barrier()

    P_f = sbuf(nc, stack, "P_f", [64, 64], F32)
    z_f = sbuf(nc, stack, "z_f", [64, 1], F32)
    P_b = sbuf(nc, stack, "P_b", [64, 64], BF16)
    z_b = sbuf(nc, stack, "z_b", [64, 1], BF16)
    with ExitStack() as ph:
        dsum = sbuf(nc, ph, "dsum", [64, 640], F32)
        dld = Ring(nc, ph, 6, (64, 640), F32, "dld")
        d8 = io.get("dprev8") if io else None
        if d8 is not None:
            ap8, g8a = d8
            acc = memset(em, "VE", dsum.ap(), 0.0)
            for j in range(8):
                for ch in range(NCH):
                    ap3 = ap8[ch // 4, j].rearrange("c s r k -> c r s k")
                    aa, ww, ii = dld.next()
                    lt = em.dma("SY", aa.rearrange("r (s k) -> r s k", s=10), ap3[ch % 4],
                                waits=ww)
                    t = em.emit("VE", lambda v, aa=aa, j=j, g8a=g8a: v.tensor_scalar(
                        aa, aa, g8a[0:64, j:j + 1], None, ALU.mult), waits=[lt])
                    acc = em.emit("VE", lambda v, aa=aa: v.tensor_add(dsum.ap(), dsum.ap(), aa),
                                  waits=[t, acc])
                    dld.used(ii, acc)
        else:
            ap3 = dprev_d.rearrange("c s r k -> c r s k")
            a0, w0, i0 = dld.next()
            t0 = em.dma("SY", dsum.ap().rearrange("r (s k) -> r s k", s=10), ap3[0], waits=w0)
            acc = t0
            for ch in range(1, NCH):
                aa, ww, ii = dld.next()
                lt = em.dma("SY", aa.rearrange("r (s k) -> r s k", s=10), ap3[ch], waits=ww)
                acc = em.emit("VE", lambda v, aa=aa: v.tensor_add(dsum.ap(), dsum.ap(), aa),
                              waits=[lt, acc])
                dld.used(ii, acc)
        z0m = memset(em, "VE", P_f.ap(), 0.0, waits=[acc])
        prev = z0m
        tmp = sbuf(nc, ph, "tmpm", [64, 64], F32)
        for k in range(9):
            t1 = em.emit("VE", lambda v, k=k: v.tensor_mul(tmp.ap(), dsum.ap()[:, 64 * k:64 * (k + 1)],
                                                           masks["f32"][:, 64 * k:64 * (k + 1)]),
                         waits=[acc, prev, masks["ftok"]])
            prev = em.emit("VE", lambda v: v.tensor_add(P_f.ap(), P_f.ap(), tmp.ap()), waits=[t1])
        zr = sbuf(nc, ph, "zr", [1, 64], BF16)
        tz = em.emit("VE", lambda v: v.tensor_copy(zr.ap(), dsum.ap()[0:1, 576:640]), waits=[acc])
        wtr = ptb.wr()
        ttr = em.emit("TE", lambda t: t.transpose(ptb.ap()[0:64, 0:1], zr.ap(), ident["bf"][0:1, 0:1]),
                      waits=[tz] + wtr)
        tz2 = em.emit("VE", lambda v: v.tensor_copy(z_f.ap(), ptb.ap()[0:64, 0:1]), waits=[ttr])
        ptb.rd(tz2)
        em.barrier()

    build_L0B._pbr = []
    build_L0B._zbr = []
    zer = sbuf(nc, stack, "zer", [64, 72], BF16)
    zer_tok = memset(em, "VE", zer.ap(), 0.0)
    sfm = sbuf(nc, stack, "sfm", [64, OWN], BF16)
    with ExitStack() as ph:
        qr = Ring(nc, ph, 2, (64, 128), BF16, "sq")
        kr = Ring(nc, ph, 2, (64, 128), BF16, "sk")
        vr = Ring(nc, ph, 2, (64, 128), BF16, "sv")
        ktr = Ring(nc, ph, 2, (128, 64), BF16, "skt")
        vtr = Ring(nc, ph, 2, (128, 64), BF16, "svt")
        auxr = Ring(nc, ph, 18, (128, 64), BF16, "sax")
        padr = Ring(nc, ph, 42, (128, 128), BF16, "spd")
        gramr = Ring(nc, ph, 4, (128, 128), BF16, "sgr")
        den1 = Ring(nc, ph, 2, (128, 1), F32, "sd1")
        den2 = Ring(nc, ph, 2, (128, 1), F32, "sd2")
        otm = Ring(nc, ph, 2, (128, 64), BF16, "sot")
        stm = Ring(nc, ph, 2, (64, 64), F32, "sst")
        bP0, bP1, bY, bS1, bS2, bG0, bG1 = (banks[0], banks[1], banks[2], banks[3],
                                            banks[4], banks[5], banks[6])
        for ch in range(nch):
            gt = 128 * ch
            qa, qw, qi = qr.next()
            ltq = em.dma("SY", qa, qf_d[:, gt:gt + 128], waits=qw)
            ka, kw, ki = kr.next()
            ltk = em.dma("SY", ka, kf_d[:, gt:gt + 128], waits=kw)
            va, vw, vi = vr.next()
            ltv = em.dma("SY", va, vf_d[:, gt:gt + 128], waits=vw)
            kta, ktw, kti = ktr.next()
            ltkt = em.dma("SY", kta, kt_d[gt:gt + 128, :], waits=ktw)
            vta, vtw, vti = vtr.next()
            ltvt = em.dma("SY", vta, vt_d[gt:gt + 128, :], waits=vtw)
            aux = {}
            for kind in range(4):
                for l in range(4):
                    aa, aw, ai = auxr.next()
                    lt = em.dma("SY", aa, aux_d[kind, gt:gt + 128, l, :], waits=aw)
                    aux[(kind, l)] = (aa, lt, ai)
            if upto < 1:
                continue
            pads = {}
            pad_jobs = []
            for l in range(4):
                for tile in range(NT_L[l]):
                    pad_jobs.append(("Qr", l, tile, 0, qa, ltq))
                    pad_jobs.append(("Vr", l, tile, 0, va, ltv))
                    pad_jobs.append(("Kl", l, tile, 1, ka, ltk))
                    pad_jobs.append(("Ql", l, tile, 2, qa, ltq))
            for tile in range(6):
                pad_jobs.append(("Lq", 0, tile, None, qa, ltq))
                pad_jobs.append(("Lv", 0, tile, None, va, ltv))
            for jn, job in enumerate(pad_jobs):
                namej, l, tile, fi, srca, srct = job
                bnk = bP0 if jn % 2 == 0 else bP1
                war = bnk.wr()
                if namej in ("Lq", "Lv"):
                    lhs = leafp.ap()[:, 128 * tile:128 * (tile + 1)]
                else:
                    base = (fi * FAMT + LOFF[l] + tile) * 128
                    lhs = xpp.ap()[:, base:base + 128]
                mm = em.emit("TE", lambda t, lhs=lhs, srca=srca, bnk=bnk: t.matmul(
                    bnk.ap()[0:128, 0:128], lhs, srca, start=True, stop=True),
                    waits=[srct] + war)
                pa, pw, pi = padr.next()
                ev = em.emit("VE", lambda v, pa=pa, bnk=bnk: v.tensor_copy(
                    pa, bnk.ap()[0:128, 0:128]), waits=[mm] + pw)
                bnk.rd(ev)
                pads[(namej, l, tile)] = (pa, ev, pi)
            if upto < 2:
                continue
            gram_jobs = []
            for leaf in range(16):
                tile, slot = leaf // 3, 32 * (leaf % 3)
                gram_jobs.append((pads[("Lv", 0, tile)], slot, pads[("Lq", 0, tile)], slot, 32,
                                  kta, ltkt, 4 * leaf, 4))
            for l in range(4):
                for n in range(NN[l]):
                    tile, slot = n // 3, 32 * (n % 3)
                    t0c = n * 2 * BS[l]
                    b0c = t0c + BS[l]
                    ku = aux[(0, l)]
                    vu = aux[(3, l)]
                    gram_jobs.append((pads[("Vr", l, tile)], slot, pads[("Qr", l, tile)], slot, 32,
                                      ku[0], ku[1], t0c, BS[l]))
                    gram_jobs.append((pads[("Kl", l, tile)], slot, pads[("Ql", l, tile)], slot, 32,
                                      vu[0], vu[1], b0c, BS[l]))
            warY = bY.wr()
            warY = bY.wr()
            mmZf = em.emit("TE", lambda t, qa=qa: t.matmul(bY.ap()[0:128, 0:72], qa, zer.ap(),
                                                           start=True, stop=False),
                           waits=[ltq, zer_tok] + warY)
            warY = [mmZf]
            for gj, (pA, offA, pB, offB, kk, sec, sectok, ocol, ow) in enumerate(gram_jobs):
                gbnk = bG0 if gj % 2 == 0 else bG1
                war = gbnk.wr()
                mm = em.emit("TE", lambda t, pA=pA, offA=offA, pB=pB, offB=offB, kk=kk, gbnk=gbnk: t.matmul(
                    gbnk.ap()[0:128, 0:128],
                    pA[0][offA:offA + kk, :], pB[0][offB:offB + kk, :],
                    start=True, stop=True), waits=[pA[1], pB[1]] + war)
                padr.used(pA[2], mm)
                padr.used(pB[2], mm)
                ga, gw, gi = gramr.next()
                evm = em.emit("VE", lambda v, ga=ga, gbnk=gbnk: v.tensor_tensor(
                    ga, gbnk.ap()[0:128, 0:128], triu["bf"], ALU.mult), waits=[mm] + gw)
                gbnk.rd(evm)
                mm2 = em.emit("TE", lambda t, ga=ga, sec=sec, ocol=ocol, ow=ow: t.matmul(
                    bY.ap()[0:128, ocol:ocol + ow], ga, sec[:, ocol:ocol + ow],
                    start=False, stop=False), waits=[evm, sectok] + (warY if gj == 0 else []))
                gramr.used(gi, mm2)
            if upto < 3:
                continue
            war = bG0.wr()
            mmD = em.emit("TE", lambda t, ka=ka, qa=qa: t.matmul(bG0.ap()[0:128, 0:128], ka, qa,
                                                   start=True, stop=True), waits=[ltk, ltq] + war)
            ga, gw, gi = gramr.next()
            evD = em.emit("VE", lambda v, ga=ga: v.tensor_tensor(
                ga, bG0.ap()[0:128, 0:128], triu["bf"], ALU.mult), waits=[mmD] + gw)
            bG0.rd(evD)
            mmDen = em.emit("TE", lambda t, ga=ga: t.matmul(bY.ap()[0:128, 64:65], ga, ones["bf"],
                                                            start=False, stop=False),
                            waits=[evD, ones["btok"]])
            gramr.used(gi, mmDen)
            cpP = em.emit("VE", lambda v: v.tensor_copy(P_b.ap(), P_f.ap()),
                          waits=getattr(build_L0B, "_pbr", []))
            cpz = em.emit("VE", lambda v: v.tensor_copy(z_b.ap(), z_f.ap()),
                          waits=getattr(build_L0B, "_zbr", []))
            mmP = em.emit("TE", lambda t, qa=qa: t.matmul(bY.ap()[0:128, 0:64], qa, P_b.ap(),
                                                   start=False, stop=False), waits=[ltq, cpP])
            mmz = em.emit("TE", lambda t, qa=qa: t.matmul(bY.ap()[0:128, 64:65], qa, z_b.ap(),
                                                   start=False, stop=True), waits=[cpz, ltq])
            build_L0B._pbr = [mmP]
            build_L0B._zbr = [mmz]
            if upto < 4:
                continue
            war1 = bS1.wr()
            war2 = bS2.wr()
            sjobs = [(0, vta, ltvt, kta, ltkt)]
            for l in range(4):
                vv = aux[(1, l)]
                ku0 = aux[(0, l)]
                kv = aux[(2, l)]
                vu0 = aux[(3, l)]
                sjobs.append((1 + l, vv[0], vv[1], ku0[0], ku0[1]))
                sjobs.append((5 + l, kv[0], kv[1], vu0[0], vu0[1]))
            smms = []
            for sk, lhsT_t, ltok, rhs_t, rtok in sjobs:
                bnk = bS1 if sk < 5 else bS2
                soff = 64 * (sk % 5)
                mm = em.emit("TE", lambda t, lhsT_t=lhsT_t, rhs_t=rhs_t, bnk=bnk, soff=soff: t.matmul(
                    bnk.ap()[0:64, soff:soff + 64], lhsT_t, rhs_t, start=True, stop=True),
                    waits=[ltok, rtok] + (war1 if sk == 0 else war2 if sk == 5 else []))
                smms.append((sk, mm, bnk, soff))
                auxr_used = None
            mmzs = em.emit("TE", lambda t, kta=kta: t.matmul(bS1.ap()[0:64, 448:449], kta, ones["bf"],
                                                    start=True, stop=True), waits=[ltkt, ones["btok"]])
            sadds = []
            for sk, mm, bnk, soff in smms:
                sa, sw, si = stm.next()
                t1 = em.emit("VE", lambda v, sa=sa, bnk=bnk, soff=soff, sk=sk: v.tensor_tensor(
                    sa, bnk.ap()[0:64, soff:soff + 64], masks["f32"][:, 64 * sk:64 * (sk + 1)], ALU.mult),
                    waits=[smms[-1][1], cpP] + sw)
                bnk.rd(t1)
                t2 = em.emit("VE", lambda v, sa=sa: v.tensor_add(P_f.ap(), P_f.ap(), sa),
                             waits=[t1] + sadds[-1:])
                stm.used(si, t2)
                sadds.append(t2)
            tz1 = em.emit("VE", lambda v: v.tensor_add(z_f.ap(), z_f.ap(), bS1.ap()[0:64, 448:449]),
                          waits=[mmzs, cpz])
            bS1.rd(tz1)
            if upto < 5:
                continue
            d1a, d1w, d1i = den1.next()
            td1 = em.emit("VE", lambda v, d1a=d1a: v.tensor_scalar_max(d1a, bY.ap()[0:128, 64:65], 1e-6),
                          waits=[mmz, mmDen] + d1w)
            d2a, d2w, d2i = den2.next()
            td2 = em.emit("VE", lambda v, d1a=d1a, d2a=d2a: v.reciprocal(d2a, d1a), waits=[td1] + d2w)
            den1.used(d1i, td2)
            oa, ow_, oi = otm.next()
            to1 = em.emit("VE", lambda v, oa=oa, d2a=d2a: v.tensor_scalar(
                oa, bY.ap()[0:128, 0:64], d2a[:, 0:1], None, ALU.mult), waits=[td2, mmP] + ow_)
            bY.rd(to1)
            bY.rd(td1)
            den2.used(d2i, to1)
            wtr = ptb.wr()
            ttr = em.emit("TE", lambda t, oa=oa: t.transpose(ptb.ap()[0:64, 0:128], oa, ident["bf"]),
                          waits=[to1] + wtr)
            otm.used(oi, ttr)
            tse = em.emit("VE", lambda v, gt=gt: v.tensor_copy(sfm.ap()[:, gt:gt + 128],
                                                               ptb.ap()[0:64, 0:128]), waits=[ttr])
            ptb.rd(tse)
            qr.used(qi, mmP)
            qr.used(qi, mmz)
            qr.used(qi, mmD)
            kr.used(ki, mmD)
            vr.used(vi, mm2)
            ktr.used(kti, mmzs)
            vtr.used(vti, smms[0][1])
            for kind in range(4):
                for l in range(4):
                    auxr.used(aux[(kind, l)][2], smms[-1][1])
        em.barrier()

    with ExitStack() as ph:
        wo_b = sbuf(nc, ph, "wo_b", [64, 1024], BF16)
        ldw = em.dma("GE", wo_b.ap(), wo_d)
        xr = Ring(nc, ph, 3, (128, 512), F32, "cx")
        gr = Ring(nc, ph, 3, (128, 512), BF16, "cg")
        orr = Ring(nc, ph, 3, (128, 512), F32, "co")
        bO0, bO1 = banks[0], banks[1]
        for it in range(NT):
            sl = slice(512 * it, 512 * (it + 1))
            for dc in range(8):
                bo = bO0 if dc % 2 == 0 else bO1
                war = bo.wr()
                mm = em.emit("TE", lambda t, dc=dc, sl=sl, bo=bo: t.matmul(
                    bo.ap()[0:128, 0:512], wo_b.ap()[:, 128 * dc:128 * (dc + 1)], sfm.ap()[:, sl],
                    start=True, stop=True), waits=[ldw] + war)
                xa, xw, xi = xr.next()
                ltx = em.dma("SY", xa, xfm[128 * dc:128 * (dc + 1), sl], waits=xw)
                ga, gw, gi = gr.next()
                ltg = em.dma("SY", ga, g0_d[128 * dc:128 * (dc + 1), sl], waits=gw)
                oa, ow_, oi = orr.next()
                t1 = em.emit("VE", lambda v, oa=oa, ga=ga, bo=bo: v.tensor_tensor(
                    oa, bo.ap()[0:128, 0:512], ga, ALU.mult), waits=[mm, ltg] + ow_)
                bo.rd(t1)
                gr.used(gi, t1)
                t2 = em.emit("VE", lambda v, oa=oa, xa=xa: v.tensor_add(oa, oa, xa), waits=[t1, ltx])
                xr.used(xi, t2)
                st = em.dma("SY", x1_o[128 * dc:128 * (dc + 1), sl], oa, waits=[t2])
                orr.used(oi, st)
        em.barrier()
    if standalone:
        em.replay()
        stack.close()
        return nc
    return None




def build_L1A(io=None, ctx=None):
    nc, stack, em, banks, ptb, standalone = _ctx(ctx)

    x1_d = _io_in(io, nc, "x1", [1024, OWN])
    wq_d = _io_in(io, nc, "l1wq", [1024, 1024])
    wk_d = _io_in(io, nc, "l1wk", [1024, 1024])
    wv_d = _io_in(io, nc, "l1wv", [1024, 1024])
    sc1_d = _io_in(io, nc, "l1sc1", [1024, 1])
    b1_d = _io_in(io, nc, "l1b1", [1024, 1])
    pq_o = _io_out(io, nc, "pq", [1024, OWN], BF16)
    pk_o = _io_out(io, nc, "pk", [1024, OWN], BF16)
    pkt_o = _io_out(io, nc, "pkt", [OWN, 1024], BF16)
    vaug_o = _io_out(io, nc, "vaug", [OWN, 1040], BF16)
    pdel_o = _io_out(io, nc, "pdel", [NCH, 16, 64, 65], F32)
    xn1_o = _io_out(io, nc, "xn1", [1024, OWN], F32)

    ones = _lc(io, em, nc, stack, "c_ones", [128, 1], want_bf=True)
    onesrow = _lc(io, em, nc, stack, "c_onesrow", [1, 128])
    ident = _lc(io, em, nc, stack, "c_ident", [128, 128], want_bf=True)
    triu = _lc(io, em, nc, stack, "c_triu", [128, 128], want_bf=True)
    e2t_c = _lc(io, em, nc, stack, "c_e2t", [2, 128], want_bf=True)
    eps = sbuf(nc, stack, "epsc", [128, 4], F32)
    memset(em, "VE", eps.ap()[:, 0:1], 1e-5)
    memset(em, "VE", eps.ap()[:, 1:2], 1e-6)
    stats = {k: sbuf(nc, stack, "st_" + k, [1, OWN], F32).ap()
             for k in ("m", "rstd", "s2", "rstd2", "var")}
    stats_scr = (io["stats_scr"] if io and "stats_scr" in io
                 else dram_scr(nc, "stats_scr", [8, OWN], F32))
    sc1c = sbuf(nc, stack, "sc1c", [128, 8], F32)
    b1c = sbuf(nc, stack, "b1c", [128, 8], F32)
    ctoks = []
    for cch in range(8):
        ctoks.append(em.dma("SY", sc1c.ap()[:, cch:cch + 1], sc1_d[128 * cch:128 * (cch + 1), :]))
        ctoks.append(em.dma("SY", b1c.ap()[:, cch:cch + 1], b1_d[128 * cch:128 * (cch + 1), :]))
    em.barrier()
    stats["e2t"] = e2t_c
    stats_phase(em, nc, stack, banks, x1_d, OWN, stats, ones["bf"], eps.ap(), want_extra=True,
                scr_dram=stats_scr)

    t_scr = dram_scr(nc, "t_scr", [1024, OWN], BF16)
    with ExitStack() as ph:
        mB = sbuf(nc, ph, "tmB", [128, OWN], F32)
        rB = sbuf(nc, ph, "trB", [128, OWN], F32)
        r2B = sbuf(nc, ph, "tr2B", [128, OWN], F32)
        b1_ = bcast_dma(em, mB.ap(), stats_scr[0:1, :], n=OWN)
        b2_ = bcast_dma(em, rB.ap(), stats_scr[1:2, :], n=OWN)
        b3_ = bcast_dma(em, r2B.ap(), stats_scr[3:4, :], n=OWN)
        xr = Ring(nc, ph, 3, (128, 512), F32, "tx")
        fr = Ring(nc, ph, 3, (128, 512), F32, "tf")
        f2r = Ring(nc, ph, 3, (128, 512), F32, "tf2")
        tbr = Ring(nc, ph, 3, (128, 512), BF16, "ttb")
        for it in range(NT):
            sl = slice(512 * it, 512 * (it + 1))
            for cch in range(8):
                xa, xw, xi = xr.next()
                lt = em.dma("SY", xa, x1_d[128 * cch:128 * (cch + 1), sl], waits=xw)
                fa, fw, fi = fr.next()
                t1 = em.emit("VE", lambda v, fa=fa, xa=xa, sl=sl: v.tensor_sub(fa, xa, mB.ap()[:, sl]),
                             waits=[lt, b1_] + fw)
                xr.used(xi, t1)
                f2a, f2w, f2i = f2r.next()
                t2 = em.emit("VE", lambda v, f2a=f2a, fa=fa, sl=sl: v.tensor_mul(f2a, fa, rB.ap()[:, sl]),
                             waits=[t1, b2_] + f2w)
                fr.used(fi, t2)
                st1 = em.dma("SY", xn1_o[128 * cch:128 * (cch + 1), sl], f2a, waits=[t2])
                t3 = em.emit("VE", lambda v, fa=fa, f2a=f2a, sl=sl: v.tensor_mul(fa, f2a, r2B.ap()[:, sl]),
                             waits=[t2, b3_])
                f2r.used(f2i, t3)
                f2r.used(f2i, st1)
                ta, tw, ti = tbr.next()
                t4 = em.emit("SE", lambda s, ta=ta, fa=fa, cch=cch: s.activation(
                    out=ta, in_=fa, func=AF.Identity, scale=sc1c.ap()[:, cch:cch + 1],
                    bias=b1c.ap()[:, cch:cch + 1]), waits=[t3] + ctoks + tw)
                fr.used(fi, t4)
                st2 = em.dma("SY", t_scr[128 * cch:128 * (cch + 1), sl], ta, waits=[t4])
                tbr.used(ti, st2)
        em.barrier()

    with ExitStack() as ph:
        wqb = sbuf(nc, ph, "wqb", [128, 8 * 1024], BF16)
        wkb = sbuf(nc, ph, "wkb", [128, 8 * 1024], BF16)
        wvb = sbuf(nc, ph, "wvb", [128, 8 * 1024], BF16)
        wl = []
        for cch in range(8):
            wl.append(em.dma("GE", wqb.ap()[:, 1024 * cch:1024 * (cch + 1)], wq_d[128 * cch:128 * (cch + 1), :]))
            wl.append(em.dma("GE", wkb.ap()[:, 1024 * cch:1024 * (cch + 1)], wk_d[128 * cch:128 * (cch + 1), :]))
            wl.append(em.dma("GE", wvb.ap()[:, 1024 * cch:1024 * (cch + 1)], wv_d[128 * cch:128 * (cch + 1), :]))
        tr = Ring(nc, ph, 10, (128, 512), BF16, "qt")
        er = Ring(nc, ph, 4, (128, 512), BF16, "qe")
        vtm = Ring(nc, ph, 3, (128, 1040), BF16, "qv")
        ktm = Ring(nc, ph, 3, (128, 1024), BF16, "qk2")
        b0, b1b, b2b, b3b = banks[0], banks[1], banks[2], banks[3]
        for it in range(NT):
            sl = slice(512 * it, 512 * (it + 1))
            t_list = []
            for cch in range(8):
                ta, tw, ti = tr.next()
                lt = em.dma("SY", ta, t_scr[128 * cch:128 * (cch + 1), sl], waits=tw)
                t_list.append((ta, lt, ti))
            for which, wb, out_d in (("q", wqb, pq_o), ("k", wkb, pk_o)):
                for hp in range(8):
                    bk = b0 if hp % 2 == 0 else b1b
                    war = bk.wr()
                    mml = None
                    for cch in range(8):
                        ta, lt, ti = t_list[cch]
                        mml = em.emit("TE", lambda t, wb=wb, hp=hp, cch=cch, ta=ta, bk=bk: t.matmul(
                            bk.ap()[0:128, 0:512], wb.ap()[:, 1024 * cch + 128 * hp:1024 * cch + 128 * (hp + 1)],
                            ta, start=(cch == 0), stop=(cch == 7)),
                            waits=[lt] + wl + (war if cch == 0 else []))
                        tr.used(ti, mml)
                    ea, ew, ei = er.next()
                    phi = em.emit("VE", lambda v, ea=ea, bk=bk: v.tensor_scalar(
                        ea, bk.ap()[0:128, 0:512], 0.0, 1e-3, ALU.max, ALU.add), waits=[mml] + ew)
                    bk.rd(phi)
                    st = em.dma("SY", out_d[128 * hp:128 * (hp + 1), sl], ea, waits=[phi])
                    er.used(ei, st)
            for ck in range(4):
                tglob = 512 * it + 128 * ck
                for half in range(2):
                    bk = b2b if half == 0 else b3b
                    war = bk.wr()
                    mml = None
                    for cch in range(8):
                        ta, lt, ti = t_list[cch]
                        mml = em.emit("TE", lambda t, ta=ta, ck=ck, half=half, cch=cch, bk=bk: t.matmul(
                            bk.ap()[0:128, 0:512], ta[:, 128 * ck:128 * (ck + 1)],
                            wvb.ap()[:, 1024 * cch + 512 * half:1024 * cch + 512 * (half + 1)],
                            start=(cch == 0), stop=(cch == 7)),
                            waits=[lt] + (war if cch == 0 else []))
                        tr.used(ti, mml)
                    va, vw, vi = vtm.next() if half == 0 else (None, None, None)
                    if half == 0:
                        build_L1A._vslot = (va, vw, vi)
                    va, vw, vi = build_L1A._vslot
                    cps = []
                    for hh in range(8):
                        h = half * 8 + hh
                        cp = em.emit("VE", lambda v, va=va, h=h, hh=hh, bk=bk: v.tensor_copy(
                            va[:, 65 * h:65 * h + 64], bk.ap()[0:128, 64 * hh:64 * (hh + 1)]),
                            waits=[mml] + (vw if (half == 0 and hh == 0) else []))
                        bk.rd(cp)
                        cps.append(cp)
                    if half == 1:
                        msv = em.emit("VE", lambda v, va=va: v.memset(
                            va.rearrange("p (h c) -> p h c", c=65)[:, :, 64:65], 1.0), waits=cps)
                        stv = em.dma("SY", vaug_o[tglob:tglob + 128, :], va, waits=[msv])
                        vtm.used(vi, stv)
                for half in range(2):
                    bk = b2b if half == 0 else b3b
                    war = bk.wr()
                    mml = None
                    for cch in range(8):
                        ta, lt, ti = t_list[cch]
                        mml = em.emit("TE", lambda t, ta=ta, ck=ck, half=half, cch=cch, bk=bk: t.matmul(
                            bk.ap()[0:128, 0:512], ta[:, 128 * ck:128 * (ck + 1)],
                            wkb.ap()[:, 1024 * cch + 512 * half:1024 * cch + 512 * (half + 1)],
                            start=(cch == 0), stop=(cch == 7)),
                            waits=[lt] + (war if cch == 0 else []))
                        tr.used(ti, mml)
                    if half == 0:
                        build_L1A._kslot = ktm.next()
                    ka2, kw2, ki2 = build_L1A._kslot
                    phi = em.emit("VE", lambda v, ka2=ka2, half=half, bk=bk: v.tensor_scalar(
                        ka2[:, 512 * half:512 * (half + 1)], bk.ap()[0:128, 0:512],
                        0.0, 1e-3, ALU.max, ALU.add),
                        waits=[mml] + (kw2 if half == 0 else []))
                    bk.rd(phi)
                    if half == 1:
                        stk = em.dma("SY", pkt_o[tglob:tglob + 128, :], ka2, waits=[phi])
                        ktm.used(ki2, stk)
        em.barrier()

    with ExitStack() as ph:
        kr = Ring(nc, ph, 2, (128, 1024), BF16, "dk")
        vr = Ring(nc, ph, 2, (128, 1040), BF16, "dv")
        dr = Ring(nc, ph, 6, (64, 65), F32, "dd")
        bD0, bD1 = banks[0], banks[1]
        for ch in range(NCH):
            gt = 128 * ch
            ka2, kw2, ki2 = kr.next()
            ltk = em.dma("SY", ka2, pkt_o[gt:gt + 128, :], waits=kw2)
            va, vw, vi = vr.next()
            ltv = em.dma("SY", va, vaug_o[gt:gt + 128, :], waits=vw)
            for h in range(16):
                bk = bD0 if h % 2 == 0 else bD1
                war = bk.wr()
                mm = em.emit("TE", lambda t, ka2=ka2, va=va, h=h, bk=bk: t.matmul(
                    bk.ap()[0:64, 0:65], ka2[:, 64 * h:64 * (h + 1)], va[:, 65 * h:65 * (h + 1)],
                    start=True, stop=True), waits=[ltk, ltv] + war)
                da, dw, di = dr.next()
                ev = em.emit("VE", lambda v, da=da, bk=bk: v.tensor_copy(da, bk.ap()[0:64, 0:65]),
                             waits=[mm] + dw)
                bk.rd(ev)
                st = em.dma("SY", pdel_o[ch, h], da, waits=[ev])
                dr.used(di, st)
                kr.used(ki2, mm)
                vr.used(vi, mm)
        em.barrier()
    if standalone:
        em.replay()
        stack.close()
        return nc
    return None




def build_L1B(io=None, ctx=None):
    nc, stack, em, banks, ptb, standalone = _ctx(ctx)

    xn1_d = _io_in(io, nc, "xn1", [1024, OWN])
    pq_d = _io_in(io, nc, "pq", [1024, OWN], BF16)
    pk_d = _io_in(io, nc, "pk", [1024, OWN], BF16)
    pkt_d = _io_in(io, nc, "pkt", [OWN, 1024], BF16)
    vaug_d = _io_in(io, nc, "vaug", [OWN, 1040], BF16)
    pdel_d = _io_in(io, nc, "pdelp", [NCH, 16, 64, 65], F32)
    wo_d = _io_in(io, nc, "l1wo", [1024, 1024])
    sc2_d = _io_in(io, nc, "l1sc2", [1024, 1])
    b2_d = _io_in(io, nc, "l1b2", [1024, 1])
    w1_d = _io_in(io, nc, "l1w1", [1024, 4096])
    b1v_d = _io_in(io, nc, "l1b1v", [4096, 1])
    w2_d = _io_in(io, nc, "l1w2", [4096, 1024])
    b2v_d = _io_in(io, nc, "l1b2v", [1024, 1])
    s1_o = _io_out(io, nc, "s1", [1024, OWN], F32)

    ones = _lc(io, em, nc, stack, "c_ones", [128, 1], want_bf=True)
    onesrow = _lc(io, em, nc, stack, "c_onesrow", [1, 128])
    ident = _lc(io, em, nc, stack, "c_ident", [128, 128], want_bf=True)
    triu = _lc(io, em, nc, stack, "c_triu", [128, 128], want_bf=True)
    e2t_c = _lc(io, em, nc, stack, "c_e2t", [2, 128], want_bf=True)
    eps = sbuf(nc, stack, "epsc", [128, 4], F32)
    memset(em, "VE", eps.ap()[:, 0:1], 1e-6)
    stats = {k: sbuf(nc, stack, "st_" + k, [1, OWN], F32).ap()
             for k in ("m", "rstd", "s2", "rstd2", "var")}
    stats_scr = (io["stats_scr"] if io and "stats_scr" in io
                 else dram_scr(nc, "stats_scr", [8, OWN], F32))
    sc2c = sbuf(nc, stack, "sc2c", [128, 8], F32)
    b2c = sbuf(nc, stack, "b2c", [128, 8], F32)
    b2vc = sbuf(nc, stack, "b2vc", [128, 8], F32)
    b1vc = sbuf(nc, stack, "b1vc", [128, 32], F32)
    ctoks = []
    for cch in range(8):
        ctoks.append(em.dma("SY", sc2c.ap()[:, cch:cch + 1], sc2_d[128 * cch:128 * (cch + 1), :]))
        ctoks.append(em.dma("SY", b2c.ap()[:, cch:cch + 1], b2_d[128 * cch:128 * (cch + 1), :]))
        ctoks.append(em.dma("SY", b2vc.ap()[:, cch:cch + 1], b2v_d[128 * cch:128 * (cch + 1), :]))
    for fc in range(32):
        ctoks.append(em.dma("SY", b1vc.ap()[:, fc:fc + 1], b1v_d[128 * fc:128 * (fc + 1), :]))
    Z_f = sbuf(nc, stack, "Z_f", [64, 16 * 65], F32)
    Z_b = sbuf(nc, stack, "Z_b", [128, 8 * 65], BF16)
    em.barrier()
    with ExitStack() as ph:
        dld = Ring(nc, ph, 6, (64, 1040), F32, "zld")
        p8 = io.get("pdelp8") if io else None
        if p8 is not None:
            ap8, g8a = p8
            acc = memset(em, "VE", Z_f.ap(), 0.0)
            for j in range(8):
                for ch in range(NCH):
                    ap3 = ap8[ch // 2, j].rearrange("c h r k -> c r h k")
                    aa, ww, ii = dld.next()
                    lt = em.dma("SY", aa.rearrange("r (h k) -> r h k", h=16), ap3[ch % 2],
                                waits=ww)
                    t = em.emit("VE", lambda v, aa=aa, j=j, g8a=g8a: v.tensor_scalar(
                        aa, aa, g8a[0:64, j:j + 1], None, ALU.mult), waits=[lt])
                    acc = em.emit("VE", lambda v, aa=aa: v.tensor_add(Z_f.ap(), Z_f.ap(), aa),
                                  waits=[t, acc])
                    dld.used(ii, acc)
        else:
            ap3 = pdel_d.rearrange("c h r k -> c r h k")
            a0, w0, i0 = dld.next()
            t0 = em.dma("SY", Z_f.ap().rearrange("r (h k) -> r h k", h=16), ap3[0], waits=w0)
            acc = t0
            for ch in range(1, NCH):
                aa, ww, ii = dld.next()
                lt = em.dma("SY", aa.rearrange("r (h k) -> r h k", h=16), ap3[ch], waits=ww)
                acc = em.emit("VE", lambda v, aa=aa: v.tensor_add(Z_f.ap(), Z_f.ap(), aa), waits=[lt, acc])
                dld.used(ii, acc)
        em.barrier()

    attnfm = sbuf(nc, stack, "attnfm", [128, 8 * OWN], BF16)
    with ExitStack() as ph:
        qr = Ring(nc, ph, 2, (128, 8 * 128), BF16, "pq8")
        kr = Ring(nc, ph, 2, (128, 8 * 128), BF16, "pk8")
        ktr = Ring(nc, ph, 2, (128, 1024), BF16, "pkt2")
        vr = Ring(nc, ph, 2, (128, 1040), BF16, "pva")
        gramr = Ring(nc, ph, 6, (128, 128), BF16, "pgr")
        denr = Ring(nc, ph, 4, (128, 1), F32, "pde")
        atr = Ring(nc, ph, 4, (128, 64), BF16, "pat")
        bG0, bG1, bO0, bO1, bS0, bS1 = (banks[0], banks[1], banks[2], banks[3], banks[4], banks[5])
        zb_readers = [[] for _ in range(16)]
        for ch in range(NCH):
            gt = 128 * ch
            qa, qw, qi = qr.next()
            ltq = None
            for pc in range(8):
                ltq = em.dma("SY", qa[:, 128 * pc:128 * (pc + 1)],
                             pq_d[128 * pc:128 * (pc + 1), gt:gt + 128], waits=(qw if pc == 0 else []))
            ka, kw, ki = kr.next()
            ltk = None
            for pc in range(8):
                ltk = em.dma("SY", ka[:, 128 * pc:128 * (pc + 1)],
                             pk_d[128 * pc:128 * (pc + 1), gt:gt + 128], waits=(kw if pc == 0 else []))
            kta, ktw, kti = ktr.next()
            ltkt = em.dma("SY", kta, pkt_d[gt:gt + 128, :], waits=ktw)
            va, vw, vi = vr.next()
            ltv = em.dma("SY", va, vaug_d[gt:gt + 128, :], waits=vw)
            stH = {}
            ptb_w = []

            def stage_a(h):
                hq = qa[:, 128 * (h // 2):128 * (h // 2) + 128][64 * (h % 2):64 * (h % 2) + 64, :]
                hk = ka[:, 128 * (h // 2):128 * (h // 2) + 128][64 * (h % 2):64 * (h % 2) + 64, :]
                gb = bG0 if h % 2 == 0 else bG1
                war = gb.wr()
                mmG = em.emit("TE", lambda t, hk=hk, hq=hq, gb=gb: t.matmul(
                    gb.ap()[0:128, 0:128], hk, hq, start=True, stop=True),
                    waits=[ltq, ltk] + war)
                ga, gw, gi = gramr.next()
                evm = em.emit("VE", lambda v, ga=ga, gb=gb: v.tensor_tensor(
                    ga, gb.ap()[0:128, 0:128], triu["bf"], ALU.mult), waits=[mmG] + gw)
                gb.rd(evm)
                qr.used(qi, mmG)
                kr.used(ki, mmG)
                stH[h] = dict(hq=hq, ga=ga, gi=gi, evm=evm)

            def stage_b(h):
                st = stH[h]
                ob = bO0 if h % 2 == 0 else bO1
                warO = ob.wr()
                mmI = em.emit("TE", lambda t, ga=st["ga"], va=va, h=h, ob=ob: t.matmul(
                    ob.ap()[0:128, 0:65], ga, va[:, 65 * h:65 * (h + 1)],
                    start=True, stop=False), waits=[st["evm"], ltv] + warO)
                gramr.used(st["gi"], mmI)
                ro = 64 * (h % 2)
                co = 65 * (h // 2)
                cpZ = em.emit("VE", lambda v, h=h, ro=ro, co=co: v.tensor_copy(
                    Z_b.ap()[ro:ro + 64, co:co + 65], Z_f.ap()[:, 65 * h:65 * (h + 1)]),
                    waits=zb_readers[h])
                mmP = em.emit("TE", lambda t, hq=st["hq"], ro=ro, co=co, ob=ob: t.matmul(
                    ob.ap()[0:128, 0:65], hq, Z_b.ap()[ro:ro + 64, co:co + 65],
                    start=False, stop=True), waits=[cpZ, ltq])
                zb_readers[h] = [mmP]
                qr.used(qi, mmP)
                sb_ = bS0 if h % 2 == 0 else bS1
                warS = sb_.wr()
                mmS = em.emit("TE", lambda t, kta=kta, va=va, h=h, sb_=sb_: t.matmul(
                    sb_.ap()[0:64, 0:65], kta[:, 64 * h:64 * (h + 1)], va[:, 65 * h:65 * (h + 1)],
                    start=True, stop=True), waits=[ltkt, ltv] + warS)
                adZ = em.emit("VE", lambda v, h=h, sb_=sb_: v.tensor_add(
                    Z_f.ap()[:, 65 * h:65 * (h + 1)], Z_f.ap()[:, 65 * h:65 * (h + 1)],
                    sb_.ap()[0:64, 0:65]), waits=[mmS, cpZ])
                sb_.rd(adZ)
                ktr.used(kti, mmS)
                vr.used(vi, mmS)
                vr.used(vi, mmI)
                st["mmP"] = mmP

            def stage_c(h):
                nonlocal ptb_w
                st = stH.pop(h)
                ob = bO0 if h % 2 == 0 else bO1
                da, dw, di = denr.next()
                td = em.emit("VE", lambda v, da=da, ob=ob: v.reciprocal(da, ob.ap()[0:128, 64:65]),
                             waits=[st["mmP"]] + dw)
                ob.rd(td)
                aa2, aw2, ai2 = atr.next()
                tn = em.emit("VE", lambda v, aa2=aa2, da=da, ob=ob: v.tensor_scalar(
                    aa2, ob.ap()[0:128, 0:64], da[:, 0:1], None, ALU.mult), waits=[td] + aw2)
                ob.rd(tn)
                denr.used(di, tn)
                g8 = h % 8
                if g8 == 0:
                    ptb_w = ptb.wr()
                ttr = em.emit("TE", lambda t, aa2=aa2, g8=g8: t.transpose(
                    ptb.ap()[0:64, 128 * g8:128 * (g8 + 1)], aa2, ident["bf"]),
                    waits=[tn] + (ptb_w if g8 == 0 else []))
                atr.used(ai2, ttr)
                pc = h // 2
                rowoff = 64 * (h % 2)
                tse = em.emit("VE", lambda v, pc=pc, rowoff=rowoff, gt=gt, g8=g8: v.tensor_copy(
                    attnfm.ap()[rowoff:rowoff + 64, OWN * pc + gt:OWN * pc + gt + 128],
                    ptb.ap()[0:64, 128 * g8:128 * (g8 + 1)]), waits=[ttr])
                ptb.rd(tse)

            for h in range(16):
                stage_a(h)
                if h >= 1:
                    stage_b(h - 1)
                if h >= 2:
                    stage_c(h - 2)
            stage_b(15)
            stage_c(14)
            stage_c(15)
        em.barrier()


    xatt_scr = dram_scr(nc, "xatt_scr", [1024, OWN], F32)
    with ExitStack() as ph:
        wob = sbuf(nc, ph, "wob", [128, 8 * 1024], BF16)
        wl = [em.dma("GE", wob.ap()[:, 1024 * kc:1024 * (kc + 1)], wo_d[128 * kc:128 * (kc + 1), :])
              for kc in range(8)]
        xr = Ring(nc, ph, 3, (128, 512), F32, "wx")
        orr = Ring(nc, ph, 3, (128, 512), F32, "wo2")
        bA, bB = banks[0], banks[1]
        for it in range(NT):
            sl = slice(512 * it, 512 * (it + 1))
            for dc in range(8):
                bk = bA if dc % 2 == 0 else bB
                war = bk.wr()
                mml = None
                for kc in range(8):
                    mml = em.emit("TE", lambda t, kc=kc, dc=dc, it=it, bk=bk: t.matmul(
                        bk.ap()[0:128, 0:512], wob.ap()[:, 1024 * kc + 128 * dc:1024 * kc + 128 * (dc + 1)],
                        attnfm.ap()[:, OWN * kc + 512 * it:OWN * kc + 512 * (it + 1)],
                        start=(kc == 0), stop=(kc == 7)), waits=wl + (war if kc == 0 else []))
                xa, xw, xi = xr.next()
                ltx = em.dma("SY", xa, xn1_d[128 * dc:128 * (dc + 1), sl], waits=xw)
                oa, ow_, oi = orr.next()
                t1 = em.emit("VE", lambda v, oa=oa, xa=xa, bk=bk: v.tensor_add(
                    oa, bk.ap()[0:128, 0:512], xa), waits=[mml, ltx] + ow_)
                bk.rd(t1)
                xr.used(xi, t1)
                st = em.dma("SY", xatt_scr[128 * dc:128 * (dc + 1), sl], oa, waits=[t1])
                orr.used(oi, st)
        em.barrier()

    stats["e2t"] = e2t_c
    stats_phase(em, nc, stack, banks, xatt_scr, OWN, stats, ones["bf"], eps.ap(), want_extra=False,
                scr_dram=stats_scr)

    hm_scr = dram_scr(nc, "hm_scr", [4096, OWN], BF16)
    with ExitStack() as ph:
        w1b = sbuf(nc, ph, "w1b", [128, 8 * 4096], BF16)
        wl = [em.dma("GE", w1b.ap()[:, 4096 * cch:4096 * (cch + 1)], w1_d[128 * cch:128 * (cch + 1), :])
              for cch in range(8)]
        mB = sbuf(nc, ph, "m2B", [128, OWN], F32)
        rB = sbuf(nc, ph, "r2B2", [128, OWN], F32)
        bm = bcast_dma(em, mB.ap(), stats_scr[0:1, :], n=OWN)
        br = bcast_dma(em, rB.ap(), stats_scr[1:2, :], n=OWN)
        xr = Ring(nc, ph, 3, (128, 512), F32, "ux2")
        fr = Ring(nc, ph, 3, (128, 512), F32, "uf2")
        ybr = Ring(nc, ph, 10, (128, 512), BF16, "uy2")
        hr = Ring(nc, ph, 3, (128, 512), BF16, "uh2")
        bH0, bH1 = banks[2], banks[3]
        for it in range(NT):
            sl = slice(512 * it, 512 * (it + 1))
            y_list = []
            for cch in range(8):
                xa, xw, xi = xr.next()
                lt = em.dma("SY", xa, xatt_scr[128 * cch:128 * (cch + 1), sl], waits=xw)
                fa, fw, fi = fr.next()
                t1 = em.emit("VE", lambda v, fa=fa, xa=xa, sl=sl: v.tensor_sub(fa, xa, mB.ap()[:, sl]),
                             waits=[lt, bm] + fw)
                xr.used(xi, t1)
                t2 = em.emit("VE", lambda v, fa=fa, sl=sl: v.tensor_mul(fa, fa, rB.ap()[:, sl]),
                             waits=[t1, br])
                ya, yw, yi = ybr.next()
                t3 = em.emit("SE", lambda s, ya=ya, fa=fa, cch=cch: s.activation(
                    out=ya, in_=fa, func=AF.Identity, scale=sc2c.ap()[:, cch:cch + 1],
                    bias=b2c.ap()[:, cch:cch + 1]), waits=[t2] + ctoks + yw)
                fr.used(fi, t3)
                y_list.append((ya, t3, yi))
            last = None
            for fc in range(32):
                hb = bH0 if fc % 2 == 0 else bH1
                war = hb.wr()
                mml = None
                for cch in range(8):
                    ya, t3, yi = y_list[cch]
                    mml = em.emit("TE", lambda t, fc=fc, cch=cch, ya=ya, hb=hb: t.matmul(
                        hb.ap()[0:128, 0:512], w1b.ap()[:, 4096 * cch + 128 * fc:4096 * cch + 128 * (fc + 1)],
                        ya, start=(cch == 0), stop=(cch == 7)),
                        waits=[t3] + wl + (war if cch == 0 else []))
                ha, hw, hi = hr.next()
                ge = em.emit("SE", lambda s, ha=ha, hb=hb, fc=fc: s.activation(
                    out=ha, in_=hb.ap()[0:128, 0:512], func=AF.Gelu_apprx_tanh,
                    bias=b1vc.ap()[:, fc:fc + 1]), waits=[mml] + hw)
                hb.rd(ge)
                sto = em.dma("SY", hm_scr[128 * fc:128 * (fc + 1), sl], ha, waits=[ge])
                hr.used(hi, sto)
                last = sto
            for ya, t3, yi in y_list:
                ybr.used(yi, last)
        em.barrier()

    with ExitStack() as ph:
        w2b = sbuf(nc, ph, "w2b", [128, 32 * 1024], BF16)
        wl = [em.dma("GE", w2b.ap()[:, 1024 * fc:1024 * (fc + 1)], w2_d[128 * fc:128 * (fc + 1), :])
              for fc in range(32)]
        hrd = Ring(nc, ph, 34, (128, 512), BF16, "dh2")
        xr = Ring(nc, ph, 3, (128, 512), F32, "dx2")
        fgr = Ring(nc, ph, 3, (128, 512), F32, "df2")
        bA, bB = banks[2], banks[3]
        for it in range(NT):
            sl = slice(512 * it, 512 * (it + 1))
            hn = []
            for fc in range(32):
                ha, hw, hi = hrd.next()
                lt = em.dma("SY", ha, hm_scr[128 * fc:128 * (fc + 1), sl], waits=hw)
                hn.append((ha, lt, hi))
            last = None
            for dc in range(8):
                bk = bA if dc % 2 == 0 else bB
                war = bk.wr()
                mml = None
                for fc in range(32):
                    ha, lt, hi = hn[fc]
                    mml = em.emit("TE", lambda t, dc=dc, fc=fc, ha=ha, bk=bk: t.matmul(
                        bk.ap()[0:128, 0:512], w2b.ap()[:, 1024 * fc + 128 * dc:1024 * fc + 128 * (dc + 1)],
                        ha, start=(fc == 0), stop=(fc == 31)),
                        waits=[lt] + wl + (war if fc == 0 else []))
                fa, fw, fi = fgr.next()
                tb = em.emit("SE", lambda s, fa=fa, bk=bk, dc=dc: s.activation(
                    out=fa, in_=bk.ap()[0:128, 0:512], func=AF.Identity, bias=b2vc.ap()[:, dc:dc + 1]),
                    waits=[mml] + fw)
                bk.rd(tb)
                xa, xw, xi = xr.next()
                ltx = em.dma("SY", xa, xatt_scr[128 * dc:128 * (dc + 1), sl], waits=xw)
                t1 = em.emit("VE", lambda v, fa=fa, xa=xa: v.tensor_add(fa, fa, xa), waits=[tb, ltx])
                xr.used(xi, t1)
                st = em.dma("SY", s1_o[128 * dc:128 * (dc + 1), sl], fa, waits=[t1])
                fgr.used(fi, st)
                last = st
            for ha, lt, hi in hn:
                hrd.used(hi, last)
        em.barrier()
    if standalone:
        em.replay()
        stack.close()
        return nc
    return None




L2_EXT = 2816
L2_QKV = 2688
L2_LEAD = 768


def build_L2(final=False, io=None, ctx=None, upto_p=99):
    nc, stack, em, banks, ptb, standalone = _ctx(ctx)

    xe_d = _io_in(io, nc, "xe", [1024, L2_EXT])
    wqkv_d = _io_in(io, nc, "l2wqkv", [1024, 1536])
    wo_d = _io_in(io, nc, "l2wo", [512, 1024])
    w1_d = _io_in(io, nc, "l2w1", [1024, 5460])
    w2_d = _io_in(io, nc, "l2w2", [2730, 1024])
    b1_d = _io_in(io, nc, "l2b1", [5460, 1])
    b2_d = _io_in(io, nc, "l2b2", [1024, 1])
    lg_d = _io_in(io, nc, "l2lg", [1024, 1])
    lb_d = _io_in(io, nc, "l2lb", [1024, 1])
    qs_d = _io_in(io, nc, "l2qs", [128, 1])
    ks_d = _io_in(io, nc, "l2ks", [128, 1])
    rqcs_d = _io_in(io, nc, "rqcs", [128, L2_QKV])
    rqss_d = _io_in(io, nc, "rqss", [128, L2_QKV])
    rkcs_d = _io_in(io, nc, "rkcs", [128, L2_QKV])
    rkss_d = _io_in(io, nc, "rkss", [128, L2_QKV])
    lmask_d = _io_in(io, nc, "lmask", [17, 128, 640])
    if final:
        x1_d = _io_in(io, nc, "x1", [1024, OWN])
        s1_d = _io_in(io, nc, "s1", [1024, OWN])
        out_o = _io_out(io, nc, "outp", [1024, OWN], F32)
    else:
        xout_o = _io_out(io, nc, "xout", [1024, OWN], F32)

    if io and "l2scr" in io:
        xa_scr, h1_scr, prod_scr, l2st_scr = io["l2scr"]
    else:
        xa_scr = dram_scr(nc, "xa_scr", [1024, 128 + OWN], F32)
        h1_scr = dram_scr(nc, "h1_scr", [5460, OWN], BF16)
        prod_scr = dram_scr(nc, "prod_scr", [2730, OWN], BF16)
        l2st_scr = dram_scr(nc, "l2st_scr", [8, OWN], F32)

    ones = _lc(io, em, nc, stack, "c_ones", [128, 1], want_bf=True)
    ident = _lc(io, em, nc, stack, "c_ident", [128, 128], want_bf=True)
    e2 = _lc(io, em, nc, stack, "c_e2", [128, 2], want_bf=True)
    e2t = _lc(io, em, nc, stack, "c_e2t", [2, 128], want_bf=True)
    eps = sbuf(nc, stack, "epsc", [128, 4], F32)
    memset(em, "VE", eps.ap()[:, 0:1], 1e-5)
    qsc = sbuf(nc, stack, "qsc", [128, 2], F32)
    ct1 = em.dma("SY", qsc.ap()[:, 0:1], qs_d)
    ct2 = em.dma("SY", qsc.ap()[:, 1:2], ks_d)
    lgc = sbuf(nc, stack, "lgc", [128, 8], F32)
    lbc = sbuf(nc, stack, "lbc", [128, 8], F32)
    b2c = sbuf(nc, stack, "b2c", [128, 8], F32)
    b1a = sbuf(nc, stack, "b1a", [128, 22], F32)
    b1g = sbuf(nc, stack, "b1g", [128, 22], F32)
    ctoks = [ct1, ct2]
    for cch in range(8):
        ctoks.append(em.dma("SY", lgc.ap()[:, cch:cch + 1], lg_d[128 * cch:128 * (cch + 1), :]))
        ctoks.append(em.dma("SY", lbc.ap()[:, cch:cch + 1], lb_d[128 * cch:128 * (cch + 1), :]))
        ctoks.append(em.dma("SY", b2c.ap()[:, cch:cch + 1], b2_d[128 * cch:128 * (cch + 1), :]))
    for j in range(22):
        w = 42 if j == 21 else 128
        ctoks.append(em.dma("SY", b1a.ap()[0:w, j:j + 1], b1_d[128 * j:128 * j + w, :]))
        ctoks.append(em.dma("SY", b1g.ap()[0:w, j:j + 1],
                            b1_d[2730 + 128 * j:2730 + 128 * j + w, :]))
    stack25 = ExitStack()
    attnfm = sbuf(nc, stack25, "at2fm", [128, 4 * 17 * 128], BF16)
    em.barrier()
    NQC = L2_QKV // 128
    stack2 = ExitStack()
    rq_cs = sbuf(nc, stack2, "rq_cs", [128, L2_QKV], BF16)
    rq_ss = sbuf(nc, stack2, "rq_ss", [128, L2_QKV], BF16)
    rk_cs = sbuf(nc, stack2, "rk_cs", [128, L2_QKV], BF16)
    rk_ss = sbuf(nc, stack2, "rk_ss", [128, L2_QKV], BF16)
    rtoks = []
    for sb_t, dd in ((rq_cs, rqcs_d), (rq_ss, rqss_d), (rk_cs, rkcs_d), (rk_ss, rkss_d)):
        rtoks.append(em.dma("GE", sb_t.ap(), dd))
    qsw = sbuf(nc, stack2, "qsw", [128, 2], F32)
    swt = []
    for b0 in (0, 64):
        swt.append(em.emit("VE", lambda v, b0=b0: v.tensor_copy(
            qsw.ap()[b0:b0 + 32, :], qsc.ap()[b0 + 32:b0 + 64, :]), waits=[ct1, ct2]))
        swt.append(em.emit("VE", lambda v, b0=b0: v.tensor_copy(
            qsw.ap()[b0 + 32:b0 + 64, :], qsc.ap()[b0:b0 + 32, :]), waits=[ct1, ct2]))
    ttoks = []
    for tb, col, swp in ((rq_cs, 0, False), (rq_ss, 0, True), (rk_cs, 1, False), (rk_ss, 1, True)):
        sc_ap = (qsw if swp else qsc).ap()[:, col:col + 1]
        ttoks.append(em.emit("SE", lambda s, tb=tb, sc_ap=sc_ap: s.activation(
            out=tb.ap(), in_=tb.ap(), func=AF.Identity, scale=sc_ap),
            waits=rtoks + swt + [ct1, ct2]))
    qrot = sbuf(nc, stack2, "qrot", [128, 4 * L2_QKV], BF16)
    krot = sbuf(nc, stack2, "krot", [128, 4 * L2_QKV], BF16)
    vtm = sbuf(nc, stack2, "vtm", [128, NQC * 512], BF16)
    with ExitStack() as ph:
        wqb = sbuf(nc, ph, "wqb2", [128, 8 * 512], BF16)
        wkb = sbuf(nc, ph, "wkb2", [128, 8 * 512], BF16)
        wvb = sbuf(nc, ph, "wvb2", [128, 8 * 512], BF16)
        wl = []
        for cch in range(8):
            wl.append(em.dma("GE", wqb.ap()[:, 512 * cch:512 * (cch + 1)],
                             wqkv_d[128 * cch:128 * (cch + 1), 0:512]))
            wl.append(em.dma("GE", wkb.ap()[:, 512 * cch:512 * (cch + 1)],
                             wqkv_d[128 * cch:128 * (cch + 1), 512:1024]))
            wl.append(em.dma("GE", wvb.ap()[:, 512 * cch:512 * (cch + 1)],
                             wqkv_d[128 * cch:128 * (cch + 1), 1024:1536]))
        hx = Ring(nc, ph, 16, (128, 512), BF16, "hx")
        sqr = Ring(nc, ph, 3, (128, 512), BF16, "l2sq")
        nrb = Ring(nc, ph, 3, (2, 512), BF16, "l2nb")
        rnbr = Ring(nc, ph, 3, (128, 512), F32, "l2rb")
        t1r = Ring(nc, ph, 3, (128, 512), F32, "l2t1")
        swr = Ring(nc, ph, 3, (128, 512), F32, "l2sw")
        abr = Ring(nc, ph, 6, (128, 512), F32, "l2ab")
        vbr = Ring(nc, ph, 4, (128, 512), BF16, "l2vb")
        bQK = [banks[0], banks[1]]
        bNp = [banks[3], banks[5]]
        bBp = [banks[4], banks[6]]
        bV = banks[2]
        TILES = [(0, 512), (512, 512), (1024, 512), (1536, 512), (2048, 512), (2560, 128)]
        iters = []
        for t0, W in TILES:
            iters.append(("load", t0, W))
            for which in ("q", "k"):
                for hp in range(4):
                    iters.append((which, hp, t0, W))
        hlists = {}
        stA = {}

        def do_load(t0, W):
            h_list = []
            for cch in range(8):
                off = (128 + t0) if cch < 4 else (127 + t0)
                ha, hw, hi = hx.next()
                lt = em.dma("GE", ha[:, 0:W], xe_d[128 * cch:128 * (cch + 1), off:off + W],
                            waits=hw)
                h_list.append((ha, lt, hi))
            hlists[t0] = h_list
            for k in range(W // 128):
                ci = t0 // 128 + k
                war = bV.wr()
                mml = None
                for cch in range(8):
                    ha, lt, hi = h_list[cch]
                    mml = em.emit("TE", lambda t, ha=ha, cch=cch, k=k: t.matmul(
                        bV.ap()[0:128, 0:512], ha[:, 128 * k:128 * (k + 1)],
                        wvb.ap()[:, 512 * cch:512 * (cch + 1)],
                        start=(cch == 0), stop=(cch == 7)),
                        waits=[lt] + (wl + war if cch == 0 else []))
                    hx.used(hi, mml)
                va, vw, vi = vbr.next()
                ev = em.emit("VE", lambda v, va=va: v.tensor_copy(va, bV.ap()[0:128, 0:512]),
                             waits=[mml] + vw)
                bV.rd(ev)
                cv = em.emit("VE", lambda v, va=va, ci=ci: v.tensor_copy(
                    vtm.ap()[:, 512 * ci:512 * (ci + 1)], va), waits=[ev])
                vbr.used(vi, cv)

        def do_A(idx, which, hp, t0, W):
            bk = bQK[idx % 2]
            wb = wqb if which == "q" else wkb
            war = bk.wr()
            mml = None
            h_list = hlists[t0]
            for cch in range(8):
                ha, lt, hi = h_list[cch]
                mml = em.emit("TE", lambda t, ha=ha, wb=wb, hp=hp, cch=cch, W=W, bk=bk: t.matmul(
                    bk.ap()[0:128, 0:W],
                    wb.ap()[:, 512 * cch + 128 * hp:512 * cch + 128 * (hp + 1)],
                    ha[:, 0:W], start=(cch == 0), stop=(cch == 7)),
                    waits=[lt] + (wl + war if cch == 0 else []))
                hx.used(hi, mml)
            stA[idx] = mml

        def do_B(idx, which, hp, t0, W):
            bk = bQK[idx % 2]
            bN_ = bNp[idx % 2]
            bB_ = bBp[idx % 2]
            rot_cs = rq_cs if which == "q" else rk_cs
            rot_ss = rq_ss if which == "q" else rk_ss
            dstap = qrot if which == "q" else krot
            mml = stA[idx]
            sa, sw, si = sqr.next()
            sq = em.emit("SE", lambda s, sa=sa, W=W, bk=bk: s.activation(
                out=sa[:, 0:W], in_=bk.ap()[0:128, 0:W], func=AF.Square), waits=[mml] + sw)
            warN = bN_.wr()
            mmN = em.emit("TE", lambda t, sa=sa, W=W, bN_=bN_: t.matmul(
                bN_.ap()[0:2, 0:W], e2["bf"], sa[:, 0:W], start=True, stop=True),
                waits=[sq, e2["btok"]] + warN)
            sqr.used(si, mmN)
            nb, nbw, nbi = nrb.next()
            t3b = em.emit("SE", lambda s, nb=nb, W=W, bN_=bN_: s.activation(
                out=nb[:, 0:W], in_=bN_.ap()[0:2, 0:W], func=AF.Copy), waits=[mmN] + nbw)
            bN_.rd(t3b)
            warB = bB_.wr()
            mmB = em.emit("TE", lambda t, nb=nb, W=W, bB_=bB_: t.matmul(
                bB_.ap()[0:128, 0:W], e2t["bf"], nb[:, 0:W], start=True, stop=True),
                waits=[t3b, e2t["btok"]] + warB)
            nrb.used(nbi, mmB)
            ra, rw, ri = rnbr.next()
            sq2 = em.emit("SE", lambda s, ra=ra, W=W, bB_=bB_: s.activation(
                out=ra[:, 0:W], in_=bB_.ap()[0:128, 0:W], func=AF.Sqrt), waits=[mmB] + rw)
            bB_.rd(sq2)
            mx = em.emit("VE", lambda v, ra=ra, W=W: v.tensor_scalar_max(
                ra[:, 0:W], ra[:, 0:W], 1e-12), waits=[sq2])
            evB = em.emit("VE", lambda v, ra=ra, W=W: v.reciprocal(ra[:, 0:W], ra[:, 0:W]),
                          waits=[mx])
            ta1, tw1, ti1 = t1r.next()
            u1 = em.emit("VE", lambda v, ta1=ta1, ra=ra, W=W, bk=bk: v.tensor_tensor(
                ta1[:, 0:W], bk.ap()[0:128, 0:W], ra[:, 0:W], ALU.mult), waits=[evB] + tw1)
            bk.rd(u1)
            rnbr.used(ri, u1)
            swa, sww, swi = swr.next()
            cps = []
            for blk in range(2):
                b0 = 64 * blk
                cps.append(em.emit("SE", lambda s, swa=swa, ta1=ta1, b0=b0, W=W: s.activation(
                    out=swa[b0:b0 + 32, 0:W], in_=ta1[b0 + 32:b0 + 64, 0:W], func=AF.Copy),
                    waits=[u1] + (sww if blk == 0 else [])))
                cps.append(em.emit("SE", lambda s, swa=swa, ta1=ta1, b0=b0, W=W: s.activation(
                    out=swa[b0 + 32:b0 + 64, 0:W], in_=ta1[b0:b0 + 32, 0:W], func=AF.Copy),
                    waits=[u1]))
            aa_, aw_, ai_ = abr.next()
            qcol = slice(t0, t0 + W)
            u3 = em.emit("VE", lambda v, aa_=aa_, ta1=ta1, rot_cs=rot_cs, qcol=qcol, W=W: v.tensor_tensor(
                aa_[:, 0:W], ta1[:, 0:W], rot_cs.ap()[:, qcol], ALU.mult),
                waits=[u1] + ttoks + aw_)
            t1r.used(ti1, u3)
            ab_, abw_, abi_ = abr.next()
            u4 = em.emit("VE", lambda v, ab_=ab_, swa=swa, rot_ss=rot_ss, qcol=qcol, W=W: v.tensor_tensor(
                ab_[:, 0:W], swa[:, 0:W], rot_ss.ap()[:, qcol], ALU.mult),
                waits=cps + ttoks + abw_)
            swr.used(swi, u4)
            u5 = em.emit("VE", lambda v, aa_=aa_, ab_=ab_, dstap=dstap, hp=hp, qcol=qcol, W=W: v.tensor_tensor(
                dstap.ap()[:, L2_QKV * hp + qcol.start:L2_QKV * hp + qcol.start + W],
                aa_[:, 0:W], ab_[:, 0:W], ALU.add), waits=[u3, u4])
            abr.used(ai_, u5)
            abr.used(abi_, u5)

        prog = []
        ni_idx = 0
        for it in iters:
            if it[0] == "load":
                prog.append(("L", it))
            else:
                prog.append(("AB", ni_idx, it))
                ni_idx += 1
        pending = None
        for step in prog:
            if step[0] == "L":
                do_load(step[1][1], step[1][2])
            else:
                _, idx, it = step
                do_A(idx, *it)
                if pending is not None:
                    do_B(pending[0], *pending[1])
                pending = (idx, it)
        if pending is not None:
            do_B(pending[0], *pending[1])
        em.barrier()

    with ExitStack() as ph:
        scr = Ring(nc, ph, 3, (128, 640), F32, "l2sc")
        negr = Ring(nc, ph, 3, (128, 1), F32, "l2ng")
        pr = Ring(nc, ph, 4, (128, 640), BF16, "l2p")
        rsr = Ring(nc, ph, 3, (128, 1), F32, "l2rs")
        ptr2 = Ring(nc, ph, 3, (128, 640), BF16, "l2pt")
        mskr = Ring(nc, ph, 3, (128, 640), F32, "l2mk")
        bS = [banks[0], banks[1]]
        bT = [banks[2], banks[3]]
        bA2 = [banks[4], banks[5]]
        NIT = 17 * 8
        state = {}

        def S(i):
            qc, h = divmod(i, 8)
            Q0 = 512 + 128 * qc
            kv0 = Q0 - 512
            hp, ro = h // 2, 64 * (h % 2)
            if h == 0:
                ma, mw, mi = mskr.next()
                ltm = em.dma("SY", ma, lmask_d[qc], waits=mw)
                state[("mask", qc)] = (ma, mi, ltm)
            ms = state[("mask", qc)]
            sb_, tb_ = bS[i % 2], bT[i % 2]
            qsl = qrot.ap()[ro:ro + 64, L2_QKV * hp + Q0:L2_QKV * hp + Q0 + 128]
            war = sb_.wr()
            mm1 = em.emit("TE", lambda t, qsl=qsl, hp=hp, ro=ro, kv0=kv0, sb_=sb_: t.matmul(
                sb_.ap()[0:128, 0:512], qsl,
                krot.ap()[ro:ro + 64, L2_QKV * hp + kv0:L2_QKV * hp + kv0 + 512],
                start=True, stop=True), waits=war)
            warT = tb_.wr()
            mm2 = em.emit("TE", lambda t, qsl=qsl, hp=hp, ro=ro, kv0=kv0, tb_=tb_: t.matmul(
                tb_.ap()[0:128, 0:128], qsl,
                krot.ap()[ro:ro + 64, L2_QKV * hp + kv0 + 512:L2_QKV * hp + kv0 + 640],
                start=True, stop=True), waits=warT)
            state[i] = dict(mm1=mm1, mm2=mm2, sb=sb_, tb=tb_, qc=qc, h=h, hp=hp, ro=ro, kv0=kv0)

        def M(i):
            st = state[i]
            ma, mi, ltm = state[("mask", st["qc"])]
            sa, sw, si = scr.next()
            e1 = em.emit("VE", lambda v, sa=sa, ma=ma, sb_=st["sb"]: v.tensor_add(
                sa[:, 0:512], sb_.ap()[0:128, 0:512], ma[:, 0:512]),
                waits=[st["mm1"], ltm] + sw)
            st["sb"].rd(e1)
            e2_ = em.emit("VE", lambda v, sa=sa, ma=ma, tb_=st["tb"]: v.tensor_add(
                sa[:, 512:640], tb_.ap()[0:128, 0:128], ma[:, 512:640]),
                waits=[st["mm2"], ltm])
            st["tb"].rd(e2_)
            mskr.used(mi, e1)
            mskr.used(mi, e2_)
            na, nw, ni = negr.next()
            rm = em.emit("VE", lambda v, na=na, sa=sa: v.tensor_reduce(
                na, sa, AX.X, ALU.max, negate=True), waits=[e1, e2_] + nw)
            pa, pw, pi = pr.next()
            rs, rsw, rsi = rsr.next()
            ex = em.emit("SE", lambda s, pa=pa, sa=sa, na=na, rs=rs: s.activation(
                out=pa, in_=sa, func=AF.Exp, bias=na[:, 0:1], accum_out=rs),
                waits=[rm] + pw + rsw)
            negr.used(ni, ex)
            scr.used(si, ex)
            rc = em.emit("VE", lambda v, rs=rs: v.reciprocal(rs, rs), waits=[ex])
            pn = em.emit("VE", lambda v, pa=pa, rs=rs: v.tensor_scalar(
                pa, pa, rs[:, 0:1], None, ALU.mult), waits=[rc])
            rsr.used(rsi, pn)
            st["pa"] = pa
            st["pi"] = pi
            st["pn"] = pn

        def T(i):
            st = state[i]
            pa, pi, pn = st["pa"], st["pi"], st["pn"]
            h, hp, ro, kv0, qc = st["h"], st["hp"], st["ro"], st["kv0"], st["qc"]
            ab_ = bA2[i % 2]
            warA = ab_.wr()
            wtr = ptb.wr()
            tts = []
            for ki in range(5):
                tts.append(em.emit("TE", lambda t, pa=pa, ki=ki: t.transpose(
                    ptb.ap()[0:128, 128 * ki:128 * (ki + 1)], pa[:, 128 * ki:128 * (ki + 1)],
                    ident["bf"]), waits=[pn, ident["btok"]] + (wtr if ki == 0 else [])))
                pr.used(pi, tts[-1])
            pta, ptw, pti = ptr2.next()
            evt = em.emit("SE", lambda s, pta=pta: s.activation(
                out=pta, in_=ptb.ap()[0:128, 0:640], func=AF.Copy), waits=tts + ptw)
            ptb.rd(evt)
            mmavl = None
            for ki in range(5):
                kvc = kv0 // 128 + ki
                mmavl = em.emit("TE", lambda t, pta=pta, kvc=kvc, h=h, ki=ki, ab_=ab_: t.matmul(
                    ab_.ap()[0:64, 0:128], vtm.ap()[:, 512 * kvc + 64 * h:512 * kvc + 64 * (h + 1)],
                    pta[:, 128 * ki:128 * (ki + 1)], start=(ki == 0), stop=(ki == 4)),
                    waits=[evt] + (warA if ki == 0 else []))
            ptr2.used(pti, mmavl)
            pr.used(pi, mmavl)
            ct = em.emit("VE", lambda v, ab_=ab_, hp=hp, ro=ro, qc=qc: v.tensor_copy(
                attnfm.ap()[ro:ro + 64, 17 * 128 * hp + 128 * qc:17 * 128 * hp + 128 * (qc + 1)],
                ab_.ap()[0:64, 0:128]), waits=[mmavl])
            ab_.rd(ct)
            del state[i]

        for k in range(NIT + 2):
            if k < NIT:
                S(k)
            if 0 <= k - 1 < NIT:
                M(k - 1)
            if 0 <= k - 2 < NIT:
                T(k - 2)
        em.barrier()
    stack2.close()

    with ExitStack() as ph:
        wob = sbuf(nc, ph, "wob", [128, 4096], BF16)
        wl = []
        for ic in range(4):
            wl.append(em.dma("GE", wob.ap()[:, 1024 * ic:1024 * (ic + 1)],
                             wo_d[128 * ic:128 * (ic + 1), :]))
        xr = Ring(nc, ph, 3, (128, 512), F32, "l3x")
        orr = Ring(nc, ph, 4, (128, 512), F32, "l3o")
        it2 = 0
        for t0 in (0, 512, 1024, 1536, 2048):
            W = 128 if t0 == 2048 else 512
            for dc in range(8):
                bk = banks[it2 % 2]
                it2 += 1
                war = bk.wr()
                mml = None
                for ic in range(4):
                    mml = em.emit("TE", lambda t, ic=ic, dc=dc, t0=t0, W=W, bk=bk: t.matmul(
                        bk.ap()[0:128, 0:W],
                        wob.ap()[:, 1024 * ic + 128 * dc:1024 * ic + 128 * (dc + 1)],
                        attnfm.ap()[:, 17 * 128 * ic + t0:17 * 128 * ic + t0 + W],
                        start=(ic == 0), stop=(ic == 3)),
                        waits=wl + (war if ic == 0 else []))
                xa, xw, xi = xr.next()
                lt = em.dma("SY", xa[:, 0:W], xe_d[128 * dc:128 * (dc + 1), 640 + t0:640 + t0 + W],
                            waits=xw)
                oa, ow_, oi = orr.next()
                ad = em.emit("VE", lambda v, oa=oa, xa=xa, W=W, bk=bk: v.tensor_add(
                    oa[:, 0:W], bk.ap()[0:128, 0:W], xa[:, 0:W]), waits=[mml, lt] + ow_)
                bk.rd(ad)
                xr.used(xi, ad)
                st = em.dma("SY", xa_scr[128 * dc:128 * (dc + 1), t0:t0 + W], oa[:, 0:W],
                            waits=[ad])
                orr.used(oi, st)
        em.barrier()
    stack25.close()

    coffs = [128] * 4 + [127] * 4
    with ExitStack() as ph:
        stats = {k: sbuf(nc, ph, "st_" + k, [1, OWN], F32).ap() for k in ("m", "rstd", "var")}
        stats["e2t"] = e2t
        stats_phase(em, nc, ph, banks, xa_scr, OWN, stats, ones["bf"], eps.ap(),
                    want_extra=False, scr_dram=l2st_scr, chunk_off=coffs)

    with ExitStack() as ph:
        w1b = sbuf(nc, ph, "w1b", [128, 8 * 5460], BF16)
        w2b = sbuf(nc, ph, "w2b", [128, 22 * 1024], BF16)
        wl = []
        for cch in range(8):
            wl.append(em.dma("GE", w1b.ap()[:, 5460 * cch:5460 * (cch + 1)],
                             w1_d[128 * cch:128 * (cch + 1), :]))
        for j in range(22):
            rw = 42 if j == 21 else 128
            wl.append(em.dma("GE", w2b.ap()[0:rw, 1024 * j:1024 * (j + 1)],
                             w2_d[128 * j:128 * j + rw, :]))
        xna = sbuf(nc, ph, "xna", [128, 8 * 512], BF16)
        parena = sbuf(nc, ph, "parena", [128, 22 * 512], BF16)
        mB = sbuf(nc, ph, "l2mB", [128, 512], F32)
        rB = sbuf(nc, ph, "l2rB", [128, 512], F32)
        xr = Ring(nc, ph, 3, (128, 512), F32, "l4x")
        fr = Ring(nc, ph, 2, (128, 512), F32, "l4f")
        gar = Ring(nc, ph, 2, (128, 512), BF16, "l4g")
        yr = Ring(nc, ph, 3, (128, 512), F32, "l6y")
        orr = Ring(nc, ph, 3, (128, 512), F32, "l6o")
        if final:
            gr2 = Ring(nc, ph, 2, (128, 512), F32, "l6g")
            x1r = Ring(nc, ph, 2, (128, 512), F32, "l6x1")
            s1r = Ring(nc, ph, 2, (128, 512), F32, "l6s1")
        bA = [banks[0], banks[1]]
        bG = [banks[2], banks[3]]
        bD = [banks[4], banks[5]]
        xna_users = []
        last_down = None
        for ti in range(4):
            sl = slice(512 * ti, 512 * (ti + 1))
            b1_ = bcast_dma(em, mB.ap(), l2st_scr[0:1, sl], n=512, waits=xna_users)
            b2_ = bcast_dma(em, rB.ap(), l2st_scr[1:2, sl], n=512, waits=xna_users)
            tts = []
            for cch in range(8):
                xa, xw, xi = xr.next()
                lt = em.dma("SY", xa, xa_scr[128 * cch:128 * (cch + 1),
                                             coffs[cch] + 512 * ti:coffs[cch] + 512 * (ti + 1)],
                            waits=xw)
                fa, fw, fi = fr.next()
                t1 = em.emit("VE", lambda v, fa=fa, xa=xa: v.tensor_sub(fa, xa, mB.ap()),
                             waits=[lt, b1_] + fw)
                xr.used(xi, t1)
                t2 = em.emit("VE", lambda v, fa=fa: v.tensor_mul(fa, fa, rB.ap()), waits=[t1, b2_])
                t3 = em.emit("SE", lambda s, fa=fa, cch=cch: s.activation(
                    out=xna.ap()[:, 512 * cch:512 * (cch + 1)], in_=fa, func=AF.Identity,
                    scale=lgc.ap()[:, cch:cch + 1], bias=lbc.ap()[:, cch:cch + 1]),
                    waits=[t2] + ctoks + xna_users)
                fr.used(fi, t3)
                tts.append(t3)
            xna_users = []
            pms = []
            for j in range(22):
                rw = 42 if j == 21 else 128
                ba = bA[j % 2]
                bg = bG[j % 2]
                warA = ba.wr()
                warG = bg.wr()
                mma = mmg = None
                for cch in range(8):
                    mma = em.emit("TE", lambda t, j=j, cch=cch, rw=rw, ba=ba: t.matmul(
                        ba.ap()[0:rw, 0:512],
                        w1b.ap()[:, 5460 * cch + 128 * j:5460 * cch + 128 * j + rw],
                        xna.ap()[:, 512 * cch:512 * (cch + 1)], start=(cch == 0), stop=(cch == 7)),
                        waits=wl + tts + (warA if cch == 0 else []))
                for cch in range(8):
                    mmg = em.emit("TE", lambda t, j=j, cch=cch, rw=rw, bg=bg: t.matmul(
                        bg.ap()[0:rw, 0:512],
                        w1b.ap()[:, 5460 * cch + 2730 + 128 * j:5460 * cch + 2730 + 128 * j + rw],
                        xna.ap()[:, 512 * cch:512 * (cch + 1)], start=(cch == 0), stop=(cch == 7)),
                        waits=(warG if cch == 0 else []))
                xna_users.append(mmg)
                ga, gw, gi = gar.next()
                geo = em.emit("SE", lambda s, ga=ga, rw=rw, j=j, bg=bg: s.activation(
                    out=ga[0:rw, :], in_=bg.ap()[0:rw, 0:512], func=AF.Gelu,
                    bias=b1g.ap()[0:rw, j:j + 1]),
                    waits=[mmg, last_down] + ctoks + gw)
                bg.rd(geo)
                pm = em.emit("VE", lambda v, ga=ga, rw=rw, j=j, ba=ba: v.scalar_tensor_tensor(
                    parena.ap()[0:rw, 512 * j:512 * (j + 1)], ba.ap()[0:rw, 0:512],
                    b1a.ap()[0:rw, j:j + 1], ga[0:rw, :], ALU.add, ALU.mult),
                    waits=[mma, geo, last_down] + ctoks)
                ba.rd(pm)
                gar.used(gi, pm)
                pms.append(pm)
            for dc in range(8):
                bk = bD[dc % 2]
                war = bk.wr()
                mml = None
                for j in range(22):
                    rw = 42 if j == 21 else 128
                    mml = em.emit("TE", lambda t, j=j, dc=dc, rw=rw, bk=bk: t.matmul(
                        bk.ap()[0:128, 0:512],
                        w2b.ap()[0:rw, 1024 * j + 128 * dc:1024 * j + 128 * (dc + 1)],
                        parena.ap()[0:rw, 512 * j:512 * (j + 1)], start=(j == 0), stop=(j == 21)),
                        waits=[pms[j]] + (war if j == 0 else []))
                last_down = mml
                ya, yw, yi = yr.next()
                evh = em.emit("SE", lambda s, ya=ya, dc=dc, bk=bk: s.activation(
                    out=ya, in_=bk.ap()[0:128, 0:512], func=AF.Identity,
                    bias=b2c.ap()[:, dc:dc + 1]), waits=[mml] + ctoks + yw)
                bk.rd(evh)
                xa, xw, xi = xr.next()
                lt = em.dma("SY", xa, xa_scr[128 * dc:128 * (dc + 1),
                                             128 + 512 * ti:128 + 512 * (ti + 1)], waits=xw)
                oa, ow_, oi = orr.next()
                ad = em.emit("VE", lambda v, oa=oa, ya=ya, xa=xa: v.tensor_add(oa, ya, xa),
                             waits=[evh, lt] + ow_)
                yr.used(yi, ad)
                xr.used(xi, ad)
                if not final:
                    st = em.dma("SY", xout_o[128 * dc:128 * (dc + 1), sl], oa, waits=[ad])
                    orr.used(oi, st)
                else:
                    ga2, gw2, gi2 = gr2.next()
                    sg = em.emit("SE", lambda s, ga2=ga2, oa=oa: s.activation(
                        out=ga2, in_=oa, func=AF.Silu), waits=[ad] + gw2)
                    orr.used(oi, sg)
                    x1a, x1w, x1i = x1r.next()
                    l1 = em.dma("SY", x1a, x1_d[128 * dc:128 * (dc + 1), sl], waits=x1w)
                    s1a, s1w, s1i = s1r.next()
                    l2 = em.dma("SY", s1a, s1_d[128 * dc:128 * (dc + 1), sl], waits=s1w)
                    t1 = em.emit("VE", lambda v, ga2=ga2, s1a=s1a: v.tensor_mul(ga2, ga2, s1a),
                                 waits=[sg, l2])
                    s1r.used(s1i, t1)
                    t2 = em.emit("VE", lambda v, ga2=ga2, x1a=x1a: v.tensor_add(ga2, ga2, x1a),
                                 waits=[t1, l1])
                    x1r.used(x1i, t2)
                    st = em.dma("SY", out_o[128 * dc:128 * (dc + 1), sl], ga2, waits=[t2])
                    gr2.used(gi2, st)
        em.barrier()
    if standalone:
        em.replay()
        stack.close()
        return nc
    return None




def build_full(upto=99, frm=0, extra_scr=None, share=True):
    nc = bass.Bass("TRN2", target_bir_lowering=False, debug=False, detect_race_conditions=False)
    ostack = ExitStack()
    em = Em(nc, ostack)
    banks, ptb = make_psum(nc)

    xfm_d = decl_in(nc, "xfm", [1024, OWN])
    gcol_d = decl_in(nc, "gcol", [128, 1])
    g8_d = decl_in(nc, "g8", [128, 8])
    outp_d = decl_out(nc, "outp", [1024, OWN], F32)
    win = {}
    for nm, shp in (("l0wq", [1024, 64]), ("l0wk", [1024, 64]), ("l0wv", [1024, 64]),
                    ("l0wup", [4096, 1024]), ("l0wdn", [1024, 4096]), ("l0wo", [64, 1024]),
                    ("l1wq", [1024, 1024]), ("l1wk", [1024, 1024]), ("l1wv", [1024, 1024]),
                    ("l1sc1", [1024, 1]), ("l1b1", [1024, 1]), ("l1wo", [1024, 1024]),
                    ("l1sc2", [1024, 1]), ("l1b2", [1024, 1]), ("l1w1", [1024, 4096]),
                    ("l1b1v", [4096, 1]), ("l1w2", [4096, 1024]), ("l1b2v", [1024, 1]),
                    ("l2wqkv", [2, 1024, 1536]), ("l2wo", [2, 512, 1024]),
                    ("l2w1", [2, 1024, 5460]), ("l2w2", [2, 2730, 1024]),
                    ("l2b1", [2, 5460, 1]), ("l2b2", [2, 1024, 1]),
                    ("l2lg", [2, 1024, 1]), ("l2lb", [2, 1024, 1]),
                    ("l2qs", [2, 128, 1]), ("l2ks", [2, 128, 1]),
                    ("rqcs", [128, L2_QKV]), ("rqss", [128, L2_QKV]),
                    ("rkcs", [128, L2_QKV]), ("rkss", [128, L2_QKV]),
                    ("lmask", [17, 128, 640])):
        win[nm] = decl_in(nc, nm, shp)
    for fam in ("UL", "VR", "UR", "VL"):
        for l in range(4):
            win[f"l0{fam}{l}"] = decl_in(nc, f"l0{fam}{l}", [BS[l], RL[l]])

    c = {}
    c["c_ones"] = load_const(em, nc, ostack, "c_ones", [128, 1], want_bf=True)
    c["c_onesrow"] = load_const(em, nc, ostack, "c_onesrow", [1, 128])
    c["c_ident"] = load_const(em, nc, ostack, "c_ident", [128, 128], want_bf=True)
    c["c_ident2"] = load_const(em, nc, ostack, "c_ident2", [128, 64], want_bf=True)
    c["c_triu"] = load_const(em, nc, ostack, "c_triu", [128, 128], want_bf=True)
    c["c_l0masks"] = load_const(em, nc, ostack, "c_l0masks", [64, 9 * 64])
    c["c_e2"] = load_const(em, nc, ostack, "c_e2", [128, 2], want_bf=True)
    c["c_e2t"] = load_const(em, nc, ostack, "c_e2t", [2, 128], want_bf=True)
    gc_sb = sbuf(nc, ostack, "gc_sb", [128, 1], F32)
    em.dma("SY", gc_sb.ap(), gcol_d)
    gc8 = sbuf(nc, ostack, "gc8", [128, 8], F32)
    em.dma("SY", gc8.ap(), g8_d)
    for nm in ("rdA", "rlA", "rdB", "rlB", "rdC", "rlC", "rdD", "rlD"):
        em.xsem(nm)

    qf_s = dram_scr(nc, "qf_s", [64, OWN], BF16)
    kf_s = dram_scr(nc, "kf_s", [64, OWN], BF16)
    vf_s = dram_scr(nc, "vf_s", [64, OWN], BF16)
    kt_s = dram_scr(nc, "kt_s", [OWN, 64], BF16)
    vt_s = dram_scr(nc, "vt_s", [OWN, 64], BF16)
    aux_s = dram_scr(nc, "aux_s", [4, OWN, 4, 64], BF16)
    g0_s = dram_scr(nc, "g0_s", [1024, OWN], BF16)
    deltas_s = dram_scr(nc, "deltas_s", [NCH, 10, 64, 64], F32)
    dcc_s = dram_scr(nc, "dcc_s", [4, 8, 4, 10, 64, 64], F32)
    x1_s = dram_scr(nc, "x1_s", [1024, OWN], F32)
    pq_s = dram_scr(nc, "pq_s", [1024, OWN], BF16)
    pk_s = dram_scr(nc, "pk_s", [1024, OWN], BF16)
    pkt_s = dram_scr(nc, "pkt_s", [OWN, 1024], BF16)
    vaug_s = dram_scr(nc, "vaug_s", [OWN, 1040], BF16)
    pdel_s = dram_scr(nc, "pdel_s", [NCH, 16, 64, 65], F32)
    pcc_s = dram_scr(nc, "pcc_s", [8, 8, 2, 16, 64, 65], F32)
    xe1_s = dram_scr(nc, "xe1_s", [1024, L2_EXT], F32)
    tail1_s = dram_scr(nc, "tail1_s", [1024, L2_LEAD], F32)
    tcc1_s = dram_scr(nc, "tcc1_s", [4, 8, 256, L2_LEAD], F32)
    s1_s = dram_scr(nc, "s1_s", [1024, OWN], F32)
    xe2_s = xe1_s
    tail2_s = dram_scr(nc, "tail2_s", [1024, L2_LEAD], F32)
    tcc2_s = dram_scr(nc, "tcc2_s", [4, 8, 256, L2_LEAD], F32)
    if share:
        stats_sh = dram_scr(nc, "stats_sh", [8, OWN], F32)
        xa_sh = dram_scr(nc, "xa_sh", [1024, 128 + OWN], F32)
        h1_sh = dram_scr(nc, "h1_sh", [5460, OWN], BF16)
        prod_sh = dram_scr(nc, "prod_sh", [2730, OWN], BF16)
        l2scr_sh = (xa_sh, h1_sh, prod_sh, dram_scr(nc, "l2st_sh", [8, OWN], F32))
        nc._scrpool = {
            "wtup_scr": h1_sh[0:2048].rearrange("(a b) c -> a (b c)", b=16),
            "wtdn_scr": h1_sh[2048:4096].rearrange("(a b) c -> a (b c)", b=16),
            "hm_scr": h1_sh[0:4096],
            "t_scr": prod_sh[0:1024],
            "xatt_scr": xa_sh[:, 0:OWN],
        }
    else:
        stats_sh = None
        l2scr_sh = None
    if extra_scr:
        for shp in extra_scr:
            dram_scr(nc, "xtra", shp, F32)
    em.barrier()

    def strip(d):
        if not share:
            for k in ("stats_scr", "l2scr"):
                d.pop(k, None)
        return d

    def sec(fn):
        sstack = ExitStack()
        fn(sstack)
        em.barrier()
        sstack.close()

    def ctx_of(sstack):
        return (nc, sstack, em, banks, ptb)

    def tail_copy(sstack, src_ap, dst_ap):
        with ExitStack() as ph:
            tr = Ring(nc, ph, 3, (128, L2_LEAD), F32, "tcp")
            for cch in range(8):
                ta, tw, ti = tr.next()
                lt = em.dma("SY", ta, src_ap[128 * cch:128 * (cch + 1), OWN:OWN + L2_LEAD],
                            waits=tw)
                st = em.dma("GE", dst_ap[128 * cch:128 * (cch + 1), :], ta, waits=[lt])
                tr.used(ti, st)

    def lead_fill(sstack, cc_ap, dst_ap):
        with ExitStack() as ph:
            lr = Ring(nc, ph, 5, (128, L2_LEAD), F32, "ldf")
            ar = Ring(nc, ph, 2, (128, L2_LEAD), F32, "lda")
            for cch in range(8):
                aa, aw, ai = ar.next()
                acc = None
                for j in range(8):
                    la, lw, li = lr.next()
                    lt = em.dma("SY", la, cc_ap[cch // 2, j,
                                               128 * (cch % 2):128 * (cch % 2) + 128, :],
                                waits=lw)
                    if j == 0:
                        acc = em.emit("VE", lambda v, aa=aa, la=la: v.tensor_scalar(
                            aa, la, gc8.ap()[:, 0:1], None, ALU.mult), waits=[lt] + aw)
                    else:
                        t = em.emit("VE", lambda v, la=la, j=j: v.tensor_scalar(
                            la, la, gc8.ap()[:, j:j + 1], None, ALU.mult), waits=[lt])
                        acc = em.emit("VE", lambda v, aa=aa, la=la: v.tensor_tensor(
                            aa, aa, la, ALU.add), waits=[t, acc])
                    lr.used(li, acc)
                st = em.dma("GE", dst_ap[128 * cch:128 * (cch + 1), 0:L2_LEAD], aa, waits=[acc])
                ar.used(ai, st)

    W = win

    def fin():
        em.barrier()
        em.replay()
        ostack.close()
        return nc

    if frm <= 0:
        sec(lambda s: build_L0A(io=strip({"xfm": xfm_d, "qf": qf_s, "kf": kf_s, "vf": vf_s,
                                    "kt": kt_s, "vt": vt_s, "aux": aux_s, "g0": g0_s,
                                    "deltas": deltas_s, "stats_scr": stats_sh, "c": c, **W}), ctx=ctx_of(s)))
    for i_ in range(4):
        em.rdma(dcc_s[i_].rearrange("j c s r k -> j (c s) (r k)"),
                deltas_s[4 * i_:4 * (i_ + 1)].rearrange("c s r k -> (c s) (r k)"),
                "R i j -> S R i j", "rdA", "rlA")
    em.barrier(extra=[("s", "rdA", 64), ("s", "rlA", 64)])
    if upto <= 1:
        return fin()
    if frm <= 1:
        sec(lambda s: build_L0B(io=strip({"xfm": xfm_d, "qf": qf_s, "kf": kf_s, "vf": vf_s,
                                    "kt": kt_s, "vt": vt_s, "aux": aux_s, "g0": g0_s,
                                    "dprev": dcc_s[0, 0], "dprev8": (dcc_s, gc8.ap()), "x1": x1_s,
                                    "c": c, **W}), ctx=ctx_of(s)))
    if upto <= 2:
        return fin()
    if frm <= 2:
        sec(lambda s: build_L1A(io=strip({"x1": x1_s, "pq": pq_s, "pk": pk_s, "pkt": pkt_s,
                                    "vaug": vaug_s, "pdel": pdel_s,
                                    "stats_scr": stats_sh,
                                    "xn1": xe1_s[:, L2_LEAD:L2_LEAD + OWN],
                                    "c": c, **W}), ctx=ctx_of(s)))
    for i_ in range(8):
        em.rdma(pcc_s[i_].rearrange("j c h r k -> j (c h) (r k)"),
                pdel_s[2 * i_:2 * (i_ + 1)].rearrange("c h r k -> (c h) (r k)"),
                "R i j -> S R i j", "rdB", "rlB")
    sec(lambda s: tail_copy(s, xe1_s, tail1_s))
    for i_ in range(4):
        em.rdma(tcc1_s[i_].rearrange("j (a b) c -> j a b c", b=2),
                tail1_s[256 * i_:256 * (i_ + 1), :].rearrange("(a b) c -> a b c", b=2),
                "R a b c -> S R a b c", "rdC", "rlC")
    em.barrier(extra=[("s", "rdB", 128), ("s", "rlB", 128), ("s", "rdC", 64), ("s", "rlC", 64)])
    sec(lambda s: lead_fill(s, tcc1_s, xe1_s))
    if upto <= 3:
        return fin()
    sec(lambda s: build_L1B(io=strip({"xn1": xe1_s[:, L2_LEAD:L2_LEAD + OWN], "stats_scr": stats_sh, "pq": pq_s,
                                "pk": pk_s, "pkt": pkt_s, "vaug": vaug_s,
                                "pdelp": pcc_s[0, 0], "pdelp8": (pcc_s, gc8.ap()), "s1": s1_s,
                                "c": c, **W}), ctx=ctx_of(s)))

    def l2io(l):
        d = {"xe": xe1_s if l == 0 else xe2_s, "c": c, "l2scr": l2scr_sh,
             "rqcs": W["rqcs"], "rqss": W["rqss"], "rkcs": W["rkcs"], "rkss": W["rkss"],
             "lmask": W["lmask"]}
        for nm in ("l2wqkv", "l2wo", "l2w1", "l2w2", "l2b1", "l2b2", "l2lg", "l2lb",
                   "l2qs", "l2ks"):
            d[nm] = W[nm][l]
        if l == 0:
            d["xout"] = xe2_s[:, L2_LEAD:L2_LEAD + OWN]
        else:
            d["x1"] = x1_s
            d["s1"] = s1_s
            d["outp"] = outp_d
        return d

    if upto <= 4:
        return fin()
    sec(lambda s: build_L2(final=False, io=strip(l2io(0)), ctx=ctx_of(s)))
    if upto <= 5:
        return fin()
    sec(lambda s: tail_copy(s, xe2_s, tail2_s))
    for i_ in range(4):
        em.rdma(tcc2_s[i_].rearrange("j (a b) c -> j a b c", b=2),
                tail2_s[256 * i_:256 * (i_ + 1), :].rearrange("(a b) c -> a b c", b=2),
                "R a b c -> S R a b c", "rdD", "rlD")
    em.barrier(extra=[("s", "rdD", 64), ("s", "rlD", 64)])
    sec(lambda s: lead_fill(s, tcc2_s, xe2_s))
    if upto <= 6:
        return fin()
    sec(lambda s: build_L2(final=True, io=strip(l2io(1)), ctx=ctx_of(s)))
    return fin()


def build_g1():
    nc, stack, em, banks, ptb, _ = _ctx(None)
    io = {"c": {}}
    c = io["c"]
    c["c_ones"] = load_const(em, nc, stack, "c_ones", [128, 1], want_bf=True)
    c["c_onesrow"] = load_const(em, nc, stack, "c_onesrow", [1, 128])
    c["c_ident"] = load_const(em, nc, stack, "c_ident", [128, 128], want_bf=True)
    c["c_triu"] = load_const(em, nc, stack, "c_triu", [128, 128], want_bf=True)
    c["c_l0masks"] = load_const(em, nc, stack, "c_l0masks", [64, 9 * 64])
    x1_s = dram_scr(nc, "x1g", [1024, OWN], F32)
    x1_o = decl_out(nc, "x1", [1024, OWN], F32)
    stats_sh = dram_scr(nc, "statsg", [8, OWN], F32)
    io["stats_scr"] = stats_sh
    io["x1"] = x1_s
    em.barrier()

    def sec(fn):
        sstack = ExitStack()
        fn(sstack)
        em.barrier()
        sstack.close()

    sec(lambda s: build_L0B(io=io, ctx=(nc, s, em, banks, ptb)))
    with ExitStack() as ph:
        cr = Ring(nc, ph, 3, (128, 512), F32, "x1c")
        for cch in range(8):
            for it in range(4):
                ca, cw, ci = cr.next()
                sl = slice(512 * it, 512 * (it + 1))
                lt = em.dma("SY", ca, x1_s[128 * cch:128 * (cch + 1), sl], waits=cw)
                st = em.dma("GE", x1_o[128 * cch:128 * (cch + 1), sl], ca, waits=[lt])
                cr.used(ci, st)
        em.barrier()
    sec(lambda s: build_L1A(io=io, ctx=(nc, s, em, banks, ptb)))
    em.barrier()
    em.replay()
    stack.close()
    return nc


def build_g2():
    nc, stack, em, banks, ptb, _ = _ctx(None)
    io = {"c": {}}
    c = io["c"]
    c["c_ones"] = load_const(em, nc, stack, "c_ones", [128, 1], want_bf=True)
    c["c_onesrow"] = load_const(em, nc, stack, "c_onesrow", [1, 128])
    c["c_ident"] = load_const(em, nc, stack, "c_ident", [128, 128], want_bf=True)
    c["c_triu"] = load_const(em, nc, stack, "c_triu", [128, 128], want_bf=True)
    c["c_e2"] = load_const(em, nc, stack, "c_e2", [128, 2], want_bf=True)
    c["c_e2t"] = load_const(em, nc, stack, "c_e2t", [2, 128], want_bf=True)
    xe_d = decl_in(nc, "xe", [1024, L2_EXT])
    stats_sh = dram_scr(nc, "statsg", [8, OWN], F32)
    io["stats_scr"] = stats_sh
    io["xn1"] = xe_d[:, L2_LEAD:L2_LEAD + OWN]
    io["xe"] = xe_d
    em.barrier()

    def sec(fn):
        sstack = ExitStack()
        fn(sstack)
        em.barrier()
        sstack.close()

    sec(lambda s: build_L1B(io=io, ctx=(nc, s, em, banks, ptb)))
    sec(lambda s: build_L2(final=False, io=io, ctx=(nc, s, em, banks, ptb)))
    em.barrier()
    em.replay()
    stack.close()
    return nc


def l2_rotary(own_start):
    DH, N, W = 64, 4096, 512
    half = np.arange(0, DH, 2, dtype=np.float64)
    inv = 1.0 / 10000.0 ** (half / DH)
    d_freq = np.concatenate([inv, inv])
    base = (half + 0.4 * DH) / (1.4 * DH)
    d_base = np.concatenate([base, base])
    pos = own_start - 640 + np.arange(L2_QKV, dtype=np.float64)
    ang = d_freq[:, None] * pos[None, :]
    cos, sin = np.cos(ang), np.sin(ang)
    power = (pos - N // 2) / (W // 2)
    scale = d_base[:, None] ** power[None, :]
    sgn = np.ones((64, 1)); sgn[:32] = -1.0
    out = {}
    for nm, t in (("rqcs", 8.0 * cos * scale), ("rqss", 8.0 * sin * scale * sgn),
                  ("rkcs", cos / scale), ("rkss", sin / scale * sgn)):
        out[nm] = np.ascontiguousarray(np.tile(t, (2, 1)).astype(np.float32))
    return out


def l2_masks(own_start):
    FM = np.float32(np.finfo(np.float32).max)
    m = np.zeros((17, 128, 640), dtype=np.float32)
    r = np.arange(128)[:, None]
    c = np.arange(640)[None, :]
    for qc in range(17):
        gq = own_start + 128 * (qc - 1)
        j = gq - 512 + c
        valid = (c >= r) & (c <= r + 512) & (j >= 0)
        m[qc] = np.where(valid, np.float32(0.0), -FM)
    return m


def _core_slices(x):
    out = []
    for c in range(NCORE):
        b, h = c // 2, c % 2
        sl = np.ascontiguousarray(x[b, 2048 * h:2048 * (h + 1), :].T.astype(np.float32))
        out.append(sl)
    return out


def _zpad9(d):
    d = np.array(d)
    d[:, 9, 1:, :] = 0.0
    return d


def _l2_ext(fm_list):
    exts = []
    for c in range(NCORE):
        own = fm_list[c]
        if c % 2 == 0:
            lead = np.zeros((1024, L2_LEAD), dtype=np.float32)
        else:
            lead = fm_list[c - 1][:, OWN - L2_LEAD:]
        exts.append(np.ascontiguousarray(np.concatenate([lead, own], axis=1)))
    return exts


def kernel(**inputs):
    inp = {k: np.ascontiguousarray(np.asarray(v)) for k, v in inputs.items()}
    cst = host_constants()
    x = np.asarray(inp["L0_attn_x"], np.float32)
    slices = _core_slices(x)
    ids = list(range(NCORE))
    cb = {"c_ones": cst["c_ones"], "c_onesrow": cst["c_onesrow"], "c_ident": cst["c_ident"]}

    nc = build_L0A()
    in_maps = []
    for c in range(NCORE):
        m = {"xfm": slices[c],
             "l0wq": inp["L0_attn_Wq"], "l0wk": inp["L0_attn_Wk"], "l0wv": inp["L0_attn_Wv"],
             "l0wup": inp["L0_ffn_w_up"], "l0wdn": inp["L0_ffn_w_down"],
             "c_ident2": cst["c_ident2"], "c_e2t": cst["c_e2t"], **cb}
        for fam in ("UL", "VR", "UR", "VL"):
            for l in range(4):
                m[f"l0{fam}{l}"] = inp[f"L0_attn_{fam}{l}"]
        in_maps.append(m)
    resA = run_bass_kernel_spmd(nc, in_maps, core_ids=ids)

    nc = build_g1()
    zdel = np.zeros((NCH, 10, 64, 64), np.float32)
    in_maps = []
    for c in range(NCORE):
        rA = resA.results[c]
        m = {"xfm": slices[c],
             "qf": np.asarray(rA["qf"]), "kf": np.asarray(rA["kf"]), "vf": np.asarray(rA["vf"]),
             "kt": np.asarray(rA["kt"]), "vt": np.asarray(rA["vt"]),
             "aux": np.asarray(rA["aux"]), "g0": np.asarray(rA["g0"]),
             "dprev": _zpad9(np.asarray(resA.results[c - 1]["deltas"], np.float32))
                      if c % 2 == 1 else zdel,
             "l0wo": inp["L0_attn_Wo"],
             "l1wq": inp["L1_attn_wq"].reshape(1024, 1024),
             "l1wk": inp["L1_attn_wk"].reshape(1024, 1024),
             "l1wv": inp["L1_attn_wv"].reshape(1024, 1024),
             "l1sc1": inp["L1_attn_ln1_scale"].reshape(1024, 1),
             "l1b1": inp["L1_attn_ln1_bias"].reshape(1024, 1),
             "c_triu": cst["c_triu"], "c_l0masks": cst["c_l0masks"],
             "c_e2t": cst["c_e2t"], **cb}
        for fam in ("UL", "VR", "UR", "VL"):
            for l in range(4):
                m[f"l0{fam}{l}"] = inp[f"L0_attn_{fam}{l}"]
        in_maps.append(m)
    resB = run_bass_kernel_spmd(nc, in_maps, core_ids=ids)
    x1fm = [np.ascontiguousarray(np.asarray(resB.results[c]["x1"], np.float32))
            for c in range(NCORE)]
    xn1fm = [np.ascontiguousarray(np.asarray(resB.results[c]["xn1"], np.float32))
             for c in range(NCORE)]

    rot = {0: l2_rotary(0), OWN: l2_rotary(OWN)}
    msk = {0: l2_masks(0), OWN: l2_masks(OWN)}

    def l2w(l):
        return {"l2wqkv": np.ascontiguousarray(inp["L1_ffn_wqkv"][l]),
                "l2wo": np.ascontiguousarray(inp["L1_ffn_wo"][l]),
                "l2w1": np.ascontiguousarray(inp["L1_ffn_w1"][l]),
                "l2w2": np.ascontiguousarray(inp["L1_ffn_w2"][l]),
                "l2b1": np.ascontiguousarray(inp["L1_ffn_b1"][l].reshape(5460, 1)),
                "l2b2": np.ascontiguousarray(inp["L1_ffn_b2"][l].reshape(1024, 1)),
                "l2lg": np.ascontiguousarray(inp["L1_ffn_ln_g"][l].reshape(1024, 1)),
                "l2lb": np.ascontiguousarray(inp["L1_ffn_ln_b"][l].reshape(1024, 1)),
                "l2qs": np.ascontiguousarray(np.tile(inp["L1_ffn_q_scale"][l], 2)
                                             .reshape(128, 1)),
                "l2ks": np.ascontiguousarray(np.tile(inp["L1_ffn_k_scale"][l], 2)
                                             .reshape(128, 1))}

    nc = build_g2()
    zp = np.zeros((NCH, 16, 64, 65), np.float32)
    exts = _l2_ext(xn1fm)
    in_maps = []
    for c in range(NCORE):
        rB = resB.results[c]
        own_start = OWN * (c % 2)
        m = {"xe": exts[c],
             "pq": np.asarray(rB["pq"]), "pk": np.asarray(rB["pk"]),
             "pkt": np.asarray(rB["pkt"]), "vaug": np.asarray(rB["vaug"]),
             "pdelp": (np.asarray(resB.results[c - 1]["pdel"], np.float32)
                       if c % 2 == 1 else zp),
             "l1wo": inp["L1_attn_wo"].reshape(1024, 1024),
             "l1sc2": inp["L1_attn_ln2_scale"].reshape(1024, 1),
             "l1b2": inp["L1_attn_ln2_bias"].reshape(1024, 1),
             "l1w1": inp["L1_attn_w1"], "l1b1v": inp["L1_attn_b1"].reshape(4096, 1),
             "l1w2": inp["L1_attn_w2"], "l1b2v": inp["L1_attn_b2"].reshape(1024, 1),
             "lmask": msk[own_start],
             "c_triu": cst["c_triu"], "c_e2": cst["c_e2"], "c_e2t": cst["c_e2t"],
             **cb, **l2w(0), **rot[own_start]}
        in_maps.append(m)
    resC = run_bass_kernel_spmd(nc, in_maps, core_ids=ids)
    s1fm = [np.ascontiguousarray(np.asarray(resC.results[c]["s1"], np.float32))
            for c in range(NCORE)]
    xafm = [np.ascontiguousarray(np.asarray(resC.results[c]["xout"], np.float32))
            for c in range(NCORE)]

    nc = build_L2(final=True)
    exts2 = _l2_ext(xafm)
    in_maps = []
    for c in range(NCORE):
        own_start = OWN * (c % 2)
        m = {"xe": exts2[c], "lmask": msk[own_start],
             "x1": x1fm[c], "s1": s1fm[c],
             "c_ones": cst["c_ones"], "c_ident": cst["c_ident"],
             "c_e2": cst["c_e2"], "c_e2t": cst["c_e2t"],
             **l2w(1), **rot[own_start]}
        in_maps.append(m)
    resD = run_bass_kernel_spmd(nc, in_maps, core_ids=ids)

    out = np.empty((4, 4096, 1024), dtype=np.float32)
    for c in range(NCORE):
        b, h = c // 2, c % 2
        out[b, OWN * h:OWN * (h + 1), :] = np.asarray(resD.results[c]["outp"], np.float32).T
    return out


# version 16 (final)
# speedup vs baseline: 1.0098x; 1.0098x over previous
import numpy as np
from contextlib import ExitStack

import concourse.bass as bass
import concourse.mybir as mybir
from concourse.bass_utils import run_bass_kernel_spmd

F32 = mybir.dt.float32
BF16 = mybir.dt.bfloat16
AF = mybir.ActivationFunctionType
ALU = mybir.AluOpType
AX = mybir.AxisListType

NCORE = 8
OWN = 2048
NT = OWN // 512
NCH = OWN // 128

BS = [4, 8, 16, 32]
NN = [8, 4, 2, 1]
RL = [4, 8, 8, 8]

ENGS = ("TE", "VE", "SE", "GE", "SY")


DPOOL = {"SY": 48, "GE": 24, "SE": 8, "CC": 4}
PAIR_RG = [[0, 1], [2, 3], [4, 5], [6, 7]]


class Em:
    def __init__(self, nc, stack):
        self.nc = nc
        self.specs = {e: [] for e in ENGS}
        self.pcnt = {e: 0 for e in ENGS}
        self.psem = {e: stack.enter_context(nc.semaphore(f"p{e}")) for e in ENGS}
        self.dpool = {e: [stack.enter_context(nc.semaphore(f"D{e}{i}")) for i in range(n)]
                      for e, n in DPOOL.items()}
        self.duses = {e: [0] * n for e, n in DPOOL.items()}
        self.drot = {e: 0 for e in DPOOL}
        self.ddirty = {e: set() for e in DPOOL}
        self.hw = {e: {} for e in ENGS}
        self.xsems = {}
        self._stack = stack

    def xsem(self, name):
        h = self._stack.enter_context(self.nc.semaphore(name))
        self.xsems[name] = h
        return h

    def emit(self, e, fn, waits=(), inc=True):
        wl = []
        for w in waits:
            if w is None:
                continue
            if w[0] == "s":
                key = ("s", w[1])
                n = w[2]
            elif w[0] == "p":
                _, src, n = w
                assert n <= self.pcnt[src], f"future token {w} vs {self.pcnt[src]}"
                key = ("p", src, n) if src == "GE" else ("p", src)
            else:
                _, src, i, n = w
                assert n <= self.duses[src][i], f"future dma token {w}"
                key = ("D", src, i)
                n = 16 * n
            if self.hw[e].get(key, 0) >= n:
                continue
            self.hw[e][key] = n
            wl.append((key, n))
        tok = None
        if inc:
            self.pcnt[e] += 1
            tok = ("p", e, self.pcnt[e])
        self.specs[e].append((fn, wl, tok))
        return tok

    def dma(self, e, out, in_, waits=(), **kw):
        i = self.drot[e] % len(self.dpool[e])
        self.drot[e] += 1
        prev = self.duses[e][i]
        w = list(waits)
        if prev:
            w.append(("D", e, i, prev))
        self.duses[e][i] = prev + 1
        self.ddirty[e].add(i)
        tok = ("D", e, i, prev + 1)
        wl = []
        for ww in w:
            if ww is None:
                continue
            if ww[0] == "s":
                key = ("s", ww[1])
                n = ww[2]
            elif ww[0] == "p":
                _, src, n = ww
                assert n <= self.pcnt[src]
                key = ("p", src, n) if src == "GE" else ("p", src)
            else:
                _, src, j, n = ww
                assert n <= self.duses[src][j]
                key = ("D", src, j)
                n = 16 * n
            if self.hw[e].get(key, 0) >= n:
                continue
            self.hw[e][key] = n
            wl.append((key, n))
        sem = self.dpool[e][i]
        self.specs[e].append((lambda eng, out=out, in_=in_, kw=kw, sem=sem:
                              eng.dma_start(out=out, in_=in_, **kw).then_inc(sem, 16), wl, None))
        return tok

    def cc(self, kind, ins, outs, rg, waits=(), op=None):
        i = self.drot["CC"] % len(self.dpool["CC"])
        self.drot["CC"] += 1
        prev = self.duses["CC"][i]
        w = list(waits)
        if prev:
            w.append(("D", "CC", i, prev))
        self.duses["CC"][i] = prev + 1
        self.ddirty["CC"].add(i)
        tok = ("D", "CC", i, prev + 1)
        wl = []
        for ww in w:
            if ww is None:
                continue
            if ww[0] == "p":
                _, s2, n = ww
                key = ("p", s2, n) if s2 == "GE" else ("p", s2)
            else:
                _, s2, j, n = ww
                key = ("D", s2, j)
                n = 16 * n
            if self.hw["GE"].get(key, 0) >= n:
                continue
            self.hw["GE"][key] = n
            wl.append((key, n))
        sem = self.dpool["CC"][i]
        aop = op if op is not None else mybir.AluOpType.bypass
        self.specs["GE"].append((lambda eng, ins=ins, outs=outs, rg=rg, aop=aop, sem=sem:
                                 eng.collective_compute(kind, aop, replica_groups=rg,
                                                        ins=ins, outs=outs).then_inc(sem, 16),
                                 wl, None))
        return tok

    def rdma(self, out_ap, in_ap, pattern, rname, lname):
        rs = self.xsems[rname]
        ls = self.xsems[lname]
        self.specs["GE"].append((lambda eng, out_ap=out_ap, in_ap=in_ap, rs=rs, ls=ls,
                                 pattern=pattern: eng.remote_dma_broadcast_einops(
                                     out_ap, in_ap, rs, local_sem=ls, repeat=pattern),
                                 [], None))
        return ("s", rname, 16)

    def barrier(self, extra=()):
        snap = {e: self.pcnt[e] for e in ENGS}
        dwaits = []
        for q in DPOOL:
            dwaits += [("D", q, i, self.duses[q][i]) for i in self.ddirty[q]]
            self.ddirty[q] = set()
        for e in ENGS:
            waits = [("p", s, snap[s]) for s in ENGS if s != e and snap[s] > 0]
            self.emit(e, lambda eng: eng.nop(), waits=waits + dwaits + list(extra), inc=True)
        return None

    def replay(self):
        nc = self.nc

        def mk(e):
            def body(engobj):
                for fn, wl, tok in self.specs[e]:
                    for key, n in wl:
                        if key[0] == "p":
                            engobj.wait_ge(self.psem[key[1]], n)
                        elif key[0] == "s":
                            engobj.wait_ge(self.xsems[key[1]], n)
                        else:
                            engobj.wait_ge(self.dpool[key[1]][key[2]], n)
                    ins = fn(engobj)
                    if tok is not None:
                        ins.then_inc(self.psem[e], 1)
            return body

        with nc.Block() as block:
            block.tensor(mk("TE"))
            block.vector(mk("VE"))
            block.scalar(mk("SE"))
            block.gpsimd(mk("GE"))
            block.sync(mk("SY"))


class Ring:

    def __init__(self, nc, stack, n, shape, dt, name="r"):
        self.t = [sbuf(nc, stack, f"{name}{i}", shape, dt) for i in range(n)]
        self.use = [[] for _ in range(n)]
        self.i = 0

    def next(self):
        i = self.i % len(self.t)
        self.i += 1
        w = self.use[i]
        self.use[i] = []
        return self.t[i].ap(), w, i

    def used(self, slot, tok):
        if tok is not None:
            self.use[slot].append(tok)


class PT:

    def __init__(self, ap):
        self._ap = ap
        self.readers = []

    def ap(self):
        return self._ap

    def wr(self):
        w = self.readers
        self.readers = []
        return w

    def rd(self, tok):
        if tok is not None:
            self.readers.append(tok)


def make_psum(nc):
    banks = [PT(nc.alloc_psum_tensor(f"pb{i}", [128, 512], F32).ap()) for i in range(7)]
    ptb = PT(nc.alloc_psum_tensor("ptb", [128, 1024], BF16).ap())
    return banks, ptb



def host_constants():
    c = {}
    c["c_ident"] = np.eye(128, dtype=np.float32)
    c["c_tril"] = np.tril(np.ones((128, 128), dtype=np.float32))
    masks = np.zeros((9, 64, 64), dtype=np.float32)
    for n in range(16):
        masks[0, 4 * n:4 * n + 4, 4 * n:4 * n + 4] = 1
    for l in range(4):
        for n in range(NN[l]):
            t0 = n * 2 * BS[l]
            b0 = t0 + BS[l]
            masks[1 + l, b0:b0 + BS[l], t0:t0 + BS[l]] = 1
            masks[5 + l, t0:t0 + BS[l], b0:b0 + BS[l]] = 1
    c["c_l0masks"] = np.ascontiguousarray(np.concatenate([masks[k] for k in range(9)], axis=1))
    c["c_triu"] = np.triu(np.ones((128, 128), dtype=np.float32))
    c["c_ident2"] = np.vstack([np.eye(64, dtype=np.float32)] * 2)
    c["c_ones"] = np.ones((128, 1), dtype=np.float32)
    c["c_onesrow"] = np.ones((1, 128), dtype=np.float32)
    e2 = np.zeros((128, 2), dtype=np.float32)
    e2[:64, 0] = 1.0
    e2[64:, 1] = 1.0
    c["c_e2"] = e2
    e2t = np.zeros((2, 128), dtype=np.float32)
    e2t[0, :64] = 1.0
    e2t[1, 64:] = 1.0
    c["c_e2t"] = e2t
    return c



def decl_in(nc, name, shape, dt=F32):
    return nc.dram_tensor(name, list(shape), dt, kind="ExternalInput").ap()


def decl_out(nc, name, shape, dt=F32):
    return nc.dram_tensor(name, list(shape), dt, kind="ExternalOutput").ap()


_NAMEC = [0]


def sbuf(nc, stack, name, shape, dt):
    _NAMEC[0] += 1
    return stack.enter_context(nc.sbuf_tensor(f"{name}_{_NAMEC[0]}", list(shape), dt))


def dram_scr(nc, name, shape, dt):
    pool = getattr(nc, "_scrpool", None)
    if pool is not None and name in pool:
        return pool[name]
    _NAMEC[0] += 1
    return nc.dram_tensor(f"{name}_{_NAMEC[0]}", list(shape), dt, kind="Internal").ap()


def _ctx(ctx):
    if ctx is not None:
        return ctx + (False,)
    nc = bass.Bass("TRN2", target_bir_lowering=False, debug=False, detect_race_conditions=False)
    stack = ExitStack()
    em = Em(nc, stack)
    banks, ptb = make_psum(nc)
    return nc, stack, em, banks, ptb, True


def _io_in(io, nc, name, shape, dt=F32):
    if io and name in io:
        return io[name]
    return decl_in(nc, name, shape, dt)


def _io_out(io, nc, name, shape, dt=F32):
    if io and name in io:
        return io[name]
    return decl_out(nc, name, shape, dt)


def _lc(io, em, nc, stack, name, shape, want_bf=False):
    if io and name in io.get("c", {}):
        return io["c"][name]
    return load_const(em, nc, stack, name, shape, want_bf=want_bf)


def load_const(em, nc, stack, name, shape, want_bf=False, q="SY"):
    d = decl_in(nc, name, shape)
    h = sbuf(nc, stack, "sb_" + name, shape, F32)
    tok = em.dma(q, h.ap(), d)
    res = {"f32": h.ap(), "ftok": tok, "btok": tok}
    if want_bf:
        hb = sbuf(nc, stack, "sbb_" + name, shape, BF16)
        res["btok"] = em.emit("VE", lambda v: v.tensor_copy(hb.ap(), h.ap()), waits=[tok])
        res["bf"] = hb.ap()
    return res


def memset(em, e, ap, val, waits=()):
    return em.emit(e, lambda eng: eng.memset(ap, val), waits=waits)


def bcast_dma(em, dst_ap, dram_row, P=128, n=None, waits=(), q="GE"):
    n = n if n is not None else dram_row.shape[-1]
    return em.dma(q, dst_ap, dram_row.broadcast_to([P, n]), waits=waits)


def bcast_row(em, bankPT, row_ap, onesrow_f32, n=512, off=0, waits=()):
    w = bankPT.wr() + list(waits)
    hi, hw_, hi_i = em.bc_hi.next()
    lo, lw_, lo_i = em.bc_lo.next()
    lf, ff_, lf_i = em.bc_f.next()
    t1 = em.emit("VE", lambda v: v.tensor_copy(hi[0:1, 0:n], row_ap), waits=list(waits) + hw_)
    t2 = em.emit("VE", lambda v: v.tensor_sub(lf[0:1, 0:n], row_ap, hi[0:1, 0:n]), waits=[t1] + ff_)
    t3 = em.emit("VE", lambda v: v.tensor_copy(lo[0:1, 0:n], lf[0:1, 0:n]), waits=[t2] + lw_)
    em.bc_f.used(lf_i, t3)
    ob = em.onesrow_bf
    m1 = em.emit("TE", lambda t, ob=ob: t.matmul(bankPT.ap()[0:128, off:off + n], ob,
                                                 hi[0:1, 0:n], start=True, stop=False), waits=[t1] + w)
    m2 = em.emit("TE", lambda t, ob=ob: t.matmul(bankPT.ap()[0:128, off:off + n], ob,
                                                 lo[0:1, 0:n], start=False, stop=True), waits=[t3])
    em.bc_hi.used(hi_i, m1)
    em.bc_lo.used(lo_i, m2)
    return m2


def stats_rows(em, nc, st, sl, pb, eps_ap, want_extra, toks, bw, sel_bf, sel_tok, W):
    cb, cw, ci = st["cbr"].next()
    c1 = em.emit("SE", lambda s, cb=cb: s.activation(out=cb[0:33, 0:W], in_=pb.ap()[0:33, 0:W],
                                                     func=AF.Copy), waits=toks + cw)
    lo, lw_, li_ = st["cbl"].next()
    c2 = em.emit("VE", lambda v, cb=cb, lo=lo: v.tensor_sub(lo[0:33, 0:W], pb.ap()[0:33, 0:W],
                                                            cb[0:33, 0:W]), waits=[c1] + lw_)
    pb.rd(c1)
    pb.rd(c2)
    war = bw.wr()
    mm1 = em.emit("TE", lambda t, cb=cb: t.matmul(bw.ap()[0:128, 0:W], sel_bf, cb[0:33, 0:W],
                                                  start=True, stop=False),
                  waits=[c1, sel_tok] + war)
    mmW = em.emit("TE", lambda t, lo=lo: t.matmul(bw.ap()[0:128, 0:W], sel_bf, lo[0:33, 0:W],
                                                  start=False, stop=True), waits=[c2])
    st["cbr"].used(ci, mm1)
    st["cbl"].used(li_, mmW)
    g = st["wide"]
    mT, vT, sT, xT = g["m"], g["v"], g["s"], g["x"]
    pw = st.get("prev_readers", [])
    tm = em.emit("VE", lambda v: v.tensor_scalar_mul(mT[0:64, 0:W], bw.ap()[0:64, 0:W],
                                                     1.0 / 1024.0), waits=[mmW] + pw)
    s1 = em.emit("VE", lambda v: v.tensor_mul(xT[0:64, 0:W], mT[0:64, 0:W], mT[0:64, 0:W]),
                 waits=[tm])
    s2 = em.emit("VE", lambda v: v.scalar_tensor_tensor(vT[0:64, 0:W], bw.ap()[64:128, 0:W],
                                                        1.0 / 1024.0, xT[0:64, 0:W],
                                                        ALU.mult, ALU.subtract), waits=[s1] + pw)
    bw.rd(tm)
    bw.rd(s2)
    t5 = em.emit("SE", lambda s: s.activation(out=sT[0:64, 0:W], in_=vT[0:64, 0:W], func=AF.Sqrt,
                                              bias=eps_ap[0:64, 0:1]), waits=[s2])
    t6 = em.emit("VE", lambda v: v.reciprocal(xT[0:64, 0:W], sT[0:64, 0:W]), waits=[t5, s1])
    rd = [em.dma("SY", st["scr_dram"][0:1, sl], mT[0:1, 0:W], waits=[tm]),
          em.dma("SY", st["scr_dram"][1:2, sl], xT[0:1, 0:W], waits=[t6])]
    last = t6
    if want_extra:
        rT, aT, bT = g["r"], g["a"], g["b"]
        u1 = em.emit("SE", lambda s: s.activation(out=rT[0:64, 0:W], in_=vT[0:64, 0:W],
                                                  func=AF.Sqrt), waits=[s2] + pw)
        u2 = em.emit("VE", lambda v: v.tensor_mul(aT[0:64, 0:W], rT[0:64, 0:W], xT[0:64, 0:W]),
                     waits=[u1, t6])
        u3 = em.emit("VE", lambda v: v.tensor_scalar_add(aT[0:64, 0:W], aT[0:64, 0:W], 1e-8),
                     waits=[u2])
        u4 = em.emit("VE", lambda v: v.reciprocal(aT[0:64, 0:W], aT[0:64, 0:W]), waits=[u3])
        u5 = em.emit("VE", lambda v: v.tensor_mul(bT[0:64, 0:W], aT[0:64, 0:W], xT[0:64, 0:W]),
                     waits=[u4])
        rd.append(em.dma("SY", st["scr_dram"][2:3, sl], bT[0:1, 0:W], waits=[u5]))
        w1 = em.emit("VE", lambda v: v.tensor_mul(rT[0:64, 0:W], xT[0:64, 0:W], xT[0:64, 0:W]),
                     waits=[u2])
        w2 = em.emit("VE", lambda v: v.tensor_mul(rT[0:64, 0:W], rT[0:64, 0:W], vT[0:64, 0:W]),
                     waits=[w1])
        w3 = em.emit("SE", lambda s: s.activation(out=rT[0:64, 0:W], in_=rT[0:64, 0:W],
                                                  func=AF.Sqrt, bias=eps_ap[0:64, 1:2]),
                     waits=[w2])
        w4 = em.emit("VE", lambda v: v.reciprocal(rT[0:64, 0:W], rT[0:64, 0:W]), waits=[w3])
        rd.append(em.dma("SY", st["scr_dram"][3:4, sl], rT[0:1, 0:W], waits=[w4]))
        last = w4
    st["prev_readers"] = rd + [last]
    return last


def stats_phase(em, nc, stack, banks, xap, ntok, st, onesb, eps_ap, want_extra=True,
                scr_dram=None, chunk_off=None):
    co = chunk_off if chunk_off is not None else [0] * 8
    assert scr_dram is not None
    with ExitStack() as ph:
        sel = sbuf(nc, ph, "sel33", [33, 128], BF16)
        z1 = memset(em, "VE", sel.ap(), 0.0)
        z2 = memset(em, "VE", sel.ap()[0:1, 0:64], 1.0, waits=[z1])
        z3 = memset(em, "VE", sel.ap()[32:33, 64:128], 1.0, waits=[z2])
        st = dict(st)
        st["scr_dram"] = scr_dram
        st["cbr"] = Ring(nc, ph, 3, (33, 512), BF16, "stcb")
        st["cbl"] = Ring(nc, ph, 3, (33, 512), BF16, "stcl")
        st["wide"] = {k: sbuf(nc, ph, "stw" + k, [64, 512], F32).ap()
                      for k in ("m", "v", "s", "x", "r", "a", "b")}
        xr = Ring(nc, ph, 3, (128, 512), F32, "stx")
        cb = Ring(nc, ph, 3, (128, 512), BF16, "stc")
        sq = Ring(nc, ph, 3, (128, 512), BF16, "stq")
        pb = banks[6]
        bw = banks[5]
        z0 = memset(em, "VE", pb.ap()[0:33, 0:512], 0.0, waits=pb.wr())
        pb.rd(z0)
        for it in range(ntok // 512):
            sl = slice(512 * it, 512 * (it + 1))
            war = pb.wr()
            mml = []
            for cch in range(8):
                xa, xw, xi = xr.next()
                lt = em.dma("SY", xa, xap[128 * cch:128 * (cch + 1),
                                          co[cch] + 512 * it:co[cch] + 512 * (it + 1)], waits=xw)
                ca, cw, ci = cb.next()
                t1 = em.emit("VE", lambda v, xa=xa, ca=ca: v.tensor_copy(ca, xa), waits=[lt] + cw)
                sa, sw, si = sq.next()
                t2 = em.emit("SE", lambda s, xa=xa, sa=sa: s.activation(out=sa, in_=xa, func=AF.Square),
                             waits=[lt] + sw)
                xr.used(xi, t1)
                xr.used(xi, t2)
                m1 = em.emit("TE", lambda t, ca=ca, cch=cch: t.matmul(
                    pb.ap()[0:1, 0:512], onesb, ca, start=(cch == 0), stop=(cch == 7)),
                    waits=[t1] + (war if cch == 0 else []))
                m2 = em.emit("TE", lambda t, sa=sa, cch=cch: t.matmul(
                    pb.ap()[32:33, 0:512], onesb, sa, start=(cch == 0), stop=(cch == 7)),
                    waits=[t2])
                cb.used(ci, m1)
                sq.used(si, m2)
                mml = [m1, m2]
            stats_rows(em, nc, st, sl, pb, eps_ap, want_extra, mml, bw, sel.ap(), z3, 512)
        em.barrier()


def norm_mm_phase(em, nc, stack, banks, xap, st_row_m, st_row_s, out_cb, it_sl, rings, bt_tok):
    raise NotImplementedError



def build_L0A(io=None, ctx=None):
    nc, stack, em, banks, ptb, standalone = _ctx(ctx)

    xfm = _io_in(io, nc, "xfm", [1024, OWN])
    wq_d = _io_in(io, nc, "l0wq", [1024, 64])
    wk_d = _io_in(io, nc, "l0wk", [1024, 64])
    wv_d = _io_in(io, nc, "l0wv", [1024, 64])
    wup_d = _io_in(io, nc, "l0wup", [4096, 1024])
    wdn_d = _io_in(io, nc, "l0wdn", [1024, 4096])
    ulmats = {fam + str(l): _io_in(io, nc, f"l0{fam}{l}", [BS[l], RL[l]])
              for fam in ("UL", "VR", "UR", "VL") for l in range(4)}

    qf_o = _io_out(io, nc, "qf", [64, OWN], BF16)
    kf_o = _io_out(io, nc, "kf", [64, OWN], BF16)
    vf_o = _io_out(io, nc, "vf", [64, OWN], BF16)
    kt_o = _io_out(io, nc, "kt", [OWN, 64], BF16)
    vt_o = _io_out(io, nc, "vt", [OWN, 64], BF16)
    aux_o = _io_out(io, nc, "aux", [4, OWN, 4, 64], BF16)
    g0_o = _io_out(io, nc, "g0", [1024, OWN], BF16)
    deltas_o = _io_out(io, nc, "deltas", [NCH, 10, 64, 64], F32)

    ones = _lc(io, em, nc, stack, "c_ones", [128, 1], want_bf=True)
    onesrow = _lc(io, em, nc, stack, "c_onesrow", [1, 128])
    ident = _lc(io, em, nc, stack, "c_ident", [128, 128], want_bf=True)
    ident2 = _lc(io, em, nc, stack, "c_ident2", [128, 64], want_bf=True)
    e2t_c = _lc(io, em, nc, stack, "c_e2t", [2, 128], want_bf=True)
    onesrow_b = sbuf(nc, stack, "onesrow_bf", [1, 128], BF16)
    em.emit("VE", lambda v: v.tensor_copy(onesrow_b.ap(), onesrow["f32"]), waits=[onesrow["ftok"]])
    em.onesrow_bf = onesrow_b.ap()
    em.bc_hi = Ring(nc, stack, 4, (1, 512), BF16, "bch")
    em.bc_lo = Ring(nc, stack, 4, (1, 512), BF16, "bcl")
    em.bc_f = Ring(nc, stack, 4, (1, 512), F32, "bcf")
    eps = sbuf(nc, stack, "epsc", [128, 4], F32)
    memset(em, "VE", eps.ap()[:, 0:1], 1e-5)
    memset(em, "VE", eps.ap()[:, 1:2], 1e-6)
    stats = {k: sbuf(nc, stack, "st_" + k, [1, OWN], F32).ap()
             for k in ("m", "rstd", "s2", "rstd2", "var")}
    stats_scr = (io["stats_scr"] if io and "stats_scr" in io
                 else dram_scr(nc, "stats_scr", [8, OWN], F32))
    em.barrier()

    stats["e2t"] = e2t_c
    stats_phase(em, nc, stack, banks, xfm, OWN, stats, ones["bf"], eps.ap(), want_extra=True,
                scr_dram=stats_scr)

    xp_toks, xp = build_xp(em, nc, stack, banks, ulmats)
    em.barrier()

    qkv_phase(em, nc, stack, banks, ptb, xfm, stats_scr, wq_d, wk_d, wv_d,
              qf_o, kf_o, vf_o, kt_o, vt_o, ones, onesrow, ident, ident2)
    em.barrier()

    aux_delta_phase(em, nc, stack, banks, xp, kf_o, vf_o, kt_o, vt_o, aux_o, deltas_o, ones)
    em.barrier()

    ffn_ternary(em, nc, stack, banks, ptb, xfm, stats_scr, wup_d, wdn_d, g0_o, ones, onesrow, ident)
    em.barrier()
    if standalone:
        em.replay()
        stack.close()
        return nc
    return None


def build_xp(em, nc, stack, banks, ulmats):
    xp = sbuf(nc, stack, "xp", [64, 16 * 64], BF16)
    mset = memset(em, "GE", xp.ap(), 0.0)
    smalb = sbuf(nc, stack, "smalb", [32, 32 * 16], BF16)
    tr_toks = {}
    off = 0
    for fam in ("UL", "VR", "UR", "VL"):
        for l in range(4):
            d = ulmats[fam + str(l)]
            tok = em.dma("GE", smalb.ap()[0:RL[l], off:off + BS[l]],
                         d.rearrange("b r -> r b"), allow_slow_non_contiguous=True)
            tr_toks[(fam, l)] = (tok, off)
            off += 32
    prod = sbuf(nc, stack, "prod", [32, 4 * 64], F32)
    pb = banks[0]
    place_toks = []
    last_place = {k: [] for k in range(4)}
    for l in range(4):
        for kidx, (famA, famB) in enumerate((("UL", "UL"), ("VR", "VR"), ("UL", "VL"), ("VR", "UR"))):
            ta, offa = tr_toks[(famA, l)]
            tb, offb = tr_toks[(famB, l)]
            war = pb.wr()
            mm = em.emit("TE", lambda t, offa=offa, offb=offb, l=l, kidx=kidx: t.matmul(
                pb.ap()[0:BS[l], 64 * kidx:64 * kidx + BS[l]],
                smalb.ap()[0:RL[l], offa:offa + BS[l]],
                smalb.ap()[0:RL[l], offb:offb + BS[l]], start=True, stop=True),
                waits=[ta, tb] + war)
            ev = em.emit("VE", lambda v, l=l, kidx=kidx: v.tensor_copy(
                prod.ap()[0:BS[l], 64 * kidx:64 * kidx + BS[l]],
                pb.ap()[0:BS[l], 64 * kidx:64 * kidx + BS[l]]),
                waits=[mm] + last_place[kidx])
            pb.rd(ev)
            last_place[kidx] = []
            xpoff = 64 * (kidx * 4 + l)
            for n in range(NN[l]):
                t0 = n * 2 * BS[l]
                o0 = t0 if kidx in (0, 2) else t0 + BS[l]
                tokp = em.dma("GE", xp.ap()[o0:o0 + BS[l], xpoff + o0:xpoff + o0 + BS[l]],
                              prod.ap()[0:BS[l], 64 * kidx:64 * kidx + BS[l]],
                              waits=[ev, mset])
                place_toks.append(tokp)
                last_place[kidx].append(tokp)
    return place_toks, xp


def qkv_phase(em, nc, stack, banks, ptb, xfm, st, wq_d, wk_d, wv_d,
              qf_o, kf_o, vf_o, kt_o, vt_o, ones, onesrow, ident, ident2):
    with ExitStack() as ph:
        wqk = sbuf(nc, ph, "wqk", [128, 8 * 128], BF16)
        wvv = sbuf(nc, ph, "wvv", [128, 8 * 64], BF16)
        wtoks = []
        for cch in range(8):
            wtoks.append(em.dma("GE", wqk.ap()[:, 128 * cch:128 * cch + 64],
                                wq_d[128 * cch:128 * (cch + 1), :]))
            wtoks.append(em.dma("GE", wqk.ap()[:, 128 * cch + 64:128 * cch + 128],
                                wk_d[128 * cch:128 * (cch + 1), :]))
            wtoks.append(em.dma("GE", wvv.ap()[:, 64 * cch:64 * (cch + 1)],
                                wv_d[128 * cch:128 * (cch + 1), :]))
        xr = Ring(nc, ph, 3, (128, 512), F32, "qx")
        xnr = Ring(nc, ph, 3, (128, 512), BF16, "qxn")
        fr = Ring(nc, ph, 3, (128, 512), F32, "qft")
        er = Ring(nc, ph, 4, (128, 512), F32, "qel")
        ebr = Ring(nc, ph, 3, (128, 512), BF16, "qeb")
        tmr = Ring(nc, ph, 4, (128, 64), BF16, "qtm")
        bQK, bV = banks[2], banks[3]
        mB = sbuf(nc, ph, "qmB", [128, OWN], F32)
        rB = sbuf(nc, ph, "qrB", [128, OWN], F32)
        btm = bcast_dma(em, mB.ap(), st[0:1, :], n=OWN)
        btr = bcast_dma(em, rB.ap(), st[1:2, :], n=OWN)
        for it in range(NT):
            sl = slice(512 * it, 512 * (it + 1))
            warqk = bQK.wr()
            warv = bV.wr()
            mmq = mmv = None
            for cch in range(8):
                xa, xw, xi = xr.next()
                lt = em.dma("SY", xa, xfm[128 * cch:128 * (cch + 1), sl], waits=xw)
                fa, fw, fi = fr.next()
                t1 = em.emit("VE", lambda v, fa=fa, xa=xa, sl=sl: v.tensor_sub(fa, xa, mB.ap()[:, sl]),
                             waits=[lt, btm] + fw)
                na, nw, ni = xnr.next()
                t2 = em.emit("VE", lambda v, na=na, fa=fa, sl=sl: v.tensor_mul(na, fa, rB.ap()[:, sl]),
                             waits=[t1, btr] + nw)
                xr.used(xi, t1)
                fr.used(fi, t2)
                mmq = em.emit("TE", lambda t, cch=cch, na=na: t.matmul(
                    bQK.ap()[0:128, 0:512], wqk.ap()[:, 128 * cch:128 * (cch + 1)],
                    na, start=(cch == 0), stop=(cch == 7)),
                    waits=[t2] + wtoks + (warqk if cch == 0 else []))
                mmv = em.emit("TE", lambda t, cch=cch, na=na: t.matmul(
                    bV.ap()[0:64, 0:512], wvv.ap()[:, 64 * cch:64 * (cch + 1)],
                    na, start=(cch == 0), stop=(cch == 7)),
                    waits=[t2] + (warv if cch == 0 else []))
                xnr.used(ni, mmq)
                xnr.used(ni, mmv)
            ea, ew, ei = er.next()
            t1 = em.emit("VE", lambda v, ea=ea: v.tensor_scalar_min(ea, bQK.ap()[0:128, 0:512], 0.0),
                         waits=[mmq] + ew)
            e2a, ew2, ei2 = er.next()
            t1b = em.emit("SE", lambda s, ea=ea, e2a=e2a: s.activation(out=e2a, in_=ea, func=AF.Exp),
                          waits=[t1] + ew2)
            fa2, fw2, fi2 = er.next()
            t2 = em.emit("VE", lambda v, fa2=fa2: v.tensor_scalar_max(fa2, bQK.ap()[0:128, 0:512], 0.0),
                         waits=[mmq] + fw2)
            bQK.rd(t1)
            bQK.rd(t2)
            qa, qw, qi = ebr.next()
            t3 = em.emit("VE", lambda v, qa=qa, e2a=e2a, fa2=fa2: v.tensor_add(qa, e2a, fa2),
                         waits=[t1b, t2] + qw)
            er.used(ei, t1b)
            er.used(ei2, t3)
            er.used(fi2, t3)
            s1 = em.dma("SY", qf_o[:, sl], qa[0:64, :], waits=[t3])
            s2 = em.dma("SY", kf_o[:, sl], qa[64:128, :], waits=[t3])
            ebr.used(qi, s1)
            ebr.used(qi, s2)
            va, vw, vi = ebr.next()
            t4 = em.emit("VE", lambda v, va=va: v.tensor_copy(va[0:64, :], bV.ap()[0:64, 0:512]),
                         waits=[mmv] + vw)
            bV.rd(t4)
            s3 = em.dma("SY", vf_o[:, sl], va[0:64, :], waits=[t4])
            ebr.used(vi, s3)
            for ck in range(4):
                gt = 512 * it + 128 * ck
                wtr = ptb.wr()
                trk = em.emit("TE", lambda t, qa=qa, ck=ck: t.transpose(
                    ptb.ap()[0:128, 0:64], qa[64:128, 128 * ck:128 * (ck + 1)],
                    ident2["bf"][64:128, 0:64]), waits=[t3] + wtr)
                ka, kw, ki = tmr.next()
                ev1 = em.emit("VE", lambda v, ka=ka: v.tensor_copy(ka, ptb.ap()[0:128, 0:64]),
                              waits=[trk] + kw)
                ptb.rd(ev1)
                sk = em.dma("SY", kt_o[gt:gt + 128, :], ka, waits=[ev1])
                tmr.used(ki, sk)
                trv = em.emit("TE", lambda t, va=va, ck=ck: t.transpose(
                    ptb.ap()[0:128, 64:128], va[0:64, 128 * ck:128 * (ck + 1)],
                    ident["bf"][0:64, 0:64]), waits=[t4, ev1])
                va2, vw2, vi2 = tmr.next()
                ev2 = em.emit("VE", lambda v, va2=va2: v.tensor_copy(va2, ptb.ap()[0:128, 64:128]),
                              waits=[trv] + vw2)
                ptb.rd(ev2)
                sv = em.dma("SY", vt_o[gt:gt + 128, :], va2, waits=[ev2])
                tmr.used(vi2, sv)
                ebr.used(qi, trk)
                ebr.used(vi, trv)


def aux_delta_phase(em, nc, stack, banks, xp, kf_o, vf_o, kt_o, vt_o, aux_o, deltas_o, ones):
    with ExitStack() as ph:
        zpad = sbuf(nc, ph, "zpad", [63, 64], F32)
        zpad_tok = memset(em, "VE", zpad.ap(), 0.0)
        kfr = Ring(nc, ph, 3, (64, 128), BF16, "akf")
        vfr = Ring(nc, ph, 3, (64, 128), BF16, "avf")
        ktr = Ring(nc, ph, 3, (128, 64), BF16, "akt")
        vtr = Ring(nc, ph, 3, (128, 64), BF16, "avt")
        auxtr = Ring(nc, ph, 10, (128, 256), BF16, "aaux")
        d1r = Ring(nc, ph, 3, (64, 320), F32, "adl1")
        d2r = Ring(nc, ph, 3, (64, 256), F32, "adl2")
        zr = Ring(nc, ph, 3, (1, 64), F32, "adlz")
        bsets = [(banks[0], banks[1]), (banks[4], banks[5])]
        bD1, bD2 = banks[2], banks[3]
        state = {}

        def stage_p(ch):
            gt = 128 * ch
            kfa, kfw, kfi = kfr.next()
            lt1 = em.dma("SY", kfa, kf_o[:, gt:gt + 128], waits=kfw)
            vfa, vfw, vfi = vfr.next()
            lt2 = em.dma("SY", vfa, vf_o[:, gt:gt + 128], waits=vfw)
            kta, ktw, kti = ktr.next()
            lt3 = em.dma("SY", kta, kt_o[gt:gt + 128, :], waits=ktw)
            vta, vtw, vti = vtr.next()
            lt4 = em.dma("SY", vta, vt_o[gt:gt + 128, :], waits=vtw)
            bs = bsets[ch % 2]
            mms = []
            for kind in range(4):
                srct, srctok = (kfa, lt1) if kind in (0, 2) else (vfa, lt2)
                bank = bs[kind // 2]
                poff = 256 * (kind % 2)
                war = bank.wr() if kind < 2 else []
                mms.append(em.emit("TE", lambda t, srct=srct, kind=kind, poff=poff, bank=bank: t.matmul(
                    bank.ap()[0:128, poff:poff + 256], srct,
                    xp.ap()[0:64, 256 * kind:256 * (kind + 1)],
                    start=True, stop=True), waits=[srctok] + war))
            state[ch] = dict(mms=mms, kta=kta, kti=kti, vta=vta, vti=vti,
                             lt3=lt3, lt4=lt4, kfi=kfi, vfi=vfi, bs=bs)

        def stage_d(ch):
            st = state.pop(ch)
            gt = 128 * ch
            bs = st["bs"]
            tiles = {}
            for kind in range(4):
                bank = bs[kind // 2]
                poff = 256 * (kind % 2)
                aa, aw, ai = auxtr.next()
                ev = em.emit("VE", lambda v, aa=aa, poff=poff, bank=bank: v.tensor_copy(
                    aa, bank.ap()[0:128, poff:poff + 256]), waits=[st["mms"][-1]] + aw)
                bank.rd(ev)
                stA = em.dma("SY", aux_o[kind, gt:gt + 128].rearrange("p l c -> p (l c)"),
                             aa, waits=[ev])
                auxtr.used(ai, stA)
                tiles[kind] = (aa, ev, ai)
            war1 = bD1.wr()
            war2 = bD2.wr()
            mmz = em.emit("TE", lambda t, kta=st["kta"]: t.matmul(
                bD1.ap()[0:1, 320:384], ones["bf"], kta, start=True, stop=True),
                waits=[st["lt3"]] + war1)
            mm0 = em.emit("TE", lambda t, kta=st["kta"], vta=st["vta"]: t.matmul(
                bD1.ap()[0:64, 0:64], vta, kta, start=True, stop=True),
                waits=[st["lt3"], st["lt4"]])
            last1 = mm0
            for l in range(4):
                vva, vvev, vvi = tiles[1]
                kua, kuev, kui = tiles[0]
                last1 = em.emit("TE", lambda t, vva=vva, kua=kua, l=l: t.matmul(
                    bD1.ap()[0:64, 64 + 64 * l:128 + 64 * l],
                    vva[:, 64 * l:64 * (l + 1)], kua[:, 64 * l:64 * (l + 1)],
                    start=True, stop=True), waits=[vvev, kuev])
                auxtr.used(vvi, last1)
                auxtr.used(kui, last1)
            last2 = None
            for l in range(4):
                kva, kvev, kvi = tiles[2]
                vua, vuev, vui = tiles[3]
                last2 = em.emit("TE", lambda t, kva=kva, vua=vua, l=l: t.matmul(
                    bD2.ap()[0:64, 64 * l:64 * (l + 1)],
                    kva[:, 64 * l:64 * (l + 1)], vua[:, 64 * l:64 * (l + 1)],
                    start=True, stop=True), waits=[kvev, vuev] + (war2 if l == 0 else []))
                auxtr.used(kvi, last2)
                auxtr.used(vui, last2)
            d1a, d1w, d1i = d1r.next()
            ev1 = em.emit("VE", lambda v, d1a=d1a: v.tensor_copy(d1a, bD1.ap()[0:64, 0:320]),
                          waits=[last1, mmz] + d1w)
            bD1.rd(ev1)
            st1 = None
            for s_ in range(5):
                st1 = em.dma("SY", deltas_o[ch, s_], d1a[:, 64 * s_:64 * (s_ + 1)], waits=[ev1])
                d1r.used(d1i, st1)
            za, zw, zi = zr.next()
            evz = em.emit("VE", lambda v, za=za: v.tensor_copy(za, bD1.ap()[0:1, 320:384]),
                          waits=[mmz, last1] + zw)
            bD1.rd(evz)
            stz = em.dma("SY", deltas_o[ch, 9, 0:1, :], za, waits=[evz])
            zr.used(zi, stz)
            em.dma("SY", deltas_o[ch, 9, 1:64, :], zpad.ap(), waits=[zpad_tok])
            d2a, d2w, d2i = d2r.next()
            ev2 = em.emit("VE", lambda v, d2a=d2a: v.tensor_copy(d2a, bD2.ap()[0:64, 0:256]),
                          waits=[last2] + d2w)
            bD2.rd(ev2)
            for s_ in range(4):
                st2 = em.dma("SY", deltas_o[ch, 5 + s_], d2a[:, 64 * s_:64 * (s_ + 1)], waits=[ev2])
                d2r.used(d2i, st2)
            ktr.used(st["kti"], mm0)
            ktr.used(st["kti"], mmz)
            vtr.used(st["vti"], mm0)
            kfr.used(st["kfi"], st["mms"][-1])
            vfr.used(st["vfi"], st["mms"][-1])

        for ch in range(NCH):
            stage_p(ch)
            if ch >= 1:
                stage_d(ch - 1)
        stage_d(NCH - 1)


def ternary_prep(em, nc, wtT_dram, banks, ptb, wd, FDIM, DDIM, ones, onesrow, ident, name):
    nfch = FDIM // 128
    DW = min(DDIM, 1024)
    ncs = DDIM // DW
    ph = ExitStack()
    wr = Ring(nc, ph, 3, (128, DW), F32, name + "wf")
    mr = Ring(nc, ph, 2, (128, DW), F32, name + "msk")
    sgr = Ring(nc, ph, 2, (128, DW), F32, name + "sgn")
    wtb = Ring(nc, ph, 3, (128, DW), BF16, name + "wtb")
    tcol = Ring(nc, ph, 2, (128, 1), F32, name + "tc")
    acc = sbuf(nc, ph, name + "acc", [128, 1], F32)
    alpha = sbuf(nc, ph, name + "al", [128, 1], F32)
    z0 = memset(em, "VE", acc.ap(), 0.0)
    atoks = [z0]
    for fc in range(nfch):
        for cs in range(ncs):
            wa, ww, wis = wr.next()
            lt = em.dma("SY", wa, wd[128 * fc:128 * (fc + 1), DW * cs:DW * (cs + 1)], waits=ww)
            ta, tw, ti = tcol.next()
            rt = em.emit("VE", lambda v, wa=wa, ta=ta: v.tensor_reduce(
                ta, wa, AX.X, ALU.add, apply_absolute_value=True), waits=[lt] + tw)
            ad = em.emit("VE", lambda v, ta=ta: v.tensor_add(acc.ap(), acc.ap(), ta), waits=[rt] + atoks)
            tcol.used(ti, ad)
            wr.used(wis, rt)
            atoks = [ad]
    pb = banks[0]
    war = pb.wr()
    spl = sbuf(nc, ph, name + "spl", [128, 4], BF16)
    spf = sbuf(nc, ph, name + "spf", [128, 2], F32)
    s1 = em.emit("VE", lambda v: v.tensor_copy(spl.ap()[:, 0:1], acc.ap()), waits=atoks)
    s2 = em.emit("VE", lambda v: v.tensor_sub(spf.ap()[:, 0:1], acc.ap(), spl.ap()[:, 0:1]), waits=[s1])
    s3 = em.emit("VE", lambda v: v.tensor_copy(spl.ap()[:, 1:2], spf.ap()[:, 0:1]), waits=[s2])
    s4 = em.emit("VE", lambda v: v.tensor_sub(spf.ap()[:, 1:2], spf.ap()[:, 0:1], spl.ap()[:, 1:2]),
                 waits=[s3])
    s5 = em.emit("VE", lambda v: v.tensor_copy(spl.ap()[:, 2:3], spf.ap()[:, 1:2]), waits=[s4])
    mm = None
    for j, stok in ((0, s1), (1, s3), (2, s5)):
        mm = em.emit("TE", lambda t, j=j: t.matmul(pb.ap()[0:1, 0:1], ones["bf"], spl.ap()[:, j:j + 1],
                                                   start=(j == 0), stop=(j == 2)),
                     waits=[stok] + (war if j == 0 else []))
    th = em.emit("VE", lambda v: v.tensor_scalar_mul(alpha.ap()[0:1, :], pb.ap()[0:1, 0:1],
                                                     0.5 / (FDIM * DDIM)), waits=[mm])
    pb.rd(th)
    war = pb.wr()
    b1 = em.emit("VE", lambda v: v.tensor_copy(spl.ap()[0:1, 3:4], alpha.ap()[0:1, 0:1]), waits=[th])
    b2 = em.emit("VE", lambda v: v.tensor_sub(spf.ap()[0:1, 1:2], alpha.ap()[0:1, 0:1],
                                              spl.ap()[0:1, 3:4]), waits=[b1])
    b3 = em.emit("VE", lambda v: v.tensor_copy(spl.ap()[0:1, 2:3], spf.ap()[0:1, 1:2]), waits=[b2])
    ob = em.onesrow_bf
    mb1 = em.emit("TE", lambda t, ob=ob: t.matmul(pb.ap()[0:128, 4:5], ob, spl.ap()[0:1, 3:4],
                                           start=True, stop=False), waits=[b1] + war)
    mb2 = em.emit("TE", lambda t, ob=ob: t.matmul(pb.ap()[0:128, 4:5], ob, spl.ap()[0:1, 2:3],
                                           start=False, stop=True), waits=[b3])
    thr_ev = em.emit("VE", lambda v: v.tensor_copy(alpha.ap(), pb.ap()[0:128, 4:5]), waits=[mb2])
    pb.rd(thr_ev)
    for fc in range(nfch):
        for cs in range(ncs):
            wa, ww, wis = wr.next()
            lt = em.dma("SY", wa, wd[128 * fc:128 * (fc + 1), DW * cs:DW * (cs + 1)], waits=ww)
            ma, mw, mi = mr.next()
            t1 = em.emit("SE", lambda s, wa=wa, ma=ma: s.activation(out=ma, in_=wa, func=AF.Abs),
                         waits=[lt] + mw)
            sa, sw, si = sgr.next()
            t3 = em.emit("SE", lambda s, wa=wa, sa=sa: s.activation(out=sa, in_=wa, func=AF.Sign),
                         waits=[lt] + sw)
            wr.used(wis, t1)
            wr.used(wis, t3)
            ba, bw, bi = wtb.next()
            t4 = em.emit("VE", lambda v, ba=ba, ma=ma, sa=sa: v.scalar_tensor_tensor(
                ba, ma, alpha.ap()[:, 0:1], sa, ALU.is_gt, ALU.mult),
                waits=[t1, t3, thr_ev] + bw)
            mr.used(mi, t4)
            sgr.used(si, t4)
            wtr = ptb.wr()
            tts = []
            for j in range(DW // 128):
                tts.append(em.emit("TE", lambda t, ba=ba, j=j: t.transpose(
                    ptb.ap()[0:128, 128 * j:128 * (j + 1)], ba[:, 128 * j:128 * (j + 1)],
                    ident["bf"]), waits=[t4] + (wtr if j == 0 else [])))
            ta5, tw5, ti5 = wtb.next()
            ev = em.emit("VE", lambda v, ta5=ta5: v.tensor_copy(ta5, ptb.ap()[0:128, 0:DW]),
                         waits=tts + tw5)
            ptb.rd(ev)
            for j in range(DW // 128):
                dc = cs * (DW // 128) + j
                st5 = em.dma("SY", wtT_dram[:, FDIM * dc + 128 * fc:FDIM * dc + 128 * (fc + 1)],
                             ta5[:, 128 * j:128 * (j + 1)], waits=[ev])
                wtb.used(ti5, st5)
            wtb.used(bi, tts[-1])
    em.barrier()
    ph.close()


def ffn_ternary(em, nc, stack, banks, ptb, xfm, st, wup_d, wdn_d, g0_o, ones, onesrow, ident):
    hkind = "ExternalOutput" if getattr(ffn_ternary, "_dbghst", False) else "Internal"
    h_d = nc.dram_tensor("h_scr", [4096, OWN], BF16, kind=hkind).ap()
    if getattr(ffn_ternary, "_dbghst", False):
        ffn_ternary._xno = nc.dram_tensor("xn_dbg_o", [1024, 512], BF16, kind="ExternalOutput").ap()
        ffn_ternary._bco = nc.dram_tensor("bcb_dbg", [4, 512], F32, kind="ExternalOutput").ap()
    hst_a = sbuf(nc, stack, "hst_a", [1, OWN], F32)
    hst_b = sbuf(nc, stack, "hst_b", [1, OWN], F32)
    hst_scr = dram_scr(nc, "hst_scr", [1, OWN], F32)

    wtup_d = dram_scr(nc, "wtup_scr", [128, 8 * 4096], BF16)
    wtdn_d = dram_scr(nc, "wtdn_scr", [128, 32 * 1024], BF16)
    ternary_prep(em, nc, wtup_d, banks, ptb, wup_d, 4096, 1024, ones, onesrow, ident, "wtu")
    ternary_prep(em, nc, wtdn_d, banks, ptb, wdn_d, 1024, 4096, ones, onesrow, ident, "wtd")

    with ExitStack() as ph:
        wtup = sbuf(nc, ph, "wtu", [128, 8 * 4096], BF16)
        for dc in range(8):
            em.dma("SY", wtup.ap()[:, 4096 * dc:4096 * (dc + 1)], wtup_d[:, 4096 * dc:4096 * (dc + 1)])
        em.barrier()
        xr = Ring(nc, ph, 3, (128, 512), F32, "ux")
        fr = Ring(nc, ph, 3, (128, 512), F32, "uf")
        xnr = Ring(nc, ph, 10, (128, 512), BF16, "uxn")
        hr = Ring(nc, ph, 3, (128, 512), BF16, "uh")
        sqr = Ring(nc, ph, 3, (128, 512), BF16, "usq")
        bH0, bH1, bS = banks[2], banks[3], banks[4]
        mB = sbuf(nc, ph, "umB", [128, OWN], F32)
        sB = sbuf(nc, ph, "usB", [128, OWN], F32)
        btm = bcast_dma(em, mB.ap(), st[0:1, :], n=OWN)
        bts = bcast_dma(em, sB.ap(), st[2:3, :], n=OWN)
        for it in range(NT):
            sl = slice(512 * it, 512 * (it + 1))
            xn_list = []
            for cch in range(8):
                xa, xw, xi = xr.next()
                lt = em.dma("SY", xa, xfm[128 * cch:128 * (cch + 1), sl], waits=xw)
                fa, fw, fi = fr.next()
                t1 = em.emit("VE", lambda v, fa=fa, xa=xa, sl=sl: v.tensor_sub(fa, xa, mB.ap()[:, sl]),
                             waits=[lt, btm] + fw)
                na, nw, ni = xnr.next()
                t2 = em.emit("VE", lambda v, na=na, fa=fa, sl=sl: v.tensor_mul(na, fa, sB.ap()[:, sl]),
                             waits=[t1, bts] + nw)
                xr.used(xi, t1)
                fr.used(fi, t2)
                xn_list.append((na, t2, ni))
                if getattr(ffn_ternary, "_dbghst", False) and it == 0:
                    dtok = em.dma("SY", ffn_ternary._xno[128 * cch:128 * (cch + 1), :], na, waits=[t2])
                    xnr.used(ni, dtok)
            if getattr(ffn_ternary, "_dbghst", False) and it == 0:
                em.dma("SY", ffn_ternary._bco[0:1, :], mB.ap()[0:1, 0:512], waits=[btm])
                em.dma("SY", ffn_ternary._bco[1:2, :], sB.ap()[0:1, 0:512], waits=[bts])
                em.dma("SY", ffn_ternary._bco[2:3, :], sB.ap()[64:65, 0:512], waits=[bts])
            last_hm = None
            for fc in range(32):
                hb = bH0 if fc % 2 == 0 else bH1
                war = hb.wr()
                mml = None
                for cch in range(8):
                    na, t2, ni = xn_list[cch]
                    mml = em.emit("TE", lambda t, fc=fc, cch=cch, na=na, hb=hb: t.matmul(
                        hb.ap()[0:128, 0:512], wtup.ap()[:, 4096 * cch + 128 * fc:4096 * cch + 128 * (fc + 1)],
                        na, start=(cch == 0), stop=(cch == 7)),
                        waits=[t2] + (war if cch == 0 else []))
                ha, hw, hi = hr.next()
                ge = em.emit("SE", lambda s, ha=ha, hb=hb: s.activation(
                    out=ha, in_=hb.ap()[0:128, 0:512], func=AF.Gelu), waits=[mml] + hw)
                hb.rd(ge)
                sto = em.dma("SY", h_d[128 * fc:128 * (fc + 1), sl], ha, waits=[ge])
                qa, qw, qi = sqr.next()
                sq = em.emit("SE", lambda s, qa=qa, ha=ha: s.activation(out=qa, in_=ha, func=AF.Square),
                             waits=[ge] + qw)
                warS = bS.wr() if fc == 0 else []
                hm = em.emit("TE", lambda t, qa=qa, fc=fc: t.matmul(
                    bS.ap()[32 * (fc % 3):32 * (fc % 3) + 1, 0:512], ones["bf"], qa,
                    start=(fc < 3), stop=(fc >= 29)),
                    waits=[sq] + warS)
                sqr.used(qi, hm)
                hr.used(hi, sto)
                hr.used(hi, sq)
                last_hm = hm
            for na, t2, ni in xn_list:
                xnr.used(ni, last_hm)
            ta1 = em.emit("VE", lambda v, sl=sl: v.tensor_copy(hst_a.ap()[0:1, sl], bS.ap()[0:1, 0:512]),
                          waits=[last_hm])
            ta2 = em.emit("VE", lambda v, sl=sl: v.tensor_add(hst_a.ap()[0:1, sl], hst_a.ap()[0:1, sl],
                                                              bS.ap()[32:33, 0:512]), waits=[ta1])
            ta3 = em.emit("VE", lambda v, sl=sl: v.tensor_add(hst_a.ap()[0:1, sl], hst_a.ap()[0:1, sl],
                                                              bS.ap()[64:65, 0:512]), waits=[ta2])
            bS.rd(ta1)
            bS.rd(ta2)
            bS.rd(ta3)
            t5 = em.emit("SE", lambda s, sl=sl: s.activation(out=hst_a.ap()[0:1, sl],
                                                             in_=hst_a.ap()[0:1, sl], func=AF.Sqrt),
                         waits=[ta3])
            t6 = em.emit("VE", lambda v, sl=sl: v.tensor_scalar(
                hst_a.ap()[0:1, sl], hst_a.ap()[0:1, sl], 1.0 / 64.0, 1e-8, ALU.mult, ALU.add), waits=[t5])
            t7 = em.emit("VE", lambda v, sl=sl: v.reciprocal(hst_b.ap()[0:1, sl], hst_a.ap()[0:1, sl]),
                         waits=[t6])
            em.dma("SY", hst_scr[0:1, sl], hst_b.ap()[0:1, sl], waits=[t7])
        em.barrier()

    with ExitStack() as ph:
        wtdn = sbuf(nc, ph, "wtd", [128, 32 * 1024], BF16)
        for dc in range(32):
            em.dma("SY", wtdn.ap()[:, 1024 * dc:1024 * (dc + 1)], wtdn_d[:, 1024 * dc:1024 * (dc + 1)])
        em.barrier()
        hrd = Ring(nc, ph, 34, (128, 512), BF16, "dh")
        glr = Ring(nc, ph, 3, (128, 512), BF16, "dg")
        fgr = Ring(nc, ph, 3, (128, 512), F32, "dgf")
        rbB = sbuf(nc, ph, "drbB", [128, OWN], F32)
        bG0, bG1 = banks[2], banks[3]
        trb = bcast_dma(em, rbB.ap(), hst_scr[0:1, :], n=OWN)
        for it in range(NT):
            sl = slice(512 * it, 512 * (it + 1))
            hn_list = []
            for fc in range(32):
                ha, hw, hi = hrd.next()
                lt = em.dma("SY", ha, h_d[128 * fc:128 * (fc + 1), sl], waits=hw)
                hn_list.append((ha, lt, hi))
            last = None
            for dc in range(8):
                gb = bG0 if dc % 2 == 0 else bG1
                war = gb.wr()
                mml = None
                for fc in range(32):
                    ha, t1, hi = hn_list[fc]
                    mml = em.emit("TE", lambda t, dc=dc, fc=fc, ha=ha, gb=gb: t.matmul(
                        gb.ap()[0:128, 0:512], wtdn.ap()[:, 1024 * fc + 128 * dc:1024 * fc + 128 * (dc + 1)],
                        ha, start=(fc == 0), stop=(fc == 31)),
                        waits=[t1] + (war if fc == 0 else []))
                fa2, fw2, fi2 = fgr.next()
                tm2 = em.emit("VE", lambda v, fa2=fa2, gb=gb, sl=sl: v.tensor_mul(
                    fa2, gb.ap()[0:128, 0:512], rbB.ap()[:, sl]), waits=[mml, trb] + fw2)
                gb.rd(tm2)
                ga, gw, gi = glr.next()
                si = em.emit("SE", lambda s, ga=ga, fa2=fa2: s.activation(out=ga, in_=fa2, func=AF.Silu),
                             waits=[tm2] + gw)
                fgr.used(fi2, si)
                sto = em.dma("SY", g0_o[128 * dc:128 * (dc + 1), sl], ga, waits=[si])
                glr.used(gi, sto)
                last = sto
            for ha, t1, hi in hn_list:
                hrd.used(hi, last)




def build_L0B(nch=NCH, upto=99, io=None, ctx=None):
    nc, stack, em, banks, ptb, standalone = _ctx(ctx)

    xfm = _io_in(io, nc, "xfm", [1024, OWN])
    qf_d = _io_in(io, nc, "qf", [64, OWN], BF16)
    kf_d = _io_in(io, nc, "kf", [64, OWN], BF16)
    vf_d = _io_in(io, nc, "vf", [64, OWN], BF16)
    kt_d = _io_in(io, nc, "kt", [OWN, 64], BF16)
    vt_d = _io_in(io, nc, "vt", [OWN, 64], BF16)
    aux_d = _io_in(io, nc, "aux", [4, OWN, 4, 64], BF16)
    g0_d = _io_in(io, nc, "g0", [1024, OWN], BF16)
    dprev_d = _io_in(io, nc, "dprev", [NCH, 10, 64, 64], F32)
    wo_d = _io_in(io, nc, "l0wo", [64, 1024])
    ulmats = {fam + str(l): _io_in(io, nc, f"l0{fam}{l}", [BS[l], RL[l]])
              for fam in ("UL", "VR", "UR", "VL") for l in range(4)}
    x1_o = _io_out(io, nc, "x1", [1024, OWN], F32)

    ones = _lc(io, em, nc, stack, "c_ones", [128, 1], want_bf=True)
    onesrow = _lc(io, em, nc, stack, "c_onesrow", [1, 128])
    ident = _lc(io, em, nc, stack, "c_ident", [128, 128], want_bf=True)
    triu = _lc(io, em, nc, stack, "c_triu", [128, 128], want_bf=True)
    masks = _lc(io, em, nc, stack, "c_l0masks", [64, 9 * 64])
    onesrow_b = sbuf(nc, stack, "onesrow_bf", [1, 128], BF16)
    em.emit("VE", lambda v: v.tensor_copy(onesrow_b.ap(), onesrow["f32"]), waits=[onesrow["ftok"]])
    em.onesrow_bf = onesrow_b.ap()
    em.bc_hi = Ring(nc, stack, 4, (1, 512), BF16, "bch")
    em.bc_lo = Ring(nc, stack, 4, (1, 512), BF16, "bcl")
    em.bc_f = Ring(nc, stack, 4, (1, 512), F32, "bcf")
    em.barrier()

    NT_L = [ (NN[l] + 2) // 3 for l in range(4) ]
    LOFF = [0]
    for l in range(4):
        LOFF.append(LOFF[-1] + NT_L[l])
    FAMT = LOFF[-1]
    xpp = sbuf(nc, stack, "xpp", [64, 3 * FAMT * 128], BF16)
    mset = memset(em, "GE", xpp.ap(), 0.0)
    smal = sbuf(nc, stack, "smalB", [32, 16 * 32], BF16)
    ltoks = {}
    off = 0
    for fam in ("VR", "UL", "VL"):
        for l in range(4):
            d = ulmats[fam + str(l)]
            tok = em.dma("GE", smal.ap()[0:BS[l], off:off + RL[l]], d)
            ltoks[(fam, l)] = (tok, off)
            off += 32
    ptoks = []
    for fi, fam in enumerate(("VR", "UL", "VL")):
        for l in range(4):
            tok, soff = ltoks[(fam, l)]
            for n in range(NN[l]):
                tile = n // 3
                slot = 32 * (n % 3)
                base = (fi * FAMT + LOFF[l] + tile) * 128
                t0 = n * 2 * BS[l]
                o0 = t0 + BS[l] if fam == "VR" else t0
                ptoks.append(em.dma("GE", xpp.ap()[o0:o0 + BS[l], base + slot:base + slot + RL[l]],
                                    smal.ap()[0:BS[l], soff:soff + RL[l]], waits=[tok, mset]))
    leafp = sbuf(nc, stack, "leafp", [64, 6 * 128], BF16)
    mset2 = memset(em, "GE", leafp.ap(), 0.0)
    for leaf in range(16):
        tile = leaf // 3
        slot = 32 * (leaf % 3)
        ptoks.append(em.dma("GE", leafp.ap()[4 * leaf:4 * leaf + 4, 128 * tile + slot:128 * tile + slot + 4],
                            ident["f32"][0:4, 0:4], waits=[ident["ftok"], mset2]))
    em.barrier()

    P_f = sbuf(nc, stack, "P_f", [64, 64], F32)
    z_f = sbuf(nc, stack, "z_f", [64, 1], F32)
    P_b = sbuf(nc, stack, "P_b", [64, 64], BF16)
    z_b = sbuf(nc, stack, "z_b", [64, 1], BF16)
    with ExitStack() as ph:
        dsum = sbuf(nc, ph, "dsum", [64, 640], F32)
        dld = Ring(nc, ph, 6, (64, 640), F32, "dld")
        d8 = io.get("dprev8") if io else None
        if d8 is not None:
            ap8, g8a = d8
            acc = memset(em, "VE", dsum.ap(), 0.0)
            for j in range(8):
                for ch in range(NCH):
                    ap3 = ap8[ch // 4, j].rearrange("c s r k -> c r s k")
                    aa, ww, ii = dld.next()
                    lt = em.dma("SY", aa.rearrange("r (s k) -> r s k", s=10), ap3[ch % 4],
                                waits=ww)
                    t = em.emit("VE", lambda v, aa=aa, j=j, g8a=g8a: v.tensor_scalar(
                        aa, aa, g8a[0:64, j:j + 1], None, ALU.mult), waits=[lt])
                    acc = em.emit("VE", lambda v, aa=aa: v.tensor_add(dsum.ap(), dsum.ap(), aa),
                                  waits=[t, acc])
                    dld.used(ii, acc)
        else:
            ap3 = dprev_d.rearrange("c s r k -> c r s k")
            a0, w0, i0 = dld.next()
            t0 = em.dma("SY", dsum.ap().rearrange("r (s k) -> r s k", s=10), ap3[0], waits=w0)
            acc = t0
            for ch in range(1, NCH):
                aa, ww, ii = dld.next()
                lt = em.dma("SY", aa.rearrange("r (s k) -> r s k", s=10), ap3[ch], waits=ww)
                acc = em.emit("VE", lambda v, aa=aa: v.tensor_add(dsum.ap(), dsum.ap(), aa),
                              waits=[lt, acc])
                dld.used(ii, acc)
        z0m = memset(em, "VE", P_f.ap(), 0.0, waits=[acc])
        prev = z0m
        tmp = sbuf(nc, ph, "tmpm", [64, 64], F32)
        for k in range(9):
            t1 = em.emit("VE", lambda v, k=k: v.tensor_mul(tmp.ap(), dsum.ap()[:, 64 * k:64 * (k + 1)],
                                                           masks["f32"][:, 64 * k:64 * (k + 1)]),
                         waits=[acc, prev, masks["ftok"]])
            prev = em.emit("VE", lambda v: v.tensor_add(P_f.ap(), P_f.ap(), tmp.ap()), waits=[t1])
        zr = sbuf(nc, ph, "zr", [1, 64], BF16)
        tz = em.emit("VE", lambda v: v.tensor_copy(zr.ap(), dsum.ap()[0:1, 576:640]), waits=[acc])
        wtr = ptb.wr()
        ttr = em.emit("TE", lambda t: t.transpose(ptb.ap()[0:64, 0:1], zr.ap(), ident["bf"][0:1, 0:1]),
                      waits=[tz] + wtr)
        tz2 = em.emit("VE", lambda v: v.tensor_copy(z_f.ap(), ptb.ap()[0:64, 0:1]), waits=[ttr])
        ptb.rd(tz2)
        em.barrier()

    build_L0B._pbr = []
    build_L0B._zbr = []
    zer = sbuf(nc, stack, "zer", [64, 72], BF16)
    zer_tok = memset(em, "VE", zer.ap(), 0.0)
    sfm = sbuf(nc, stack, "sfm", [64, OWN], BF16)
    with ExitStack() as ph:
        qr = Ring(nc, ph, 2, (64, 128), BF16, "sq")
        kr = Ring(nc, ph, 2, (64, 128), BF16, "sk")
        vr = Ring(nc, ph, 2, (64, 128), BF16, "sv")
        ktr = Ring(nc, ph, 2, (128, 64), BF16, "skt")
        vtr = Ring(nc, ph, 2, (128, 64), BF16, "svt")
        auxr = Ring(nc, ph, 18, (128, 64), BF16, "sax")
        padr = Ring(nc, ph, 42, (128, 128), BF16, "spd")
        gramr = Ring(nc, ph, 4, (128, 128), BF16, "sgr")
        den1 = Ring(nc, ph, 2, (128, 1), F32, "sd1")
        den2 = Ring(nc, ph, 2, (128, 1), F32, "sd2")
        otm = Ring(nc, ph, 2, (128, 64), BF16, "sot")
        stm = Ring(nc, ph, 2, (64, 64), F32, "sst")
        bP0, bP1, bY, bS1, bS2, bG0, bG1 = (banks[0], banks[1], banks[2], banks[3],
                                            banks[4], banks[5], banks[6])
        for ch in range(nch):
            gt = 128 * ch
            qa, qw, qi = qr.next()
            ltq = em.dma("SY", qa, qf_d[:, gt:gt + 128], waits=qw)
            ka, kw, ki = kr.next()
            ltk = em.dma("SY", ka, kf_d[:, gt:gt + 128], waits=kw)
            va, vw, vi = vr.next()
            ltv = em.dma("SY", va, vf_d[:, gt:gt + 128], waits=vw)
            kta, ktw, kti = ktr.next()
            ltkt = em.dma("SY", kta, kt_d[gt:gt + 128, :], waits=ktw)
            vta, vtw, vti = vtr.next()
            ltvt = em.dma("SY", vta, vt_d[gt:gt + 128, :], waits=vtw)
            aux = {}
            for kind in range(4):
                for l in range(4):
                    aa, aw, ai = auxr.next()
                    lt = em.dma("SY", aa, aux_d[kind, gt:gt + 128, l, :], waits=aw)
                    aux[(kind, l)] = (aa, lt, ai)
            if upto < 1:
                continue
            pads = {}
            pad_jobs = []
            for l in range(4):
                for tile in range(NT_L[l]):
                    pad_jobs.append(("Qr", l, tile, 0, qa, ltq))
                    pad_jobs.append(("Vr", l, tile, 0, va, ltv))
                    pad_jobs.append(("Kl", l, tile, 1, ka, ltk))
                    pad_jobs.append(("Ql", l, tile, 2, qa, ltq))
            for tile in range(6):
                pad_jobs.append(("Lq", 0, tile, None, qa, ltq))
                pad_jobs.append(("Lv", 0, tile, None, va, ltv))
            for jn, job in enumerate(pad_jobs):
                namej, l, tile, fi, srca, srct = job
                bnk = bP0 if jn % 2 == 0 else bP1
                war = bnk.wr()
                if namej in ("Lq", "Lv"):
                    lhs = leafp.ap()[:, 128 * tile:128 * (tile + 1)]
                else:
                    base = (fi * FAMT + LOFF[l] + tile) * 128
                    lhs = xpp.ap()[:, base:base + 128]
                mm = em.emit("TE", lambda t, lhs=lhs, srca=srca, bnk=bnk: t.matmul(
                    bnk.ap()[0:128, 0:128], lhs, srca, start=True, stop=True),
                    waits=[srct] + war)
                pa, pw, pi = padr.next()
                ev = em.emit("VE", lambda v, pa=pa, bnk=bnk: v.tensor_copy(
                    pa, bnk.ap()[0:128, 0:128]), waits=[mm] + pw)
                bnk.rd(ev)
                pads[(namej, l, tile)] = (pa, ev, pi)
            if upto < 2:
                continue
            gram_jobs = []
            for leaf in range(16):
                tile, slot = leaf // 3, 32 * (leaf % 3)
                gram_jobs.append((pads[("Lv", 0, tile)], slot, pads[("Lq", 0, tile)], slot, 32,
                                  kta, ltkt, 4 * leaf, 4))
            for l in range(4):
                for n in range(NN[l]):
                    tile, slot = n // 3, 32 * (n % 3)
                    t0c = n * 2 * BS[l]
                    b0c = t0c + BS[l]
                    ku = aux[(0, l)]
                    vu = aux[(3, l)]
                    gram_jobs.append((pads[("Vr", l, tile)], slot, pads[("Qr", l, tile)], slot, 32,
                                      ku[0], ku[1], t0c, BS[l]))
                    gram_jobs.append((pads[("Kl", l, tile)], slot, pads[("Ql", l, tile)], slot, 32,
                                      vu[0], vu[1], b0c, BS[l]))
            warY = bY.wr()
            warY = bY.wr()
            mmZf = em.emit("TE", lambda t, qa=qa: t.matmul(bY.ap()[0:128, 0:72], qa, zer.ap(),
                                                           start=True, stop=False),
                           waits=[ltq, zer_tok] + warY)
            warY = [mmZf]
            for gj, (pA, offA, pB, offB, kk, sec, sectok, ocol, ow) in enumerate(gram_jobs):
                gbnk = bG0 if gj % 2 == 0 else bG1
                war = gbnk.wr()
                mm = em.emit("TE", lambda t, pA=pA, offA=offA, pB=pB, offB=offB, kk=kk, gbnk=gbnk: t.matmul(
                    gbnk.ap()[0:128, 0:128],
                    pA[0][offA:offA + kk, :], pB[0][offB:offB + kk, :],
                    start=True, stop=True), waits=[pA[1], pB[1]] + war)
                padr.used(pA[2], mm)
                padr.used(pB[2], mm)
                ga, gw, gi = gramr.next()
                evm = em.emit("VE", lambda v, ga=ga, gbnk=gbnk: v.tensor_tensor(
                    ga, gbnk.ap()[0:128, 0:128], triu["bf"], ALU.mult), waits=[mm] + gw)
                gbnk.rd(evm)
                mm2 = em.emit("TE", lambda t, ga=ga, sec=sec, ocol=ocol, ow=ow: t.matmul(
                    bY.ap()[0:128, ocol:ocol + ow], ga, sec[:, ocol:ocol + ow],
                    start=False, stop=False), waits=[evm, sectok] + (warY if gj == 0 else []))
                gramr.used(gi, mm2)
            if upto < 3:
                continue
            war = bG0.wr()
            mmD = em.emit("TE", lambda t, ka=ka, qa=qa: t.matmul(bG0.ap()[0:128, 0:128], ka, qa,
                                                   start=True, stop=True), waits=[ltk, ltq] + war)
            ga, gw, gi = gramr.next()
            evD = em.emit("VE", lambda v, ga=ga: v.tensor_tensor(
                ga, bG0.ap()[0:128, 0:128], triu["bf"], ALU.mult), waits=[mmD] + gw)
            bG0.rd(evD)
            mmDen = em.emit("TE", lambda t, ga=ga: t.matmul(bY.ap()[0:128, 64:65], ga, ones["bf"],
                                                            start=False, stop=False),
                            waits=[evD, ones["btok"]])
            gramr.used(gi, mmDen)
            cpP = em.emit("VE", lambda v: v.tensor_copy(P_b.ap(), P_f.ap()),
                          waits=getattr(build_L0B, "_pbr", []))
            cpz = em.emit("VE", lambda v: v.tensor_copy(z_b.ap(), z_f.ap()),
                          waits=getattr(build_L0B, "_zbr", []))
            mmP = em.emit("TE", lambda t, qa=qa: t.matmul(bY.ap()[0:128, 0:64], qa, P_b.ap(),
                                                   start=False, stop=False), waits=[ltq, cpP])
            mmz = em.emit("TE", lambda t, qa=qa: t.matmul(bY.ap()[0:128, 64:65], qa, z_b.ap(),
                                                   start=False, stop=True), waits=[cpz, ltq])
            build_L0B._pbr = [mmP]
            build_L0B._zbr = [mmz]
            if upto < 4:
                continue
            war1 = bS1.wr()
            war2 = bS2.wr()
            sjobs = [(0, vta, ltvt, kta, ltkt)]
            for l in range(4):
                vv = aux[(1, l)]
                ku0 = aux[(0, l)]
                kv = aux[(2, l)]
                vu0 = aux[(3, l)]
                sjobs.append((1 + l, vv[0], vv[1], ku0[0], ku0[1]))
                sjobs.append((5 + l, kv[0], kv[1], vu0[0], vu0[1]))
            smms = []
            for sk, lhsT_t, ltok, rhs_t, rtok in sjobs:
                bnk = bS1 if sk < 5 else bS2
                soff = 64 * (sk % 5)
                mm = em.emit("TE", lambda t, lhsT_t=lhsT_t, rhs_t=rhs_t, bnk=bnk, soff=soff: t.matmul(
                    bnk.ap()[0:64, soff:soff + 64], lhsT_t, rhs_t, start=True, stop=True),
                    waits=[ltok, rtok] + (war1 if sk == 0 else war2 if sk == 5 else []))
                smms.append((sk, mm, bnk, soff))
                auxr_used = None
            mmzs = em.emit("TE", lambda t, kta=kta: t.matmul(bS1.ap()[0:64, 448:449], kta, ones["bf"],
                                                    start=True, stop=True), waits=[ltkt, ones["btok"]])
            sadds = []
            for sk, mm, bnk, soff in smms:
                sa, sw, si = stm.next()
                t1 = em.emit("VE", lambda v, sa=sa, bnk=bnk, soff=soff, sk=sk: v.tensor_tensor(
                    sa, bnk.ap()[0:64, soff:soff + 64], masks["f32"][:, 64 * sk:64 * (sk + 1)], ALU.mult),
                    waits=[smms[-1][1], cpP] + sw)
                bnk.rd(t1)
                t2 = em.emit("VE", lambda v, sa=sa: v.tensor_add(P_f.ap(), P_f.ap(), sa),
                             waits=[t1] + sadds[-1:])
                stm.used(si, t2)
                sadds.append(t2)
            tz1 = em.emit("VE", lambda v: v.tensor_add(z_f.ap(), z_f.ap(), bS1.ap()[0:64, 448:449]),
                          waits=[mmzs, cpz])
            bS1.rd(tz1)
            if upto < 5:
                continue
            d1a, d1w, d1i = den1.next()
            td1 = em.emit("VE", lambda v, d1a=d1a: v.tensor_scalar_max(d1a, bY.ap()[0:128, 64:65], 1e-6),
                          waits=[mmz, mmDen] + d1w)
            d2a, d2w, d2i = den2.next()
            td2 = em.emit("VE", lambda v, d1a=d1a, d2a=d2a: v.reciprocal(d2a, d1a), waits=[td1] + d2w)
            den1.used(d1i, td2)
            oa, ow_, oi = otm.next()
            to1 = em.emit("VE", lambda v, oa=oa, d2a=d2a: v.tensor_scalar(
                oa, bY.ap()[0:128, 0:64], d2a[:, 0:1], None, ALU.mult), waits=[td2, mmP] + ow_)
            bY.rd(to1)
            bY.rd(td1)
            den2.used(d2i, to1)
            wtr = ptb.wr()
            ttr = em.emit("TE", lambda t, oa=oa: t.transpose(ptb.ap()[0:64, 0:128], oa, ident["bf"]),
                          waits=[to1] + wtr)
            otm.used(oi, ttr)
            tse = em.emit("VE", lambda v, gt=gt: v.tensor_copy(sfm.ap()[:, gt:gt + 128],
                                                               ptb.ap()[0:64, 0:128]), waits=[ttr])
            ptb.rd(tse)
            qr.used(qi, mmP)
            qr.used(qi, mmz)
            qr.used(qi, mmD)
            kr.used(ki, mmD)
            vr.used(vi, mm2)
            ktr.used(kti, mmzs)
            vtr.used(vti, smms[0][1])
            for kind in range(4):
                for l in range(4):
                    auxr.used(aux[(kind, l)][2], smms[-1][1])
        em.barrier()

    with ExitStack() as ph:
        wo_b = sbuf(nc, ph, "wo_b", [64, 1024], BF16)
        ldw = em.dma("GE", wo_b.ap(), wo_d)
        xr = Ring(nc, ph, 3, (128, 512), F32, "cx")
        gr = Ring(nc, ph, 3, (128, 512), BF16, "cg")
        orr = Ring(nc, ph, 3, (128, 512), F32, "co")
        bO0, bO1 = banks[0], banks[1]
        for it in range(NT):
            sl = slice(512 * it, 512 * (it + 1))
            for dc in range(8):
                bo = bO0 if dc % 2 == 0 else bO1
                war = bo.wr()
                mm = em.emit("TE", lambda t, dc=dc, sl=sl, bo=bo: t.matmul(
                    bo.ap()[0:128, 0:512], wo_b.ap()[:, 128 * dc:128 * (dc + 1)], sfm.ap()[:, sl],
                    start=True, stop=True), waits=[ldw] + war)
                xa, xw, xi = xr.next()
                ltx = em.dma("SY", xa, xfm[128 * dc:128 * (dc + 1), sl], waits=xw)
                ga, gw, gi = gr.next()
                ltg = em.dma("SY", ga, g0_d[128 * dc:128 * (dc + 1), sl], waits=gw)
                oa, ow_, oi = orr.next()
                t1 = em.emit("VE", lambda v, oa=oa, ga=ga, bo=bo: v.tensor_tensor(
                    oa, bo.ap()[0:128, 0:512], ga, ALU.mult), waits=[mm, ltg] + ow_)
                bo.rd(t1)
                gr.used(gi, t1)
                t2 = em.emit("VE", lambda v, oa=oa, xa=xa: v.tensor_add(oa, oa, xa), waits=[t1, ltx])
                xr.used(xi, t2)
                st = em.dma("SY", x1_o[128 * dc:128 * (dc + 1), sl], oa, waits=[t2])
                orr.used(oi, st)
        em.barrier()
    if standalone:
        em.replay()
        stack.close()
        return nc
    return None




def build_L1A(io=None, ctx=None):
    nc, stack, em, banks, ptb, standalone = _ctx(ctx)

    x1_d = _io_in(io, nc, "x1", [1024, OWN])
    wq_d = _io_in(io, nc, "l1wq", [1024, 1024])
    wk_d = _io_in(io, nc, "l1wk", [1024, 1024])
    wv_d = _io_in(io, nc, "l1wv", [1024, 1024])
    sc1_d = _io_in(io, nc, "l1sc1", [1024, 1])
    b1_d = _io_in(io, nc, "l1b1", [1024, 1])
    pq_o = _io_out(io, nc, "pq", [1024, OWN], BF16)
    pk_o = _io_out(io, nc, "pk", [1024, OWN], BF16)
    pkt_o = _io_out(io, nc, "pkt", [OWN, 1024], BF16)
    vaug_o = _io_out(io, nc, "vaug", [OWN, 1040], BF16)
    pdel_o = _io_out(io, nc, "pdel", [NCH, 16, 64, 65], F32)
    xn1_o = _io_out(io, nc, "xn1", [1024, OWN], F32)

    ones = _lc(io, em, nc, stack, "c_ones", [128, 1], want_bf=True)
    onesrow = _lc(io, em, nc, stack, "c_onesrow", [1, 128])
    ident = _lc(io, em, nc, stack, "c_ident", [128, 128], want_bf=True)
    triu = _lc(io, em, nc, stack, "c_triu", [128, 128], want_bf=True)
    e2t_c = _lc(io, em, nc, stack, "c_e2t", [2, 128], want_bf=True)
    eps = sbuf(nc, stack, "epsc", [128, 4], F32)
    memset(em, "VE", eps.ap()[:, 0:1], 1e-5)
    memset(em, "VE", eps.ap()[:, 1:2], 1e-6)
    stats = {k: sbuf(nc, stack, "st_" + k, [1, OWN], F32).ap()
             for k in ("m", "rstd", "s2", "rstd2", "var")}
    stats_scr = (io["stats_scr"] if io and "stats_scr" in io
                 else dram_scr(nc, "stats_scr", [8, OWN], F32))
    sc1c = sbuf(nc, stack, "sc1c", [128, 8], F32)
    b1c = sbuf(nc, stack, "b1c", [128, 8], F32)
    ctoks = []
    for cch in range(8):
        ctoks.append(em.dma("SY", sc1c.ap()[:, cch:cch + 1], sc1_d[128 * cch:128 * (cch + 1), :]))
        ctoks.append(em.dma("SY", b1c.ap()[:, cch:cch + 1], b1_d[128 * cch:128 * (cch + 1), :]))
    em.barrier()
    stats["e2t"] = e2t_c
    stats_phase(em, nc, stack, banks, x1_d, OWN, stats, ones["bf"], eps.ap(), want_extra=True,
                scr_dram=stats_scr)

    t_scr = dram_scr(nc, "t_scr", [1024, OWN], BF16)
    with ExitStack() as ph:
        mB = sbuf(nc, ph, "tmB", [128, OWN], F32)
        rB = sbuf(nc, ph, "trB", [128, OWN], F32)
        r2B = sbuf(nc, ph, "tr2B", [128, OWN], F32)
        b1_ = bcast_dma(em, mB.ap(), stats_scr[0:1, :], n=OWN)
        b2_ = bcast_dma(em, rB.ap(), stats_scr[1:2, :], n=OWN)
        b3_ = bcast_dma(em, r2B.ap(), stats_scr[3:4, :], n=OWN)
        xr = Ring(nc, ph, 3, (128, 512), F32, "tx")
        fr = Ring(nc, ph, 3, (128, 512), F32, "tf")
        f2r = Ring(nc, ph, 3, (128, 512), F32, "tf2")
        tbr = Ring(nc, ph, 3, (128, 512), BF16, "ttb")
        for it in range(NT):
            sl = slice(512 * it, 512 * (it + 1))
            for cch in range(8):
                xa, xw, xi = xr.next()
                lt = em.dma("SY", xa, x1_d[128 * cch:128 * (cch + 1), sl], waits=xw)
                fa, fw, fi = fr.next()
                t1 = em.emit("VE", lambda v, fa=fa, xa=xa, sl=sl: v.tensor_sub(fa, xa, mB.ap()[:, sl]),
                             waits=[lt, b1_] + fw)
                xr.used(xi, t1)
                f2a, f2w, f2i = f2r.next()
                t2 = em.emit("VE", lambda v, f2a=f2a, fa=fa, sl=sl: v.tensor_mul(f2a, fa, rB.ap()[:, sl]),
                             waits=[t1, b2_] + f2w)
                fr.used(fi, t2)
                st1 = em.dma("GE", xn1_o[128 * cch:128 * (cch + 1), sl], f2a, waits=[t2])
                t3 = em.emit("VE", lambda v, fa=fa, f2a=f2a, sl=sl: v.tensor_mul(fa, f2a, r2B.ap()[:, sl]),
                             waits=[t2, b3_])
                f2r.used(f2i, t3)
                f2r.used(f2i, st1)
                ta, tw, ti = tbr.next()
                t4 = em.emit("SE", lambda s, ta=ta, fa=fa, cch=cch: s.activation(
                    out=ta, in_=fa, func=AF.Identity, scale=sc1c.ap()[:, cch:cch + 1],
                    bias=b1c.ap()[:, cch:cch + 1]), waits=[t3] + ctoks + tw)
                fr.used(fi, t4)
                st2 = em.dma("GE", t_scr[128 * cch:128 * (cch + 1), sl], ta, waits=[t4])
                tbr.used(ti, st2)
        em.barrier()

    with ExitStack() as ph:
        wqb = sbuf(nc, ph, "wqb", [128, 8 * 1024], BF16)
        wkb = sbuf(nc, ph, "wkb", [128, 8 * 1024], BF16)
        wvb = sbuf(nc, ph, "wvb", [128, 8 * 1024], BF16)
        wl = []
        for cch in range(8):
            wl.append(em.dma("GE", wqb.ap()[:, 1024 * cch:1024 * (cch + 1)], wq_d[128 * cch:128 * (cch + 1), :]))
            wl.append(em.dma("GE", wkb.ap()[:, 1024 * cch:1024 * (cch + 1)], wk_d[128 * cch:128 * (cch + 1), :]))
            wl.append(em.dma("GE", wvb.ap()[:, 1024 * cch:1024 * (cch + 1)], wv_d[128 * cch:128 * (cch + 1), :]))
        tr = Ring(nc, ph, 10, (128, 512), BF16, "qt")
        er = Ring(nc, ph, 4, (128, 512), BF16, "qe")
        vtm = Ring(nc, ph, 3, (128, 1040), BF16, "qv")
        ktm = Ring(nc, ph, 3, (128, 1024), BF16, "qk2")
        b0, b1b, b2b, b3b = banks[0], banks[1], banks[2], banks[3]
        for it in range(NT):
            sl = slice(512 * it, 512 * (it + 1))
            t_list = []
            for cch in range(8):
                ta, tw, ti = tr.next()
                lt = em.dma("SY", ta, t_scr[128 * cch:128 * (cch + 1), sl], waits=tw)
                t_list.append((ta, lt, ti))
            for which, wb, out_d in (("q", wqb, pq_o), ("k", wkb, pk_o)):
                for hp in range(8):
                    bk = b0 if hp % 2 == 0 else b1b
                    war = bk.wr()
                    mml = None
                    for cch in range(8):
                        ta, lt, ti = t_list[cch]
                        mml = em.emit("TE", lambda t, wb=wb, hp=hp, cch=cch, ta=ta, bk=bk: t.matmul(
                            bk.ap()[0:128, 0:512], wb.ap()[:, 1024 * cch + 128 * hp:1024 * cch + 128 * (hp + 1)],
                            ta, start=(cch == 0), stop=(cch == 7)),
                            waits=[lt] + wl + (war if cch == 0 else []))
                        tr.used(ti, mml)
                    ea, ew, ei = er.next()
                    phi = em.emit("VE", lambda v, ea=ea, bk=bk: v.tensor_scalar(
                        ea, bk.ap()[0:128, 0:512], 0.0, 1e-3, ALU.max, ALU.add), waits=[mml] + ew)
                    bk.rd(phi)
                    st = em.dma("SY", out_d[128 * hp:128 * (hp + 1), sl], ea, waits=[phi])
                    er.used(ei, st)
            for ck in range(4):
                tglob = 512 * it + 128 * ck
                for half in range(2):
                    bk = b2b if half == 0 else b3b
                    war = bk.wr()
                    mml = None
                    for cch in range(8):
                        ta, lt, ti = t_list[cch]
                        mml = em.emit("TE", lambda t, ta=ta, ck=ck, half=half, cch=cch, bk=bk: t.matmul(
                            bk.ap()[0:128, 0:512], ta[:, 128 * ck:128 * (ck + 1)],
                            wvb.ap()[:, 1024 * cch + 512 * half:1024 * cch + 512 * (half + 1)],
                            start=(cch == 0), stop=(cch == 7)),
                            waits=[lt] + (war if cch == 0 else []))
                        tr.used(ti, mml)
                    va, vw, vi = vtm.next() if half == 0 else (None, None, None)
                    if half == 0:
                        build_L1A._vslot = (va, vw, vi)
                    va, vw, vi = build_L1A._vslot
                    cps = []
                    for hh in range(8):
                        h = half * 8 + hh
                        cp = em.emit("VE", lambda v, va=va, h=h, hh=hh, bk=bk: v.tensor_copy(
                            va[:, 65 * h:65 * h + 64], bk.ap()[0:128, 64 * hh:64 * (hh + 1)]),
                            waits=[mml] + (vw if (half == 0 and hh == 0) else []))
                        bk.rd(cp)
                        cps.append(cp)
                    if half == 1:
                        msv = em.emit("VE", lambda v, va=va: v.memset(
                            va.rearrange("p (h c) -> p h c", c=65)[:, :, 64:65], 1.0), waits=cps)
                        stv = em.dma("SY", vaug_o[tglob:tglob + 128, :], va, waits=[msv])
                        vtm.used(vi, stv)
                for half in range(2):
                    bk = b2b if half == 0 else b3b
                    war = bk.wr()
                    mml = None
                    for cch in range(8):
                        ta, lt, ti = t_list[cch]
                        mml = em.emit("TE", lambda t, ta=ta, ck=ck, half=half, cch=cch, bk=bk: t.matmul(
                            bk.ap()[0:128, 0:512], ta[:, 128 * ck:128 * (ck + 1)],
                            wkb.ap()[:, 1024 * cch + 512 * half:1024 * cch + 512 * (half + 1)],
                            start=(cch == 0), stop=(cch == 7)),
                            waits=[lt] + (war if cch == 0 else []))
                        tr.used(ti, mml)
                    if half == 0:
                        build_L1A._kslot = ktm.next()
                    ka2, kw2, ki2 = build_L1A._kslot
                    phi = em.emit("VE", lambda v, ka2=ka2, half=half, bk=bk: v.tensor_scalar(
                        ka2[:, 512 * half:512 * (half + 1)], bk.ap()[0:128, 0:512],
                        0.0, 1e-3, ALU.max, ALU.add),
                        waits=[mml] + (kw2 if half == 0 else []))
                    bk.rd(phi)
                    if half == 1:
                        stk = em.dma("SY", pkt_o[tglob:tglob + 128, :], ka2, waits=[phi])
                        ktm.used(ki2, stk)
        em.barrier()

    with ExitStack() as ph:
        kr = Ring(nc, ph, 2, (128, 1024), BF16, "dk")
        vr = Ring(nc, ph, 2, (128, 1040), BF16, "dv")
        dr = Ring(nc, ph, 6, (64, 65), F32, "dd")
        bD0, bD1 = banks[0], banks[1]
        for ch in range(NCH):
            gt = 128 * ch
            ka2, kw2, ki2 = kr.next()
            ltk = em.dma("SY", ka2, pkt_o[gt:gt + 128, :], waits=kw2)
            va, vw, vi = vr.next()
            ltv = em.dma("SY", va, vaug_o[gt:gt + 128, :], waits=vw)
            for h in range(16):
                bk = bD0 if h % 2 == 0 else bD1
                war = bk.wr()
                mm = em.emit("TE", lambda t, ka2=ka2, va=va, h=h, bk=bk: t.matmul(
                    bk.ap()[0:64, 0:65], ka2[:, 64 * h:64 * (h + 1)], va[:, 65 * h:65 * (h + 1)],
                    start=True, stop=True), waits=[ltk, ltv] + war)
                da, dw, di = dr.next()
                ev = em.emit("VE", lambda v, da=da, bk=bk: v.tensor_copy(da, bk.ap()[0:64, 0:65]),
                             waits=[mm] + dw)
                bk.rd(ev)
                st = em.dma("SY", pdel_o[ch, h], da, waits=[ev])
                dr.used(di, st)
                kr.used(ki2, mm)
                vr.used(vi, mm)
        em.barrier()
    if standalone:
        em.replay()
        stack.close()
        return nc
    return None




def build_L1B(io=None, ctx=None):
    nc, stack, em, banks, ptb, standalone = _ctx(ctx)

    xn1_d = _io_in(io, nc, "xn1", [1024, OWN])
    pq_d = _io_in(io, nc, "pq", [1024, OWN], BF16)
    pk_d = _io_in(io, nc, "pk", [1024, OWN], BF16)
    pkt_d = _io_in(io, nc, "pkt", [OWN, 1024], BF16)
    vaug_d = _io_in(io, nc, "vaug", [OWN, 1040], BF16)
    pdel_d = _io_in(io, nc, "pdelp", [NCH, 16, 64, 65], F32)
    wo_d = _io_in(io, nc, "l1wo", [1024, 1024])
    sc2_d = _io_in(io, nc, "l1sc2", [1024, 1])
    b2_d = _io_in(io, nc, "l1b2", [1024, 1])
    w1_d = _io_in(io, nc, "l1w1", [1024, 4096])
    b1v_d = _io_in(io, nc, "l1b1v", [4096, 1])
    w2_d = _io_in(io, nc, "l1w2", [4096, 1024])
    b2v_d = _io_in(io, nc, "l1b2v", [1024, 1])
    s1_o = _io_out(io, nc, "s1", [1024, OWN], F32)

    ones = _lc(io, em, nc, stack, "c_ones", [128, 1], want_bf=True)
    onesrow = _lc(io, em, nc, stack, "c_onesrow", [1, 128])
    ident = _lc(io, em, nc, stack, "c_ident", [128, 128], want_bf=True)
    triu = _lc(io, em, nc, stack, "c_triu", [128, 128], want_bf=True)
    e2t_c = _lc(io, em, nc, stack, "c_e2t", [2, 128], want_bf=True)
    eps = sbuf(nc, stack, "epsc", [128, 4], F32)
    memset(em, "VE", eps.ap()[:, 0:1], 1e-6)
    stats = {k: sbuf(nc, stack, "st_" + k, [1, OWN], F32).ap()
             for k in ("m", "rstd", "s2", "rstd2", "var")}
    stats_scr = (io["stats_scr"] if io and "stats_scr" in io
                 else dram_scr(nc, "stats_scr", [8, OWN], F32))
    sc2c = sbuf(nc, stack, "sc2c", [128, 8], F32)
    b2c = sbuf(nc, stack, "b2c", [128, 8], F32)
    b2vc = sbuf(nc, stack, "b2vc", [128, 8], F32)
    b1vc = sbuf(nc, stack, "b1vc", [128, 32], F32)
    ctoks = []
    for cch in range(8):
        ctoks.append(em.dma("SY", sc2c.ap()[:, cch:cch + 1], sc2_d[128 * cch:128 * (cch + 1), :]))
        ctoks.append(em.dma("SY", b2c.ap()[:, cch:cch + 1], b2_d[128 * cch:128 * (cch + 1), :]))
        ctoks.append(em.dma("SY", b2vc.ap()[:, cch:cch + 1], b2v_d[128 * cch:128 * (cch + 1), :]))
    for fc in range(32):
        ctoks.append(em.dma("SY", b1vc.ap()[:, fc:fc + 1], b1v_d[128 * fc:128 * (fc + 1), :]))
    Z_f = sbuf(nc, stack, "Z_f", [64, 16 * 65], F32)
    Z_b = sbuf(nc, stack, "Z_b", [128, 8 * 65], BF16)
    em.barrier()
    with ExitStack() as ph:
        dld = Ring(nc, ph, 6, (64, 1040), F32, "zld")
        p8 = io.get("pdelp8") if io else None
        if p8 is not None:
            ap8, g8a = p8
            acc = memset(em, "VE", Z_f.ap(), 0.0)
            for j in range(8):
                for ch in range(NCH):
                    ap3 = ap8[ch // 2, j].rearrange("c h r k -> c r h k")
                    aa, ww, ii = dld.next()
                    lt = em.dma("SY", aa.rearrange("r (h k) -> r h k", h=16), ap3[ch % 2],
                                waits=ww)
                    t = em.emit("VE", lambda v, aa=aa, j=j, g8a=g8a: v.tensor_scalar(
                        aa, aa, g8a[0:64, j:j + 1], None, ALU.mult), waits=[lt])
                    acc = em.emit("VE", lambda v, aa=aa: v.tensor_add(Z_f.ap(), Z_f.ap(), aa),
                                  waits=[t, acc])
                    dld.used(ii, acc)
        else:
            ap3 = pdel_d.rearrange("c h r k -> c r h k")
            a0, w0, i0 = dld.next()
            t0 = em.dma("SY", Z_f.ap().rearrange("r (h k) -> r h k", h=16), ap3[0], waits=w0)
            acc = t0
            for ch in range(1, NCH):
                aa, ww, ii = dld.next()
                lt = em.dma("SY", aa.rearrange("r (h k) -> r h k", h=16), ap3[ch], waits=ww)
                acc = em.emit("VE", lambda v, aa=aa: v.tensor_add(Z_f.ap(), Z_f.ap(), aa), waits=[lt, acc])
                dld.used(ii, acc)
        em.barrier()

    attnfm = sbuf(nc, stack, "attnfm", [128, 8 * OWN], BF16)
    with ExitStack() as ph:
        qr = Ring(nc, ph, 2, (128, 8 * 128), BF16, "pq8")
        kr = Ring(nc, ph, 2, (128, 8 * 128), BF16, "pk8")
        ktr = Ring(nc, ph, 2, (128, 1024), BF16, "pkt2")
        vr = Ring(nc, ph, 2, (128, 1040), BF16, "pva")
        gramr = Ring(nc, ph, 6, (128, 128), BF16, "pgr")
        denr = Ring(nc, ph, 4, (128, 1), F32, "pde")
        atr = Ring(nc, ph, 4, (128, 64), BF16, "pat")
        bG0, bG1, bO0, bO1, bS0, bS1 = (banks[0], banks[1], banks[2], banks[3], banks[4], banks[5])
        zb_readers = [[] for _ in range(16)]
        for ch in range(NCH):
            gt = 128 * ch
            qa, qw, qi = qr.next()
            ltq = None
            for pc in range(8):
                ltq = em.dma("SY", qa[:, 128 * pc:128 * (pc + 1)],
                             pq_d[128 * pc:128 * (pc + 1), gt:gt + 128], waits=(qw if pc == 0 else []))
            ka, kw, ki = kr.next()
            ltk = None
            for pc in range(8):
                ltk = em.dma("SY", ka[:, 128 * pc:128 * (pc + 1)],
                             pk_d[128 * pc:128 * (pc + 1), gt:gt + 128], waits=(kw if pc == 0 else []))
            kta, ktw, kti = ktr.next()
            ltkt = em.dma("SY", kta, pkt_d[gt:gt + 128, :], waits=ktw)
            va, vw, vi = vr.next()
            ltv = em.dma("SY", va, vaug_d[gt:gt + 128, :], waits=vw)
            stH = {}
            ptb_w = []

            def stage_a(h):
                hq = qa[:, 128 * (h // 2):128 * (h // 2) + 128][64 * (h % 2):64 * (h % 2) + 64, :]
                hk = ka[:, 128 * (h // 2):128 * (h // 2) + 128][64 * (h % 2):64 * (h % 2) + 64, :]
                gb = bG0 if h % 2 == 0 else bG1
                war = gb.wr()
                mmG = em.emit("TE", lambda t, hk=hk, hq=hq, gb=gb: t.matmul(
                    gb.ap()[0:128, 0:128], hk, hq, start=True, stop=True),
                    waits=[ltq, ltk] + war)
                ga, gw, gi = gramr.next()
                evm = em.emit("VE", lambda v, ga=ga, gb=gb: v.tensor_tensor(
                    ga, gb.ap()[0:128, 0:128], triu["bf"], ALU.mult), waits=[mmG] + gw)
                gb.rd(evm)
                qr.used(qi, mmG)
                kr.used(ki, mmG)
                stH[h] = dict(hq=hq, ga=ga, gi=gi, evm=evm)

            def stage_b(h):
                st = stH[h]
                ob = bO0 if h % 2 == 0 else bO1
                warO = ob.wr()
                mmI = em.emit("TE", lambda t, ga=st["ga"], va=va, h=h, ob=ob: t.matmul(
                    ob.ap()[0:128, 0:65], ga, va[:, 65 * h:65 * (h + 1)],
                    start=True, stop=False), waits=[st["evm"], ltv] + warO)
                gramr.used(st["gi"], mmI)
                ro = 64 * (h % 2)
                co = 65 * (h // 2)
                cpZ = em.emit("VE", lambda v, h=h, ro=ro, co=co: v.tensor_copy(
                    Z_b.ap()[ro:ro + 64, co:co + 65], Z_f.ap()[:, 65 * h:65 * (h + 1)]),
                    waits=zb_readers[h])
                mmP = em.emit("TE", lambda t, hq=st["hq"], ro=ro, co=co, ob=ob: t.matmul(
                    ob.ap()[0:128, 0:65], hq, Z_b.ap()[ro:ro + 64, co:co + 65],
                    start=False, stop=True), waits=[cpZ, ltq])
                zb_readers[h] = [mmP]
                qr.used(qi, mmP)
                sb_ = bS0 if h % 2 == 0 else bS1
                warS = sb_.wr()
                mmS = em.emit("TE", lambda t, kta=kta, va=va, h=h, sb_=sb_: t.matmul(
                    sb_.ap()[0:64, 0:65], kta[:, 64 * h:64 * (h + 1)], va[:, 65 * h:65 * (h + 1)],
                    start=True, stop=True), waits=[ltkt, ltv] + warS)
                adZ = em.emit("VE", lambda v, h=h, sb_=sb_: v.tensor_add(
                    Z_f.ap()[:, 65 * h:65 * (h + 1)], Z_f.ap()[:, 65 * h:65 * (h + 1)],
                    sb_.ap()[0:64, 0:65]), waits=[mmS, cpZ])
                sb_.rd(adZ)
                ktr.used(kti, mmS)
                vr.used(vi, mmS)
                vr.used(vi, mmI)
                st["mmP"] = mmP

            def stage_c(h):
                nonlocal ptb_w
                st = stH.pop(h)
                ob = bO0 if h % 2 == 0 else bO1
                da, dw, di = denr.next()
                td = em.emit("VE", lambda v, da=da, ob=ob: v.reciprocal(da, ob.ap()[0:128, 64:65]),
                             waits=[st["mmP"]] + dw)
                ob.rd(td)
                aa2, aw2, ai2 = atr.next()
                tn = em.emit("VE", lambda v, aa2=aa2, da=da, ob=ob: v.tensor_scalar(
                    aa2, ob.ap()[0:128, 0:64], da[:, 0:1], None, ALU.mult), waits=[td] + aw2)
                ob.rd(tn)
                denr.used(di, tn)
                g8 = h % 8
                if g8 == 0:
                    ptb_w = ptb.wr()
                ttr = em.emit("TE", lambda t, aa2=aa2, g8=g8: t.transpose(
                    ptb.ap()[0:64, 128 * g8:128 * (g8 + 1)], aa2, ident["bf"]),
                    waits=[tn] + (ptb_w if g8 == 0 else []))
                atr.used(ai2, ttr)
                pc = h // 2
                rowoff = 64 * (h % 2)
                tse = em.emit("VE", lambda v, pc=pc, rowoff=rowoff, gt=gt, g8=g8: v.tensor_copy(
                    attnfm.ap()[rowoff:rowoff + 64, OWN * pc + gt:OWN * pc + gt + 128],
                    ptb.ap()[0:64, 128 * g8:128 * (g8 + 1)]), waits=[ttr])
                ptb.rd(tse)

            for h in range(16):
                stage_a(h)
                if h >= 1:
                    stage_b(h - 1)
                if h >= 2:
                    stage_c(h - 2)
            stage_b(15)
            stage_c(14)
            stage_c(15)
        em.barrier()


    xatt_scr = dram_scr(nc, "xatt_scr", [1024, OWN], F32)
    with ExitStack() as ph:
        wob = sbuf(nc, ph, "wob", [128, 8 * 1024], BF16)
        wl = [em.dma("GE", wob.ap()[:, 1024 * kc:1024 * (kc + 1)], wo_d[128 * kc:128 * (kc + 1), :])
              for kc in range(8)]
        xr = Ring(nc, ph, 3, (128, 512), F32, "wx")
        orr = Ring(nc, ph, 3, (128, 512), F32, "wo2")
        bA, bB = banks[0], banks[1]
        for it in range(NT):
            sl = slice(512 * it, 512 * (it + 1))
            for dc in range(8):
                bk = bA if dc % 2 == 0 else bB
                war = bk.wr()
                mml = None
                for kc in range(8):
                    mml = em.emit("TE", lambda t, kc=kc, dc=dc, it=it, bk=bk: t.matmul(
                        bk.ap()[0:128, 0:512], wob.ap()[:, 1024 * kc + 128 * dc:1024 * kc + 128 * (dc + 1)],
                        attnfm.ap()[:, OWN * kc + 512 * it:OWN * kc + 512 * (it + 1)],
                        start=(kc == 0), stop=(kc == 7)), waits=wl + (war if kc == 0 else []))
                xa, xw, xi = xr.next()
                ltx = em.dma("SY", xa, xn1_d[128 * dc:128 * (dc + 1), sl], waits=xw)
                oa, ow_, oi = orr.next()
                t1 = em.emit("VE", lambda v, oa=oa, xa=xa, bk=bk: v.tensor_add(
                    oa, bk.ap()[0:128, 0:512], xa), waits=[mml, ltx] + ow_)
                bk.rd(t1)
                xr.used(xi, t1)
                st = em.dma("SY", xatt_scr[128 * dc:128 * (dc + 1), sl], oa, waits=[t1])
                orr.used(oi, st)
        em.barrier()

    stats["e2t"] = e2t_c
    stats_phase(em, nc, stack, banks, xatt_scr, OWN, stats, ones["bf"], eps.ap(), want_extra=False,
                scr_dram=stats_scr)

    hm_scr = dram_scr(nc, "hm_scr", [4096, OWN], BF16)
    with ExitStack() as ph:
        w1b = sbuf(nc, ph, "w1b", [128, 8 * 4096], BF16)
        wl = [em.dma("GE", w1b.ap()[:, 4096 * cch:4096 * (cch + 1)], w1_d[128 * cch:128 * (cch + 1), :])
              for cch in range(8)]
        mB = sbuf(nc, ph, "m2B", [128, OWN], F32)
        rB = sbuf(nc, ph, "r2B2", [128, OWN], F32)
        bm = bcast_dma(em, mB.ap(), stats_scr[0:1, :], n=OWN)
        br = bcast_dma(em, rB.ap(), stats_scr[1:2, :], n=OWN)
        xr = Ring(nc, ph, 3, (128, 512), F32, "ux2")
        fr = Ring(nc, ph, 3, (128, 512), F32, "uf2")
        ybr = Ring(nc, ph, 10, (128, 512), BF16, "uy2")
        hr = Ring(nc, ph, 3, (128, 512), BF16, "uh2")
        bH0, bH1 = banks[2], banks[3]
        for it in range(NT):
            sl = slice(512 * it, 512 * (it + 1))
            y_list = []
            for cch in range(8):
                xa, xw, xi = xr.next()
                lt = em.dma("SY", xa, xatt_scr[128 * cch:128 * (cch + 1), sl], waits=xw)
                fa, fw, fi = fr.next()
                t1 = em.emit("VE", lambda v, fa=fa, xa=xa, sl=sl: v.tensor_sub(fa, xa, mB.ap()[:, sl]),
                             waits=[lt, bm] + fw)
                xr.used(xi, t1)
                t2 = em.emit("VE", lambda v, fa=fa, sl=sl: v.tensor_mul(fa, fa, rB.ap()[:, sl]),
                             waits=[t1, br])
                ya, yw, yi = ybr.next()
                t3 = em.emit("SE", lambda s, ya=ya, fa=fa, cch=cch: s.activation(
                    out=ya, in_=fa, func=AF.Identity, scale=sc2c.ap()[:, cch:cch + 1],
                    bias=b2c.ap()[:, cch:cch + 1]), waits=[t2] + ctoks + yw)
                fr.used(fi, t3)
                y_list.append((ya, t3, yi))
            last = None
            for fc in range(32):
                hb = bH0 if fc % 2 == 0 else bH1
                war = hb.wr()
                mml = None
                for cch in range(8):
                    ya, t3, yi = y_list[cch]
                    mml = em.emit("TE", lambda t, fc=fc, cch=cch, ya=ya, hb=hb: t.matmul(
                        hb.ap()[0:128, 0:512], w1b.ap()[:, 4096 * cch + 128 * fc:4096 * cch + 128 * (fc + 1)],
                        ya, start=(cch == 0), stop=(cch == 7)),
                        waits=[t3] + wl + (war if cch == 0 else []))
                ha, hw, hi = hr.next()
                ge = em.emit("SE", lambda s, ha=ha, hb=hb, fc=fc: s.activation(
                    out=ha, in_=hb.ap()[0:128, 0:512], func=AF.Gelu_apprx_tanh,
                    bias=b1vc.ap()[:, fc:fc + 1]), waits=[mml] + hw)
                hb.rd(ge)
                sto = em.dma("SY", hm_scr[128 * fc:128 * (fc + 1), sl], ha, waits=[ge])
                hr.used(hi, sto)
                last = sto
            for ya, t3, yi in y_list:
                ybr.used(yi, last)
        em.barrier()

    with ExitStack() as ph:
        w2b = sbuf(nc, ph, "w2b", [128, 32 * 1024], BF16)
        wl = [em.dma("GE", w2b.ap()[:, 1024 * fc:1024 * (fc + 1)], w2_d[128 * fc:128 * (fc + 1), :])
              for fc in range(32)]
        hrd = Ring(nc, ph, 34, (128, 512), BF16, "dh2")
        xr = Ring(nc, ph, 3, (128, 512), F32, "dx2")
        fgr = Ring(nc, ph, 3, (128, 512), F32, "df2")
        bA, bB = banks[2], banks[3]
        for it in range(NT):
            sl = slice(512 * it, 512 * (it + 1))
            hn = []
            for fc in range(32):
                ha, hw, hi = hrd.next()
                lt = em.dma("SY", ha, hm_scr[128 * fc:128 * (fc + 1), sl], waits=hw)
                hn.append((ha, lt, hi))
            last = None
            for dc in range(8):
                bk = bA if dc % 2 == 0 else bB
                war = bk.wr()
                mml = None
                for fc in range(32):
                    ha, lt, hi = hn[fc]
                    mml = em.emit("TE", lambda t, dc=dc, fc=fc, ha=ha, bk=bk: t.matmul(
                        bk.ap()[0:128, 0:512], w2b.ap()[:, 1024 * fc + 128 * dc:1024 * fc + 128 * (dc + 1)],
                        ha, start=(fc == 0), stop=(fc == 31)),
                        waits=[lt] + wl + (war if fc == 0 else []))
                fa, fw, fi = fgr.next()
                tb = em.emit("SE", lambda s, fa=fa, bk=bk, dc=dc: s.activation(
                    out=fa, in_=bk.ap()[0:128, 0:512], func=AF.Identity, bias=b2vc.ap()[:, dc:dc + 1]),
                    waits=[mml] + fw)
                bk.rd(tb)
                xa, xw, xi = xr.next()
                ltx = em.dma("SY", xa, xatt_scr[128 * dc:128 * (dc + 1), sl], waits=xw)
                t1 = em.emit("VE", lambda v, fa=fa, xa=xa: v.tensor_add(fa, fa, xa), waits=[tb, ltx])
                xr.used(xi, t1)
                st = em.dma("SY", s1_o[128 * dc:128 * (dc + 1), sl], fa, waits=[t1])
                fgr.used(fi, st)
                last = st
            for ha, lt, hi in hn:
                hrd.used(hi, last)
        em.barrier()
    if standalone:
        em.replay()
        stack.close()
        return nc
    return None




L2_EXT = 2816
L2_QKV = 2688
L2_LEAD = 768


def build_L2(final=False, io=None, ctx=None, upto_p=99):
    nc, stack, em, banks, ptb, standalone = _ctx(ctx)

    xe_d = _io_in(io, nc, "xe", [1024, L2_EXT])
    wqkv_d = _io_in(io, nc, "l2wqkv", [1024, 1536])
    wo_d = _io_in(io, nc, "l2wo", [512, 1024])
    w1_d = _io_in(io, nc, "l2w1", [1024, 5460])
    w2_d = _io_in(io, nc, "l2w2", [2730, 1024])
    b1_d = _io_in(io, nc, "l2b1", [5460, 1])
    b2_d = _io_in(io, nc, "l2b2", [1024, 1])
    lg_d = _io_in(io, nc, "l2lg", [1024, 1])
    lb_d = _io_in(io, nc, "l2lb", [1024, 1])
    qs_d = _io_in(io, nc, "l2qs", [128, 1])
    ks_d = _io_in(io, nc, "l2ks", [128, 1])
    rqcs_d = _io_in(io, nc, "rqcs", [128, L2_QKV])
    rqss_d = _io_in(io, nc, "rqss", [128, L2_QKV])
    rkcs_d = _io_in(io, nc, "rkcs", [128, L2_QKV])
    rkss_d = _io_in(io, nc, "rkss", [128, L2_QKV])
    lmask_d = _io_in(io, nc, "lmask", [17, 128, 640])
    if final:
        x1_d = _io_in(io, nc, "x1", [1024, OWN])
        s1_d = _io_in(io, nc, "s1", [1024, OWN])
        out_o = _io_out(io, nc, "outp", [1024, OWN], F32)
    else:
        xout_o = _io_out(io, nc, "xout", [1024, OWN], F32)

    if io and "l2scr" in io:
        xa_scr, h1_scr, prod_scr, l2st_scr = io["l2scr"]
    else:
        xa_scr = dram_scr(nc, "xa_scr", [1024, 128 + OWN], F32)
        h1_scr = dram_scr(nc, "h1_scr", [5460, OWN], BF16)
        prod_scr = dram_scr(nc, "prod_scr", [2730, OWN], BF16)
        l2st_scr = dram_scr(nc, "l2st_scr", [8, OWN], F32)

    ones = _lc(io, em, nc, stack, "c_ones", [128, 1], want_bf=True)
    ident = _lc(io, em, nc, stack, "c_ident", [128, 128], want_bf=True)
    e2 = _lc(io, em, nc, stack, "c_e2", [128, 2], want_bf=True)
    e2t = _lc(io, em, nc, stack, "c_e2t", [2, 128], want_bf=True)
    eps = sbuf(nc, stack, "epsc", [128, 4], F32)
    memset(em, "VE", eps.ap()[:, 0:1], 1e-5)
    qsc = sbuf(nc, stack, "qsc", [128, 2], F32)
    ct1 = em.dma("SY", qsc.ap()[:, 0:1], qs_d)
    ct2 = em.dma("SY", qsc.ap()[:, 1:2], ks_d)
    lgc = sbuf(nc, stack, "lgc", [128, 8], F32)
    lbc = sbuf(nc, stack, "lbc", [128, 8], F32)
    b2c = sbuf(nc, stack, "b2c", [128, 8], F32)
    b1a = sbuf(nc, stack, "b1a", [128, 22], F32)
    b1g = sbuf(nc, stack, "b1g", [128, 22], F32)
    ctoks = [ct1, ct2]
    for cch in range(8):
        ctoks.append(em.dma("SY", lgc.ap()[:, cch:cch + 1], lg_d[128 * cch:128 * (cch + 1), :]))
        ctoks.append(em.dma("SY", lbc.ap()[:, cch:cch + 1], lb_d[128 * cch:128 * (cch + 1), :]))
        ctoks.append(em.dma("SY", b2c.ap()[:, cch:cch + 1], b2_d[128 * cch:128 * (cch + 1), :]))
    for j in range(22):
        w = 42 if j == 21 else 128
        ctoks.append(em.dma("SY", b1a.ap()[0:w, j:j + 1], b1_d[128 * j:128 * j + w, :]))
        ctoks.append(em.dma("SY", b1g.ap()[0:w, j:j + 1],
                            b1_d[2730 + 128 * j:2730 + 128 * j + w, :]))
    stack25 = ExitStack()
    attnfm = sbuf(nc, stack25, "at2fm", [128, 4 * 17 * 128], BF16)
    em.barrier()
    NQC = L2_QKV // 128
    stack2 = ExitStack()
    rq_cs = sbuf(nc, stack2, "rq_cs", [128, L2_QKV], BF16)
    rq_ss = sbuf(nc, stack2, "rq_ss", [128, L2_QKV], BF16)
    rk_cs = sbuf(nc, stack2, "rk_cs", [128, L2_QKV], BF16)
    rk_ss = sbuf(nc, stack2, "rk_ss", [128, L2_QKV], BF16)
    rtoks = []
    for sb_t, dd in ((rq_cs, rqcs_d), (rq_ss, rqss_d), (rk_cs, rkcs_d), (rk_ss, rkss_d)):
        rtoks.append(em.dma("GE", sb_t.ap(), dd))
    qsw = sbuf(nc, stack2, "qsw", [128, 2], F32)
    swt = []
    for b0 in (0, 64):
        swt.append(em.emit("VE", lambda v, b0=b0: v.tensor_copy(
            qsw.ap()[b0:b0 + 32, :], qsc.ap()[b0 + 32:b0 + 64, :]), waits=[ct1, ct2]))
        swt.append(em.emit("VE", lambda v, b0=b0: v.tensor_copy(
            qsw.ap()[b0 + 32:b0 + 64, :], qsc.ap()[b0:b0 + 32, :]), waits=[ct1, ct2]))
    ttoks = []
    for tb, col, swp in ((rq_cs, 0, False), (rq_ss, 0, True), (rk_cs, 1, False), (rk_ss, 1, True)):
        sc_ap = (qsw if swp else qsc).ap()[:, col:col + 1]
        ttoks.append(em.emit("SE", lambda s, tb=tb, sc_ap=sc_ap: s.activation(
            out=tb.ap(), in_=tb.ap(), func=AF.Identity, scale=sc_ap),
            waits=rtoks + swt + [ct1, ct2]))
    qrot = sbuf(nc, stack2, "qrot", [128, 4 * L2_QKV], BF16)
    krot = sbuf(nc, stack2, "krot", [128, 4 * L2_QKV], BF16)
    vtm = sbuf(nc, stack2, "vtm", [128, NQC * 512], BF16)
    with ExitStack() as ph:
        wqb = sbuf(nc, ph, "wqb2", [128, 8 * 512], BF16)
        wkb = sbuf(nc, ph, "wkb2", [128, 8 * 512], BF16)
        wvb = sbuf(nc, ph, "wvb2", [128, 8 * 512], BF16)
        wl = []
        for cch in range(8):
            wl.append(em.dma("GE", wqb.ap()[:, 512 * cch:512 * (cch + 1)],
                             wqkv_d[128 * cch:128 * (cch + 1), 0:512]))
            wl.append(em.dma("GE", wkb.ap()[:, 512 * cch:512 * (cch + 1)],
                             wqkv_d[128 * cch:128 * (cch + 1), 512:1024]))
            wl.append(em.dma("GE", wvb.ap()[:, 512 * cch:512 * (cch + 1)],
                             wqkv_d[128 * cch:128 * (cch + 1), 1024:1536]))
        hx = Ring(nc, ph, 16, (128, 512), BF16, "hx")
        sqr = Ring(nc, ph, 3, (128, 512), BF16, "l2sq")
        nrb = Ring(nc, ph, 3, (2, 512), BF16, "l2nb")
        rnbr = Ring(nc, ph, 3, (128, 512), F32, "l2rb")
        t1r = Ring(nc, ph, 3, (128, 512), F32, "l2t1")
        swr = Ring(nc, ph, 3, (128, 512), F32, "l2sw")
        abr = Ring(nc, ph, 6, (128, 512), F32, "l2ab")
        vbr = Ring(nc, ph, 4, (128, 512), BF16, "l2vb")
        bQK = [banks[0], banks[1]]
        bNp = [banks[3], banks[5]]
        bBp = [banks[4], banks[6]]
        bV = banks[2]
        TILES = [(0, 512), (512, 512), (1024, 512), (1536, 512), (2048, 512), (2560, 128)]
        iters = []
        for t0, W in TILES:
            iters.append(("load", t0, W))
            for which in ("q", "k"):
                for hp in range(4):
                    iters.append((which, hp, t0, W))
        hlists = {}
        stA = {}

        def do_load(t0, W):
            h_list = []
            for cch in range(8):
                off = (128 + t0) if cch < 4 else (127 + t0)
                ha, hw, hi = hx.next()
                lt = em.dma("GE", ha[:, 0:W], xe_d[128 * cch:128 * (cch + 1), off:off + W],
                            waits=hw)
                h_list.append((ha, lt, hi))
            hlists[t0] = h_list
            for k in range(W // 128):
                ci = t0 // 128 + k
                war = bV.wr()
                mml = None
                for cch in range(8):
                    ha, lt, hi = h_list[cch]
                    mml = em.emit("TE", lambda t, ha=ha, cch=cch, k=k: t.matmul(
                        bV.ap()[0:128, 0:512], ha[:, 128 * k:128 * (k + 1)],
                        wvb.ap()[:, 512 * cch:512 * (cch + 1)],
                        start=(cch == 0), stop=(cch == 7)),
                        waits=[lt] + (wl + war if cch == 0 else []))
                    hx.used(hi, mml)
                va, vw, vi = vbr.next()
                ev = em.emit("VE", lambda v, va=va: v.tensor_copy(va, bV.ap()[0:128, 0:512]),
                             waits=[mml] + vw)
                bV.rd(ev)
                cv = em.emit("VE", lambda v, va=va, ci=ci: v.tensor_copy(
                    vtm.ap()[:, 512 * ci:512 * (ci + 1)], va), waits=[ev])
                vbr.used(vi, cv)

        def do_A(idx, which, hp, t0, W):
            bk = bQK[idx % 2]
            wb = wqb if which == "q" else wkb
            war = bk.wr()
            mml = None
            h_list = hlists[t0]
            for cch in range(8):
                ha, lt, hi = h_list[cch]
                mml = em.emit("TE", lambda t, ha=ha, wb=wb, hp=hp, cch=cch, W=W, bk=bk: t.matmul(
                    bk.ap()[0:128, 0:W],
                    wb.ap()[:, 512 * cch + 128 * hp:512 * cch + 128 * (hp + 1)],
                    ha[:, 0:W], start=(cch == 0), stop=(cch == 7)),
                    waits=[lt] + (wl + war if cch == 0 else []))
                hx.used(hi, mml)
            stA[idx] = mml

        def do_B(idx, which, hp, t0, W):
            bk = bQK[idx % 2]
            bN_ = bNp[idx % 2]
            bB_ = bBp[idx % 2]
            rot_cs = rq_cs if which == "q" else rk_cs
            rot_ss = rq_ss if which == "q" else rk_ss
            dstap = qrot if which == "q" else krot
            mml = stA[idx]
            sa, sw, si = sqr.next()
            sq = em.emit("SE", lambda s, sa=sa, W=W, bk=bk: s.activation(
                out=sa[:, 0:W], in_=bk.ap()[0:128, 0:W], func=AF.Square), waits=[mml] + sw)
            warN = bN_.wr()
            mmN = em.emit("TE", lambda t, sa=sa, W=W, bN_=bN_: t.matmul(
                bN_.ap()[0:2, 0:W], e2["bf"], sa[:, 0:W], start=True, stop=True),
                waits=[sq, e2["btok"]] + warN)
            sqr.used(si, mmN)
            nb, nbw, nbi = nrb.next()
            t3b = em.emit("SE", lambda s, nb=nb, W=W, bN_=bN_: s.activation(
                out=nb[:, 0:W], in_=bN_.ap()[0:2, 0:W], func=AF.Copy), waits=[mmN] + nbw)
            bN_.rd(t3b)
            warB = bB_.wr()
            mmB = em.emit("TE", lambda t, nb=nb, W=W, bB_=bB_: t.matmul(
                bB_.ap()[0:128, 0:W], e2t["bf"], nb[:, 0:W], start=True, stop=True),
                waits=[t3b, e2t["btok"]] + warB)
            nrb.used(nbi, mmB)
            ra, rw, ri = rnbr.next()
            sq2 = em.emit("SE", lambda s, ra=ra, W=W, bB_=bB_: s.activation(
                out=ra[:, 0:W], in_=bB_.ap()[0:128, 0:W], func=AF.Sqrt), waits=[mmB] + rw)
            bB_.rd(sq2)
            mx = em.emit("VE", lambda v, ra=ra, W=W: v.tensor_scalar_max(
                ra[:, 0:W], ra[:, 0:W], 1e-12), waits=[sq2])
            evB = em.emit("VE", lambda v, ra=ra, W=W: v.reciprocal(ra[:, 0:W], ra[:, 0:W]),
                          waits=[mx])
            ta1, tw1, ti1 = t1r.next()
            u1 = em.emit("VE", lambda v, ta1=ta1, ra=ra, W=W, bk=bk: v.tensor_tensor(
                ta1[:, 0:W], bk.ap()[0:128, 0:W], ra[:, 0:W], ALU.mult), waits=[evB] + tw1)
            bk.rd(u1)
            rnbr.used(ri, u1)
            swa, sww, swi = swr.next()
            cps = []
            for blk in range(2):
                b0 = 64 * blk
                cps.append(em.emit("SE", lambda s, swa=swa, ta1=ta1, b0=b0, W=W: s.activation(
                    out=swa[b0:b0 + 32, 0:W], in_=ta1[b0 + 32:b0 + 64, 0:W], func=AF.Copy),
                    waits=[u1] + (sww if blk == 0 else [])))
                cps.append(em.emit("SE", lambda s, swa=swa, ta1=ta1, b0=b0, W=W: s.activation(
                    out=swa[b0 + 32:b0 + 64, 0:W], in_=ta1[b0:b0 + 32, 0:W], func=AF.Copy),
                    waits=[u1]))
            aa_, aw_, ai_ = abr.next()
            qcol = slice(t0, t0 + W)
            u3 = em.emit("VE", lambda v, aa_=aa_, ta1=ta1, rot_cs=rot_cs, qcol=qcol, W=W: v.tensor_tensor(
                aa_[:, 0:W], ta1[:, 0:W], rot_cs.ap()[:, qcol], ALU.mult),
                waits=[u1] + ttoks + aw_)
            t1r.used(ti1, u3)
            ab_, abw_, abi_ = abr.next()
            u4 = em.emit("VE", lambda v, ab_=ab_, swa=swa, rot_ss=rot_ss, qcol=qcol, W=W: v.tensor_tensor(
                ab_[:, 0:W], swa[:, 0:W], rot_ss.ap()[:, qcol], ALU.mult),
                waits=cps + ttoks + abw_)
            swr.used(swi, u4)
            u5 = em.emit("VE", lambda v, aa_=aa_, ab_=ab_, dstap=dstap, hp=hp, qcol=qcol, W=W: v.tensor_tensor(
                dstap.ap()[:, L2_QKV * hp + qcol.start:L2_QKV * hp + qcol.start + W],
                aa_[:, 0:W], ab_[:, 0:W], ALU.add), waits=[u3, u4])
            abr.used(ai_, u5)
            abr.used(abi_, u5)

        prog = []
        ni_idx = 0
        for it in iters:
            if it[0] == "load":
                prog.append(("L", it))
            else:
                prog.append(("AB", ni_idx, it))
                ni_idx += 1
        pending = None
        for step in prog:
            if step[0] == "L":
                do_load(step[1][1], step[1][2])
            else:
                _, idx, it = step
                do_A(idx, *it)
                if pending is not None:
                    do_B(pending[0], *pending[1])
                pending = (idx, it)
        if pending is not None:
            do_B(pending[0], *pending[1])
        em.barrier()

    with ExitStack() as ph:
        scr = Ring(nc, ph, 3, (128, 640), F32, "l2sc")
        negr = Ring(nc, ph, 3, (128, 1), F32, "l2ng")
        pr = Ring(nc, ph, 4, (128, 640), BF16, "l2p")
        rsr = Ring(nc, ph, 3, (128, 1), F32, "l2rs")
        ptr2 = Ring(nc, ph, 3, (128, 640), BF16, "l2pt")
        mskr = Ring(nc, ph, 3, (128, 640), F32, "l2mk")
        bS = [banks[0], banks[1]]
        bT = [banks[2], banks[3]]
        bA2 = [banks[4], banks[5]]
        NIT = 17 * 8
        state = {}

        def S(i):
            qc, h = divmod(i, 8)
            Q0 = 512 + 128 * qc
            kv0 = Q0 - 512
            hp, ro = h // 2, 64 * (h % 2)
            if h == 0:
                ma, mw, mi = mskr.next()
                ltm = em.dma("SY", ma, lmask_d[qc], waits=mw)
                state[("mask", qc)] = (ma, mi, ltm)
            ms = state[("mask", qc)]
            sb_, tb_ = bS[i % 2], bT[i % 2]
            qsl = qrot.ap()[ro:ro + 64, L2_QKV * hp + Q0:L2_QKV * hp + Q0 + 128]
            war = sb_.wr()
            mm1 = em.emit("TE", lambda t, qsl=qsl, hp=hp, ro=ro, kv0=kv0, sb_=sb_: t.matmul(
                sb_.ap()[0:128, 0:512], qsl,
                krot.ap()[ro:ro + 64, L2_QKV * hp + kv0:L2_QKV * hp + kv0 + 512],
                start=True, stop=True), waits=war)
            warT = tb_.wr()
            mm2 = em.emit("TE", lambda t, qsl=qsl, hp=hp, ro=ro, kv0=kv0, tb_=tb_: t.matmul(
                tb_.ap()[0:128, 0:128], qsl,
                krot.ap()[ro:ro + 64, L2_QKV * hp + kv0 + 512:L2_QKV * hp + kv0 + 640],
                start=True, stop=True), waits=warT)
            state[i] = dict(mm1=mm1, mm2=mm2, sb=sb_, tb=tb_, qc=qc, h=h, hp=hp, ro=ro, kv0=kv0)

        def M(i):
            st = state[i]
            ma, mi, ltm = state[("mask", st["qc"])]
            sa, sw, si = scr.next()
            e1 = em.emit("VE", lambda v, sa=sa, ma=ma, sb_=st["sb"]: v.tensor_add(
                sa[:, 0:512], sb_.ap()[0:128, 0:512], ma[:, 0:512]),
                waits=[st["mm1"], ltm] + sw)
            st["sb"].rd(e1)
            e2_ = em.emit("VE", lambda v, sa=sa, ma=ma, tb_=st["tb"]: v.tensor_add(
                sa[:, 512:640], tb_.ap()[0:128, 0:128], ma[:, 512:640]),
                waits=[st["mm2"], ltm])
            st["tb"].rd(e2_)
            mskr.used(mi, e1)
            mskr.used(mi, e2_)
            na, nw, ni = negr.next()
            rm = em.emit("VE", lambda v, na=na, sa=sa: v.tensor_reduce(
                na, sa, AX.X, ALU.max, negate=True), waits=[e1, e2_] + nw)
            pa, pw, pi = pr.next()
            rs, rsw, rsi = rsr.next()
            ex = em.emit("SE", lambda s, pa=pa, sa=sa, na=na, rs=rs: s.activation(
                out=pa, in_=sa, func=AF.Exp, bias=na[:, 0:1], accum_out=rs),
                waits=[rm] + pw + rsw)
            negr.used(ni, ex)
            scr.used(si, ex)
            rc = em.emit("VE", lambda v, rs=rs: v.reciprocal(rs, rs), waits=[ex])
            pn = em.emit("VE", lambda v, pa=pa, rs=rs: v.tensor_scalar(
                pa, pa, rs[:, 0:1], None, ALU.mult), waits=[rc])
            rsr.used(rsi, pn)
            st["pa"] = pa
            st["pi"] = pi
            st["pn"] = pn

        def T(i):
            st = state[i]
            pa, pi, pn = st["pa"], st["pi"], st["pn"]
            h, hp, ro, kv0, qc = st["h"], st["hp"], st["ro"], st["kv0"], st["qc"]
            ab_ = bA2[i % 2]
            warA = ab_.wr()
            wtr = ptb.wr()
            tts = []
            for ki in range(5):
                tts.append(em.emit("TE", lambda t, pa=pa, ki=ki: t.transpose(
                    ptb.ap()[0:128, 128 * ki:128 * (ki + 1)], pa[:, 128 * ki:128 * (ki + 1)],
                    ident["bf"]), waits=[pn, ident["btok"]] + (wtr if ki == 0 else [])))
                pr.used(pi, tts[-1])
            pta, ptw, pti = ptr2.next()
            evt = em.emit("SE", lambda s, pta=pta: s.activation(
                out=pta, in_=ptb.ap()[0:128, 0:640], func=AF.Copy), waits=tts + ptw)
            ptb.rd(evt)
            mmavl = None
            for ki in range(5):
                kvc = kv0 // 128 + ki
                mmavl = em.emit("TE", lambda t, pta=pta, kvc=kvc, h=h, ki=ki, ab_=ab_: t.matmul(
                    ab_.ap()[0:64, 0:128], vtm.ap()[:, 512 * kvc + 64 * h:512 * kvc + 64 * (h + 1)],
                    pta[:, 128 * ki:128 * (ki + 1)], start=(ki == 0), stop=(ki == 4)),
                    waits=[evt] + (warA if ki == 0 else []))
            ptr2.used(pti, mmavl)
            pr.used(pi, mmavl)
            ct = em.emit("VE", lambda v, ab_=ab_, hp=hp, ro=ro, qc=qc: v.tensor_copy(
                attnfm.ap()[ro:ro + 64, 17 * 128 * hp + 128 * qc:17 * 128 * hp + 128 * (qc + 1)],
                ab_.ap()[0:64, 0:128]), waits=[mmavl])
            ab_.rd(ct)
            del state[i]

        for k in range(NIT + 2):
            if k < NIT:
                S(k)
            if 0 <= k - 1 < NIT:
                M(k - 1)
            if 0 <= k - 2 < NIT:
                T(k - 2)
        em.barrier()
    stack2.close()

    with ExitStack() as ph:
        wob = sbuf(nc, ph, "wob", [128, 4096], BF16)
        wl = []
        for ic in range(4):
            wl.append(em.dma("GE", wob.ap()[:, 1024 * ic:1024 * (ic + 1)],
                             wo_d[128 * ic:128 * (ic + 1), :]))
        xr = Ring(nc, ph, 3, (128, 512), F32, "l3x")
        orr = Ring(nc, ph, 4, (128, 512), F32, "l3o")
        it2 = 0
        for t0 in (0, 512, 1024, 1536, 2048):
            W = 128 if t0 == 2048 else 512
            for dc in range(8):
                bk = banks[it2 % 2]
                it2 += 1
                war = bk.wr()
                mml = None
                for ic in range(4):
                    mml = em.emit("TE", lambda t, ic=ic, dc=dc, t0=t0, W=W, bk=bk: t.matmul(
                        bk.ap()[0:128, 0:W],
                        wob.ap()[:, 1024 * ic + 128 * dc:1024 * ic + 128 * (dc + 1)],
                        attnfm.ap()[:, 17 * 128 * ic + t0:17 * 128 * ic + t0 + W],
                        start=(ic == 0), stop=(ic == 3)),
                        waits=wl + (war if ic == 0 else []))
                xa, xw, xi = xr.next()
                lt = em.dma("SY", xa[:, 0:W], xe_d[128 * dc:128 * (dc + 1), 640 + t0:640 + t0 + W],
                            waits=xw)
                oa, ow_, oi = orr.next()
                ad = em.emit("VE", lambda v, oa=oa, xa=xa, W=W, bk=bk: v.tensor_add(
                    oa[:, 0:W], bk.ap()[0:128, 0:W], xa[:, 0:W]), waits=[mml, lt] + ow_)
                bk.rd(ad)
                xr.used(xi, ad)
                st = em.dma("SY", xa_scr[128 * dc:128 * (dc + 1), t0:t0 + W], oa[:, 0:W],
                            waits=[ad])
                orr.used(oi, st)
        em.barrier()
    stack25.close()

    coffs = [128] * 4 + [127] * 4
    with ExitStack() as ph:
        stats = {k: sbuf(nc, ph, "st_" + k, [1, OWN], F32).ap() for k in ("m", "rstd", "var")}
        stats["e2t"] = e2t
        stats_phase(em, nc, ph, banks, xa_scr, OWN, stats, ones["bf"], eps.ap(),
                    want_extra=False, scr_dram=l2st_scr, chunk_off=coffs)

    with ExitStack() as ph:
        w1b = sbuf(nc, ph, "w1b", [128, 8 * 5460], BF16)
        w2b = sbuf(nc, ph, "w2b", [128, 22 * 1024], BF16)
        wl = []
        for cch in range(8):
            wl.append(em.dma("GE", w1b.ap()[:, 5460 * cch:5460 * (cch + 1)],
                             w1_d[128 * cch:128 * (cch + 1), :]))
        for j in range(22):
            rw = 42 if j == 21 else 128
            wl.append(em.dma("GE", w2b.ap()[0:rw, 1024 * j:1024 * (j + 1)],
                             w2_d[128 * j:128 * j + rw, :]))
        xna = sbuf(nc, ph, "xna", [128, 8 * 512], BF16)
        parena = sbuf(nc, ph, "parena", [128, 22 * 512], BF16)
        mB = sbuf(nc, ph, "l2mB", [128, 512], F32)
        rB = sbuf(nc, ph, "l2rB", [128, 512], F32)
        xr = Ring(nc, ph, 3, (128, 512), F32, "l4x")
        fr = Ring(nc, ph, 2, (128, 512), F32, "l4f")
        gar = Ring(nc, ph, 2, (128, 512), BF16, "l4g")
        yr = Ring(nc, ph, 3, (128, 512), F32, "l6y")
        orr = Ring(nc, ph, 3, (128, 512), F32, "l6o")
        if final:
            gr2 = Ring(nc, ph, 2, (128, 512), F32, "l6g")
            x1r = Ring(nc, ph, 2, (128, 512), F32, "l6x1")
            s1r = Ring(nc, ph, 2, (128, 512), F32, "l6s1")
        bA = [banks[0], banks[1]]
        bG = [banks[2], banks[3]]
        bD = [banks[4], banks[5]]
        xna_users = []
        last_down = None
        for ti in range(4):
            sl = slice(512 * ti, 512 * (ti + 1))
            b1_ = bcast_dma(em, mB.ap(), l2st_scr[0:1, sl], n=512, waits=xna_users)
            b2_ = bcast_dma(em, rB.ap(), l2st_scr[1:2, sl], n=512, waits=xna_users)
            tts = []
            for cch in range(8):
                xa, xw, xi = xr.next()
                lt = em.dma("SY", xa, xa_scr[128 * cch:128 * (cch + 1),
                                             coffs[cch] + 512 * ti:coffs[cch] + 512 * (ti + 1)],
                            waits=xw)
                fa, fw, fi = fr.next()
                t1 = em.emit("VE", lambda v, fa=fa, xa=xa: v.tensor_sub(fa, xa, mB.ap()),
                             waits=[lt, b1_] + fw)
                xr.used(xi, t1)
                t2 = em.emit("VE", lambda v, fa=fa: v.tensor_mul(fa, fa, rB.ap()), waits=[t1, b2_])
                t3 = em.emit("SE", lambda s, fa=fa, cch=cch: s.activation(
                    out=xna.ap()[:, 512 * cch:512 * (cch + 1)], in_=fa, func=AF.Identity,
                    scale=lgc.ap()[:, cch:cch + 1], bias=lbc.ap()[:, cch:cch + 1]),
                    waits=[t2] + ctoks + xna_users)
                fr.used(fi, t3)
                tts.append(t3)
            xna_users = []
            pms = []
            for j in range(22):
                rw = 42 if j == 21 else 128
                ba = bA[j % 2]
                bg = bG[j % 2]
                warA = ba.wr()
                warG = bg.wr()
                mma = mmg = None
                for cch in range(8):
                    mma = em.emit("TE", lambda t, j=j, cch=cch, rw=rw, ba=ba: t.matmul(
                        ba.ap()[0:rw, 0:512],
                        w1b.ap()[:, 5460 * cch + 128 * j:5460 * cch + 128 * j + rw],
                        xna.ap()[:, 512 * cch:512 * (cch + 1)], start=(cch == 0), stop=(cch == 7)),
                        waits=wl + tts + (warA if cch == 0 else []))
                for cch in range(8):
                    mmg = em.emit("TE", lambda t, j=j, cch=cch, rw=rw, bg=bg: t.matmul(
                        bg.ap()[0:rw, 0:512],
                        w1b.ap()[:, 5460 * cch + 2730 + 128 * j:5460 * cch + 2730 + 128 * j + rw],
                        xna.ap()[:, 512 * cch:512 * (cch + 1)], start=(cch == 0), stop=(cch == 7)),
                        waits=(warG if cch == 0 else []))
                xna_users.append(mmg)
                ga, gw, gi = gar.next()
                geo = em.emit("SE", lambda s, ga=ga, rw=rw, j=j, bg=bg: s.activation(
                    out=ga[0:rw, :], in_=bg.ap()[0:rw, 0:512], func=AF.Gelu,
                    bias=b1g.ap()[0:rw, j:j + 1]),
                    waits=[mmg, last_down] + ctoks + gw)
                bg.rd(geo)
                pm = em.emit("VE", lambda v, ga=ga, rw=rw, j=j, ba=ba: v.scalar_tensor_tensor(
                    parena.ap()[0:rw, 512 * j:512 * (j + 1)], ba.ap()[0:rw, 0:512],
                    b1a.ap()[0:rw, j:j + 1], ga[0:rw, :], ALU.add, ALU.mult),
                    waits=[mma, geo, last_down] + ctoks)
                ba.rd(pm)
                gar.used(gi, pm)
                pms.append(pm)
            for dc in range(8):
                bk = bD[dc % 2]
                war = bk.wr()
                mml = None
                for j in range(22):
                    rw = 42 if j == 21 else 128
                    mml = em.emit("TE", lambda t, j=j, dc=dc, rw=rw, bk=bk: t.matmul(
                        bk.ap()[0:128, 0:512],
                        w2b.ap()[0:rw, 1024 * j + 128 * dc:1024 * j + 128 * (dc + 1)],
                        parena.ap()[0:rw, 512 * j:512 * (j + 1)], start=(j == 0), stop=(j == 21)),
                        waits=[pms[j]] + (war if j == 0 else []))
                last_down = mml
                ya, yw, yi = yr.next()
                evh = em.emit("SE", lambda s, ya=ya, dc=dc, bk=bk: s.activation(
                    out=ya, in_=bk.ap()[0:128, 0:512], func=AF.Identity,
                    bias=b2c.ap()[:, dc:dc + 1]), waits=[mml] + ctoks + yw)
                bk.rd(evh)
                xa, xw, xi = xr.next()
                lt = em.dma("SY", xa, xa_scr[128 * dc:128 * (dc + 1),
                                             128 + 512 * ti:128 + 512 * (ti + 1)], waits=xw)
                oa, ow_, oi = orr.next()
                ad = em.emit("VE", lambda v, oa=oa, ya=ya, xa=xa: v.tensor_add(oa, ya, xa),
                             waits=[evh, lt] + ow_)
                yr.used(yi, ad)
                xr.used(xi, ad)
                if not final:
                    st = em.dma("SY", xout_o[128 * dc:128 * (dc + 1), sl], oa, waits=[ad])
                    orr.used(oi, st)
                else:
                    ga2, gw2, gi2 = gr2.next()
                    sg = em.emit("SE", lambda s, ga2=ga2, oa=oa: s.activation(
                        out=ga2, in_=oa, func=AF.Silu), waits=[ad] + gw2)
                    orr.used(oi, sg)
                    x1a, x1w, x1i = x1r.next()
                    l1 = em.dma("SY", x1a, x1_d[128 * dc:128 * (dc + 1), sl], waits=x1w)
                    s1a, s1w, s1i = s1r.next()
                    l2 = em.dma("SY", s1a, s1_d[128 * dc:128 * (dc + 1), sl], waits=s1w)
                    t1 = em.emit("VE", lambda v, ga2=ga2, s1a=s1a: v.tensor_mul(ga2, ga2, s1a),
                                 waits=[sg, l2])
                    s1r.used(s1i, t1)
                    t2 = em.emit("VE", lambda v, ga2=ga2, x1a=x1a: v.tensor_add(ga2, ga2, x1a),
                                 waits=[t1, l1])
                    x1r.used(x1i, t2)
                    st = em.dma("SY", out_o[128 * dc:128 * (dc + 1), sl], ga2, waits=[t2])
                    gr2.used(gi2, st)
        em.barrier()
    if standalone:
        em.replay()
        stack.close()
        return nc
    return None




def build_full(upto=99, frm=0, extra_scr=None, share=True):
    nc = bass.Bass("TRN2", target_bir_lowering=False, debug=False, detect_race_conditions=False)
    ostack = ExitStack()
    em = Em(nc, ostack)
    banks, ptb = make_psum(nc)

    xfm_d = decl_in(nc, "xfm", [1024, OWN])
    gcol_d = decl_in(nc, "gcol", [128, 1])
    g8_d = decl_in(nc, "g8", [128, 8])
    outp_d = decl_out(nc, "outp", [1024, OWN], F32)
    win = {}
    for nm, shp in (("l0wq", [1024, 64]), ("l0wk", [1024, 64]), ("l0wv", [1024, 64]),
                    ("l0wup", [4096, 1024]), ("l0wdn", [1024, 4096]), ("l0wo", [64, 1024]),
                    ("l1wq", [1024, 1024]), ("l1wk", [1024, 1024]), ("l1wv", [1024, 1024]),
                    ("l1sc1", [1024, 1]), ("l1b1", [1024, 1]), ("l1wo", [1024, 1024]),
                    ("l1sc2", [1024, 1]), ("l1b2", [1024, 1]), ("l1w1", [1024, 4096]),
                    ("l1b1v", [4096, 1]), ("l1w2", [4096, 1024]), ("l1b2v", [1024, 1]),
                    ("l2wqkv", [2, 1024, 1536]), ("l2wo", [2, 512, 1024]),
                    ("l2w1", [2, 1024, 5460]), ("l2w2", [2, 2730, 1024]),
                    ("l2b1", [2, 5460, 1]), ("l2b2", [2, 1024, 1]),
                    ("l2lg", [2, 1024, 1]), ("l2lb", [2, 1024, 1]),
                    ("l2qs", [2, 128, 1]), ("l2ks", [2, 128, 1]),
                    ("rqcs", [128, L2_QKV]), ("rqss", [128, L2_QKV]),
                    ("rkcs", [128, L2_QKV]), ("rkss", [128, L2_QKV]),
                    ("lmask", [17, 128, 640])):
        win[nm] = decl_in(nc, nm, shp)
    for fam in ("UL", "VR", "UR", "VL"):
        for l in range(4):
            win[f"l0{fam}{l}"] = decl_in(nc, f"l0{fam}{l}", [BS[l], RL[l]])

    c = {}
    c["c_ones"] = load_const(em, nc, ostack, "c_ones", [128, 1], want_bf=True)
    c["c_onesrow"] = load_const(em, nc, ostack, "c_onesrow", [1, 128])
    c["c_ident"] = load_const(em, nc, ostack, "c_ident", [128, 128], want_bf=True)
    c["c_ident2"] = load_const(em, nc, ostack, "c_ident2", [128, 64], want_bf=True)
    c["c_triu"] = load_const(em, nc, ostack, "c_triu", [128, 128], want_bf=True)
    c["c_l0masks"] = load_const(em, nc, ostack, "c_l0masks", [64, 9 * 64])
    c["c_e2"] = load_const(em, nc, ostack, "c_e2", [128, 2], want_bf=True)
    c["c_e2t"] = load_const(em, nc, ostack, "c_e2t", [2, 128], want_bf=True)
    gc_sb = sbuf(nc, ostack, "gc_sb", [128, 1], F32)
    em.dma("SY", gc_sb.ap(), gcol_d)
    gc8 = sbuf(nc, ostack, "gc8", [128, 8], F32)
    em.dma("SY", gc8.ap(), g8_d)
    for nm in ("rdA", "rlA", "rdB", "rlB", "rdC", "rlC", "rdD", "rlD"):
        em.xsem(nm)

    qf_s = dram_scr(nc, "qf_s", [64, OWN], BF16)
    kf_s = dram_scr(nc, "kf_s", [64, OWN], BF16)
    vf_s = dram_scr(nc, "vf_s", [64, OWN], BF16)
    kt_s = dram_scr(nc, "kt_s", [OWN, 64], BF16)
    vt_s = dram_scr(nc, "vt_s", [OWN, 64], BF16)
    aux_s = dram_scr(nc, "aux_s", [4, OWN, 4, 64], BF16)
    g0_s = dram_scr(nc, "g0_s", [1024, OWN], BF16)
    deltas_s = dram_scr(nc, "deltas_s", [NCH, 10, 64, 64], F32)
    dcc_s = dram_scr(nc, "dcc_s", [4, 8, 4, 10, 64, 64], F32)
    x1_s = dram_scr(nc, "x1_s", [1024, OWN], F32)
    pq_s = dram_scr(nc, "pq_s", [1024, OWN], BF16)
    pk_s = dram_scr(nc, "pk_s", [1024, OWN], BF16)
    pkt_s = dram_scr(nc, "pkt_s", [OWN, 1024], BF16)
    vaug_s = dram_scr(nc, "vaug_s", [OWN, 1040], BF16)
    pdel_s = dram_scr(nc, "pdel_s", [NCH, 16, 64, 65], F32)
    pcc_s = dram_scr(nc, "pcc_s", [8, 8, 2, 16, 64, 65], F32)
    xe1_s = dram_scr(nc, "xe1_s", [1024, L2_EXT], F32)
    tail1_s = dram_scr(nc, "tail1_s", [1024, L2_LEAD], F32)
    tcc1_s = dram_scr(nc, "tcc1_s", [4, 8, 256, L2_LEAD], F32)
    s1_s = dram_scr(nc, "s1_s", [1024, OWN], F32)
    xe2_s = xe1_s
    tail2_s = dram_scr(nc, "tail2_s", [1024, L2_LEAD], F32)
    tcc2_s = dram_scr(nc, "tcc2_s", [4, 8, 256, L2_LEAD], F32)
    if share:
        stats_sh = dram_scr(nc, "stats_sh", [8, OWN], F32)
        xa_sh = dram_scr(nc, "xa_sh", [1024, 128 + OWN], F32)
        h1_sh = dram_scr(nc, "h1_sh", [5460, OWN], BF16)
        prod_sh = dram_scr(nc, "prod_sh", [2730, OWN], BF16)
        l2scr_sh = (xa_sh, h1_sh, prod_sh, dram_scr(nc, "l2st_sh", [8, OWN], F32))
        nc._scrpool = {
            "wtup_scr": h1_sh[0:2048].rearrange("(a b) c -> a (b c)", b=16),
            "wtdn_scr": h1_sh[2048:4096].rearrange("(a b) c -> a (b c)", b=16),
            "hm_scr": h1_sh[0:4096],
            "t_scr": prod_sh[0:1024],
            "xatt_scr": xa_sh[:, 0:OWN],
        }
    else:
        stats_sh = None
        l2scr_sh = None
    if extra_scr:
        for shp in extra_scr:
            dram_scr(nc, "xtra", shp, F32)
    em.barrier()

    def strip(d):
        if not share:
            for k in ("stats_scr", "l2scr"):
                d.pop(k, None)
        return d

    def sec(fn):
        sstack = ExitStack()
        fn(sstack)
        em.barrier()
        sstack.close()

    def ctx_of(sstack):
        return (nc, sstack, em, banks, ptb)

    def tail_copy(sstack, src_ap, dst_ap):
        with ExitStack() as ph:
            tr = Ring(nc, ph, 3, (128, L2_LEAD), F32, "tcp")
            for cch in range(8):
                ta, tw, ti = tr.next()
                lt = em.dma("SY", ta, src_ap[128 * cch:128 * (cch + 1), OWN:OWN + L2_LEAD],
                            waits=tw)
                st = em.dma("GE", dst_ap[128 * cch:128 * (cch + 1), :], ta, waits=[lt])
                tr.used(ti, st)

    def lead_fill(sstack, cc_ap, dst_ap):
        with ExitStack() as ph:
            lr = Ring(nc, ph, 5, (128, L2_LEAD), F32, "ldf")
            ar = Ring(nc, ph, 2, (128, L2_LEAD), F32, "lda")
            for cch in range(8):
                aa, aw, ai = ar.next()
                acc = None
                for j in range(8):
                    la, lw, li = lr.next()
                    lt = em.dma("SY", la, cc_ap[cch // 2, j,
                                               128 * (cch % 2):128 * (cch % 2) + 128, :],
                                waits=lw)
                    if j == 0:
                        acc = em.emit("VE", lambda v, aa=aa, la=la: v.tensor_scalar(
                            aa, la, gc8.ap()[:, 0:1], None, ALU.mult), waits=[lt] + aw)
                    else:
                        t = em.emit("VE", lambda v, la=la, j=j: v.tensor_scalar(
                            la, la, gc8.ap()[:, j:j + 1], None, ALU.mult), waits=[lt])
                        acc = em.emit("VE", lambda v, aa=aa, la=la: v.tensor_tensor(
                            aa, aa, la, ALU.add), waits=[t, acc])
                    lr.used(li, acc)
                st = em.dma("GE", dst_ap[128 * cch:128 * (cch + 1), 0:L2_LEAD], aa, waits=[acc])
                ar.used(ai, st)

    W = win

    def fin():
        em.barrier()
        em.replay()
        ostack.close()
        return nc

    if frm <= 0:
        sec(lambda s: build_L0A(io=strip({"xfm": xfm_d, "qf": qf_s, "kf": kf_s, "vf": vf_s,
                                    "kt": kt_s, "vt": vt_s, "aux": aux_s, "g0": g0_s,
                                    "deltas": deltas_s, "stats_scr": stats_sh, "c": c, **W}), ctx=ctx_of(s)))
    for i_ in range(4):
        em.rdma(dcc_s[i_].rearrange("j c s r k -> j (c s) (r k)"),
                deltas_s[4 * i_:4 * (i_ + 1)].rearrange("c s r k -> (c s) (r k)"),
                "R i j -> S R i j", "rdA", "rlA")
    em.barrier(extra=[("s", "rdA", 64), ("s", "rlA", 64)])
    if upto <= 1:
        return fin()
    if frm <= 1:
        sec(lambda s: build_L0B(io=strip({"xfm": xfm_d, "qf": qf_s, "kf": kf_s, "vf": vf_s,
                                    "kt": kt_s, "vt": vt_s, "aux": aux_s, "g0": g0_s,
                                    "dprev": dcc_s[0, 0], "dprev8": (dcc_s, gc8.ap()), "x1": x1_s,
                                    "c": c, **W}), ctx=ctx_of(s)))
    if upto <= 2:
        return fin()
    if frm <= 2:
        sec(lambda s: build_L1A(io=strip({"x1": x1_s, "pq": pq_s, "pk": pk_s, "pkt": pkt_s,
                                    "vaug": vaug_s, "pdel": pdel_s,
                                    "stats_scr": stats_sh,
                                    "xn1": xe1_s[:, L2_LEAD:L2_LEAD + OWN],
                                    "c": c, **W}), ctx=ctx_of(s)))
    for i_ in range(8):
        em.rdma(pcc_s[i_].rearrange("j c h r k -> j (c h) (r k)"),
                pdel_s[2 * i_:2 * (i_ + 1)].rearrange("c h r k -> (c h) (r k)"),
                "R i j -> S R i j", "rdB", "rlB")
    sec(lambda s: tail_copy(s, xe1_s, tail1_s))
    for i_ in range(4):
        em.rdma(tcc1_s[i_].rearrange("j (a b) c -> j a b c", b=2),
                tail1_s[256 * i_:256 * (i_ + 1), :].rearrange("(a b) c -> a b c", b=2),
                "R a b c -> S R a b c", "rdC", "rlC")
    em.barrier(extra=[("s", "rdB", 128), ("s", "rlB", 128), ("s", "rdC", 64), ("s", "rlC", 64)])
    sec(lambda s: lead_fill(s, tcc1_s, xe1_s))
    if upto <= 3:
        return fin()
    sec(lambda s: build_L1B(io=strip({"xn1": xe1_s[:, L2_LEAD:L2_LEAD + OWN], "stats_scr": stats_sh, "pq": pq_s,
                                "pk": pk_s, "pkt": pkt_s, "vaug": vaug_s,
                                "pdelp": pcc_s[0, 0], "pdelp8": (pcc_s, gc8.ap()), "s1": s1_s,
                                "c": c, **W}), ctx=ctx_of(s)))

    def l2io(l):
        d = {"xe": xe1_s if l == 0 else xe2_s, "c": c, "l2scr": l2scr_sh,
             "rqcs": W["rqcs"], "rqss": W["rqss"], "rkcs": W["rkcs"], "rkss": W["rkss"],
             "lmask": W["lmask"]}
        for nm in ("l2wqkv", "l2wo", "l2w1", "l2w2", "l2b1", "l2b2", "l2lg", "l2lb",
                   "l2qs", "l2ks"):
            d[nm] = W[nm][l]
        if l == 0:
            d["xout"] = xe2_s[:, L2_LEAD:L2_LEAD + OWN]
        else:
            d["x1"] = x1_s
            d["s1"] = s1_s
            d["outp"] = outp_d
        return d

    if upto <= 4:
        return fin()
    sec(lambda s: build_L2(final=False, io=strip(l2io(0)), ctx=ctx_of(s)))
    if upto <= 5:
        return fin()
    sec(lambda s: tail_copy(s, xe2_s, tail2_s))
    for i_ in range(4):
        em.rdma(tcc2_s[i_].rearrange("j (a b) c -> j a b c", b=2),
                tail2_s[256 * i_:256 * (i_ + 1), :].rearrange("(a b) c -> a b c", b=2),
                "R a b c -> S R a b c", "rdD", "rlD")
    em.barrier(extra=[("s", "rdD", 64), ("s", "rlD", 64)])
    sec(lambda s: lead_fill(s, tcc2_s, xe2_s))
    if upto <= 6:
        return fin()
    sec(lambda s: build_L2(final=True, io=strip(l2io(1)), ctx=ctx_of(s)))
    return fin()


def build_g1():
    nc, stack, em, banks, ptb, _ = _ctx(None)
    io = {"c": {}}
    c = io["c"]
    c["c_ones"] = load_const(em, nc, stack, "c_ones", [128, 1], want_bf=True)
    c["c_onesrow"] = load_const(em, nc, stack, "c_onesrow", [1, 128])
    c["c_ident"] = load_const(em, nc, stack, "c_ident", [128, 128], want_bf=True)
    c["c_triu"] = load_const(em, nc, stack, "c_triu", [128, 128], want_bf=True)
    c["c_l0masks"] = load_const(em, nc, stack, "c_l0masks", [64, 9 * 64])
    x1_s = dram_scr(nc, "x1g", [1024, OWN], F32)
    x1_o = decl_out(nc, "x1", [1024, OWN], F32)
    stats_sh = dram_scr(nc, "statsg", [8, OWN], F32)
    io["stats_scr"] = stats_sh
    io["x1"] = x1_s
    em.barrier()

    def sec(fn):
        sstack = ExitStack()
        fn(sstack)
        em.barrier()
        sstack.close()

    sec(lambda s: build_L0B(io=io, ctx=(nc, s, em, banks, ptb)))
    with ExitStack() as ph:
        cr = Ring(nc, ph, 3, (128, 512), F32, "x1c")
        for cch in range(8):
            for it in range(4):
                ca, cw, ci = cr.next()
                sl = slice(512 * it, 512 * (it + 1))
                lt = em.dma("SY", ca, x1_s[128 * cch:128 * (cch + 1), sl], waits=cw)
                st = em.dma("GE", x1_o[128 * cch:128 * (cch + 1), sl], ca, waits=[lt])
                cr.used(ci, st)
        em.barrier()
    sec(lambda s: build_L1A(io=io, ctx=(nc, s, em, banks, ptb)))
    em.barrier()
    em.replay()
    stack.close()
    return nc


def build_g2():
    nc, stack, em, banks, ptb, _ = _ctx(None)
    io = {"c": {}}
    c = io["c"]
    c["c_ones"] = load_const(em, nc, stack, "c_ones", [128, 1], want_bf=True)
    c["c_onesrow"] = load_const(em, nc, stack, "c_onesrow", [1, 128])
    c["c_ident"] = load_const(em, nc, stack, "c_ident", [128, 128], want_bf=True)
    c["c_triu"] = load_const(em, nc, stack, "c_triu", [128, 128], want_bf=True)
    c["c_e2"] = load_const(em, nc, stack, "c_e2", [128, 2], want_bf=True)
    c["c_e2t"] = load_const(em, nc, stack, "c_e2t", [2, 128], want_bf=True)
    xe_d = decl_in(nc, "xe", [1024, L2_EXT])
    stats_sh = dram_scr(nc, "statsg", [8, OWN], F32)
    io["stats_scr"] = stats_sh
    io["xn1"] = xe_d[:, L2_LEAD:L2_LEAD + OWN]
    io["xe"] = xe_d
    em.barrier()

    def sec(fn):
        sstack = ExitStack()
        fn(sstack)
        em.barrier()
        sstack.close()

    sec(lambda s: build_L1B(io=io, ctx=(nc, s, em, banks, ptb)))
    sec(lambda s: build_L2(final=False, io=io, ctx=(nc, s, em, banks, ptb)))
    em.barrier()
    em.replay()
    stack.close()
    return nc


def l2_rotary(own_start):
    DH, N, W = 64, 4096, 512
    half = np.arange(0, DH, 2, dtype=np.float64)
    inv = 1.0 / 10000.0 ** (half / DH)
    d_freq = np.concatenate([inv, inv])
    base = (half + 0.4 * DH) / (1.4 * DH)
    d_base = np.concatenate([base, base])
    pos = own_start - 640 + np.arange(L2_QKV, dtype=np.float64)
    ang = d_freq[:, None] * pos[None, :]
    cos, sin = np.cos(ang), np.sin(ang)
    power = (pos - N // 2) / (W // 2)
    scale = d_base[:, None] ** power[None, :]
    sgn = np.ones((64, 1)); sgn[:32] = -1.0
    out = {}
    for nm, t in (("rqcs", 8.0 * cos * scale), ("rqss", 8.0 * sin * scale * sgn),
                  ("rkcs", cos / scale), ("rkss", sin / scale * sgn)):
        out[nm] = np.ascontiguousarray(np.tile(t, (2, 1)).astype(np.float32))
    return out


def l2_masks(own_start):
    FM = np.float32(np.finfo(np.float32).max)
    m = np.zeros((17, 128, 640), dtype=np.float32)
    r = np.arange(128)[:, None]
    c = np.arange(640)[None, :]
    for qc in range(17):
        gq = own_start + 128 * (qc - 1)
        j = gq - 512 + c
        valid = (c >= r) & (c <= r + 512) & (j >= 0)
        m[qc] = np.where(valid, np.float32(0.0), -FM)
    return m


def _core_slices(x):
    out = []
    for c in range(NCORE):
        b, h = c // 2, c % 2
        sl = np.ascontiguousarray(x[b, 2048 * h:2048 * (h + 1), :].T.astype(np.float32))
        out.append(sl)
    return out


def _zpad9(d):
    d = np.array(d)
    d[:, 9, 1:, :] = 0.0
    return d


def _l2_ext(fm_list):
    exts = []
    for c in range(NCORE):
        own = fm_list[c]
        if c % 2 == 0:
            lead = np.zeros((1024, L2_LEAD), dtype=np.float32)
        else:
            lead = fm_list[c - 1][:, OWN - L2_LEAD:]
        exts.append(np.ascontiguousarray(np.concatenate([lead, own], axis=1)))
    return exts


def kernel(**inputs):
    inp = {k: np.ascontiguousarray(np.asarray(v)) for k, v in inputs.items()}
    cst = host_constants()
    x = np.asarray(inp["L0_attn_x"], np.float32)
    slices = _core_slices(x)
    ids = list(range(NCORE))
    cb = {"c_ones": cst["c_ones"], "c_onesrow": cst["c_onesrow"], "c_ident": cst["c_ident"]}

    nc = build_L0A()
    in_maps = []
    for c in range(NCORE):
        m = {"xfm": slices[c],
             "l0wq": inp["L0_attn_Wq"], "l0wk": inp["L0_attn_Wk"], "l0wv": inp["L0_attn_Wv"],
             "l0wup": inp["L0_ffn_w_up"], "l0wdn": inp["L0_ffn_w_down"],
             "c_ident2": cst["c_ident2"], "c_e2t": cst["c_e2t"], **cb}
        for fam in ("UL", "VR", "UR", "VL"):
            for l in range(4):
                m[f"l0{fam}{l}"] = inp[f"L0_attn_{fam}{l}"]
        in_maps.append(m)
    resA = run_bass_kernel_spmd(nc, in_maps, core_ids=ids)

    nc = build_g1()
    zdel = np.zeros((NCH, 10, 64, 64), np.float32)
    in_maps = []
    for c in range(NCORE):
        rA = resA.results[c]
        m = {"xfm": slices[c],
             "qf": np.asarray(rA["qf"]), "kf": np.asarray(rA["kf"]), "vf": np.asarray(rA["vf"]),
             "kt": np.asarray(rA["kt"]), "vt": np.asarray(rA["vt"]),
             "aux": np.asarray(rA["aux"]), "g0": np.asarray(rA["g0"]),
             "dprev": _zpad9(np.asarray(resA.results[c - 1]["deltas"], np.float32))
                      if c % 2 == 1 else zdel,
             "l0wo": inp["L0_attn_Wo"],
             "l1wq": inp["L1_attn_wq"].reshape(1024, 1024),
             "l1wk": inp["L1_attn_wk"].reshape(1024, 1024),
             "l1wv": inp["L1_attn_wv"].reshape(1024, 1024),
             "l1sc1": inp["L1_attn_ln1_scale"].reshape(1024, 1),
             "l1b1": inp["L1_attn_ln1_bias"].reshape(1024, 1),
             "c_triu": cst["c_triu"], "c_l0masks": cst["c_l0masks"],
             "c_e2t": cst["c_e2t"], **cb}
        for fam in ("UL", "VR", "UR", "VL"):
            for l in range(4):
                m[f"l0{fam}{l}"] = inp[f"L0_attn_{fam}{l}"]
        in_maps.append(m)
    resB = run_bass_kernel_spmd(nc, in_maps, core_ids=ids)
    x1fm = [np.ascontiguousarray(np.asarray(resB.results[c]["x1"], np.float32))
            for c in range(NCORE)]
    xn1fm = [np.ascontiguousarray(np.asarray(resB.results[c]["xn1"], np.float32))
             for c in range(NCORE)]

    rot = {0: l2_rotary(0), OWN: l2_rotary(OWN)}
    msk = {0: l2_masks(0), OWN: l2_masks(OWN)}

    def l2w(l):
        return {"l2wqkv": np.ascontiguousarray(inp["L1_ffn_wqkv"][l]),
                "l2wo": np.ascontiguousarray(inp["L1_ffn_wo"][l]),
                "l2w1": np.ascontiguousarray(inp["L1_ffn_w1"][l]),
                "l2w2": np.ascontiguousarray(inp["L1_ffn_w2"][l]),
                "l2b1": np.ascontiguousarray(inp["L1_ffn_b1"][l].reshape(5460, 1)),
                "l2b2": np.ascontiguousarray(inp["L1_ffn_b2"][l].reshape(1024, 1)),
                "l2lg": np.ascontiguousarray(inp["L1_ffn_ln_g"][l].reshape(1024, 1)),
                "l2lb": np.ascontiguousarray(inp["L1_ffn_ln_b"][l].reshape(1024, 1)),
                "l2qs": np.ascontiguousarray(np.tile(inp["L1_ffn_q_scale"][l], 2)
                                             .reshape(128, 1)),
                "l2ks": np.ascontiguousarray(np.tile(inp["L1_ffn_k_scale"][l], 2)
                                             .reshape(128, 1))}

    nc = build_g2()
    zp = np.zeros((NCH, 16, 64, 65), np.float32)
    exts = _l2_ext(xn1fm)
    in_maps = []
    for c in range(NCORE):
        rB = resB.results[c]
        own_start = OWN * (c % 2)
        m = {"xe": exts[c],
             "pq": np.asarray(rB["pq"]), "pk": np.asarray(rB["pk"]),
             "pkt": np.asarray(rB["pkt"]), "vaug": np.asarray(rB["vaug"]),
             "pdelp": (np.asarray(resB.results[c - 1]["pdel"], np.float32)
                       if c % 2 == 1 else zp),
             "l1wo": inp["L1_attn_wo"].reshape(1024, 1024),
             "l1sc2": inp["L1_attn_ln2_scale"].reshape(1024, 1),
             "l1b2": inp["L1_attn_ln2_bias"].reshape(1024, 1),
             "l1w1": inp["L1_attn_w1"], "l1b1v": inp["L1_attn_b1"].reshape(4096, 1),
             "l1w2": inp["L1_attn_w2"], "l1b2v": inp["L1_attn_b2"].reshape(1024, 1),
             "lmask": msk[own_start],
             "c_triu": cst["c_triu"], "c_e2": cst["c_e2"], "c_e2t": cst["c_e2t"],
             **cb, **l2w(0), **rot[own_start]}
        in_maps.append(m)
    resC = run_bass_kernel_spmd(nc, in_maps, core_ids=ids)
    s1fm = [np.ascontiguousarray(np.asarray(resC.results[c]["s1"], np.float32))
            for c in range(NCORE)]
    xafm = [np.ascontiguousarray(np.asarray(resC.results[c]["xout"], np.float32))
            for c in range(NCORE)]

    nc = build_L2(final=True)
    exts2 = _l2_ext(xafm)
    in_maps = []
    for c in range(NCORE):
        own_start = OWN * (c % 2)
        m = {"xe": exts2[c], "lmask": msk[own_start],
             "x1": x1fm[c], "s1": s1fm[c],
             "c_ones": cst["c_ones"], "c_ident": cst["c_ident"],
             "c_e2": cst["c_e2"], "c_e2t": cst["c_e2t"],
             **l2w(1), **rot[own_start]}
        in_maps.append(m)
    resD = run_bass_kernel_spmd(nc, in_maps, core_ids=ids)

    out = np.empty((4, 4096, 1024), dtype=np.float32)
    for c in range(NCORE):
        b, h = c // 2, c % 2
        out[b, OWN * h:OWN * (h + 1), :] = np.asarray(resD.results[c]["outp"], np.float32).T
    return out
